# version 06
# baseline (speedup 1.0000x reference)
import contextlib
import numpy as np
import ml_dtypes
import concourse.bass as bass
import concourse.mybir as mybir
from concourse.bass_utils import run_bass_kernel_spmd
from concourse.aot_env import allow_ici_all_to_all

F32 = mybir.dt.float32
BF16 = mybir.dt.bfloat16
AL = mybir.AluOpType
AX = mybir.AxisListType
ACT = mybir.ActivationFunctionType

NCORE = 8
H, KVH, DH, G = 32, 8, 128, 4
Bq, S, D = 2, 4096, 4096
CB = 16
WIN = 64
SCALE = float(DH) ** -0.5
TCH = 256
NT = Bq * S
TPC = NT // NCORE
EPS = 1e-5
BIG = 1e30
RSTR = 648

DEBUG = False
_DBG_SHAPES = {}


def _host_prep_core(inp, core):
    P = np.concatenate([np.arange(0, DH, 2), np.arange(1, DH, 2)])
    d = {}
    wqkv = inp["L0_attn_w_qkv"]
    qrows = np.concatenate(
        [wqkv[(4 * core + g) * DH + P] for g in range(G)], axis=0)
    krows = wqkv[H * DH + core * DH + P]
    vrows = wqkv[H * DH + KVH * DH + core * DH: H * DH + KVH * DH + (core + 1) * DH]
    d["wqkvT"] = np.ascontiguousarray(
        np.concatenate([qrows, krows, vrows], axis=0).T.astype(np.float32))
    wg = inp["L0_attn_w_gate"]
    grows = np.stack([wg[3 * (4 * core + g) + j] for g in range(G) for j in range(3)])
    d["wgT"] = np.ascontiguousarray(grows.T.astype(np.float32))
    bgv = inp["L0_attn_b_gate"]
    d["bg"] = np.ascontiguousarray(
        np.array([bgv[3 * (4 * core + g) + j] for g in range(G) for j in range(3)],
                 np.float32).reshape(12, 1))
    d["kposT"] = np.ascontiguousarray(inp["L0_attn_k_pos"][core][:, P].T.astype(np.float32))
    d["vposT"] = np.ascontiguousarray(inp["L0_attn_v_pos"][core].T.astype(np.float32))
    d["memk"] = np.ascontiguousarray(
        inp["L0_attn_mem_kv"][0, core, 0][P].reshape(128, 1).astype(np.float32))
    d["memv"] = np.ascontiguousarray(
        inp["L0_attn_mem_kv"][1, core, 0].reshape(1, 128).astype(np.float32))
    return d


def _host_prep_shared(inp):
    P = np.concatenate([np.arange(0, DH, 2), np.arange(1, DH, 2)])
    d = {}
    X = np.ascontiguousarray(inp["L0_attn_hidden_states"].reshape(NT, D))
    d["xT"] = np.ascontiguousarray(X.T)
    perm2048 = (np.arange(CB)[:, None] * DH + P[None, :]).reshape(-1)
    d["kw1T"] = np.ascontiguousarray(inp["L0_attn_k_w1"].T[perm2048].astype(np.float32))
    d["vw1T"] = np.ascontiguousarray(inp["L0_attn_v_w1"].T.astype(np.float32))
    d["kb1"] = np.ascontiguousarray(inp["L0_attn_k_b1"].reshape(2048, 1).astype(np.float32))
    d["vb1"] = np.ascontiguousarray(inp["L0_attn_v_b1"].reshape(2048, 1).astype(np.float32))
    d["kw2T"] = np.ascontiguousarray(inp["L0_attn_k_w2"].T[:, P].astype(np.float32))
    d["vw2T"] = np.ascontiguousarray(inp["L0_attn_v_w2"].T.astype(np.float32))
    d["kb2"] = np.ascontiguousarray(inp["L0_attn_k_b2"][P].reshape(128, 1).astype(np.float32))
    vb2z = np.zeros((128, 128), np.float32)
    vb2z[0, :] = inp["L0_attn_v_b2"]
    d["vb2z"] = vb2z
    d["woT"] = np.ascontiguousarray(inp["L0_attn_w_o"].T.astype(np.float32))
    d["wfT"] = np.ascontiguousarray(inp["L0_ffn_W"].T.astype(np.float32))
    d["bfb"] = np.ascontiguousarray(
        inp["L0_ffn_bias"].reshape(32, 128).T.astype(np.float32))
    d["laT"] = np.ascontiguousarray(
        inp["L0_ffn_lora_A"].reshape(128, D).T.astype(np.float32))
    d["lbT"] = np.ascontiguousarray(
        inp["L0_ffn_lora_B"].transpose(0, 2, 1).reshape(128, D).astype(np.float32))
    d["lw128"] = np.ascontiguousarray(
        np.repeat(np.asarray(inp["L0_ffn_lora_weights"]), 16).reshape(128, 1).astype(np.float32))
    d["identc"] = np.eye(128, dtype=np.float32)
    d["ones128"] = np.ones((128, 1), np.float32)
    e1t = np.zeros((128, 128), np.float32)
    e1t[0, :] = 1.0
    d["e1t"] = e1t
    xs = np.arange(896) - 256
    bst32 = np.zeros((32, 896), np.float32)
    for r in range(32):
        bst32[r, xs >= r + 1] = -BIG
    d["bst32"] = bst32
    bst8 = np.zeros((8, 896), np.float32)
    for r in range(8):
        bst8[r, xs >= r + 1] = -BIG
    d["bst8"] = bst8
    e32 = np.zeros((32, 512), np.float32)
    for q in range(512):
        e32[q // 16, q] = 1.0
    d["e32"] = e32
    e8 = np.zeros((8, 128), np.float32)
    for q in range(128):
        e8[q // 16, q] = 1.0
    d["e8"] = e8
    e8o = np.zeros((8, 1024), np.float32)
    for r in range(8):
        e8o[r, r * 128:(r + 1) * 128] = 1.0
    d["e8ones"] = e8o
    ebig = np.zeros((128, 16 * 128), np.float32)
    for j in range(16):
        for m in range(128):
            ebig[8 * j + m // 16, j * 128 + m] = BIG
    d["ebigt"] = ebig
    ownt = np.zeros((128, 128), np.float32)
    for k in range(128):
        for q in range(128):
            if k // 16 == q // 16 and k % 16 <= q % 16:
                ownt[k, q] = BIG
    d["ownt"] = ownt
    sm0 = np.full((128, 64), -BIG, np.float32)
    sm1 = np.full((128, 64), -BIG, np.float32)
    for k in range(128):
        for q in range(64):
            if k <= q:
                sm0[k, q] = 0.0
            if q <= k <= q + 64:
                sm1[k, q] = 0.0
    d["smaskt0"] = sm0
    d["smaskt1"] = sm1
    inv = (1.0 / 10000.0 ** (np.arange(0, DH, 2, dtype=np.float64) / DH))
    ang = inv[:, None] * np.arange(S, dtype=np.float64)[None, :]
    d["cosT"] = np.cos(ang).astype(np.float32)
    d["sinT"] = np.sin(ang).astype(np.float32)
    return d


def build_nc():
    nc = bass.Bass("TRN2", target_bir_lowering=False)

    def dram(name, shape, dt=F32, kind="ExternalInput"):
        return nc.dram_tensor(name, list(shape), dt, kind=kind).ap()

    xT = dram("xT", (D, NT))
    wqkvT = dram("wqkvT", (D, 768))
    wgT = dram("wgT", (D, 12))
    bg = dram("bg", (12, 1))
    kw1T = dram("kw1T", (2048, 2048))
    vw1T = dram("vw1T", (2048, 2048))
    kb1 = dram("kb1", (2048, 1))
    vb1 = dram("vb1", (2048, 1))
    kw2T = dram("kw2T", (2048, 128))
    vw2T = dram("vw2T", (2048, 128))
    kb2 = dram("kb2", (128, 1))
    vb2z = dram("vb2z", (128, 128))
    kposT = dram("kposT", (128, 16))
    vposT = dram("vposT", (128, 16))
    memk = dram("memk", (128, 1))
    memv = dram("memv", (1, 128))
    woT = dram("woT", (D, D))
    wfT = dram("wfT", (D, D))
    bfb_h = dram("bfb", (128, 32))
    laT = dram("laT", (D, 128))
    lbT = dram("lbT", (128, D))
    lw128 = dram("lw128", (128, 1))
    identc = dram("identc", (128, 128))
    ones128 = dram("ones128", (128, 1))
    e1t_h = dram("e1t", (128, 128))
    bst32_h = dram("bst32", (32, 896))
    bst8_h = dram("bst8", (8, 896))
    e32_h = dram("e32", (32, 512))
    e8_h = dram("e8", (8, 128))
    e8ones_h = dram("e8ones", (8, 1024))
    ebigt_h = dram("ebigt", (128, 2048))
    ownt_h = dram("ownt", (128, 128))
    sm0_h = dram("smaskt0", (128, 64))
    sm1_h = dram("smaskt1", (128, 64))
    cosT = dram("cosT", (64, S))
    sinT = dram("sinT", (64, S))

    y = dram("yT", (D, TPC), F32, "ExternalOutput")

    qsc = dram("qsc", (G, 128, NT), BF16, None) if False else \
        nc.dram_tensor("qsc", [G, 128, NT], BF16).ap()
    rqsc = nc.dram_tensor("rqsc", [G, 2, 64, NT], BF16).ap()
    gsc = nc.dram_tensor("gsc", [12, NT], BF16).ap()
    bfsc = nc.dram_tensor("bfsc", [G, NT], F32).ap()
    qnsc = nc.dram_tensor("qnsc", [G, NT], F32).ap()
    bcsc = nc.dram_tensor("bcsc", [G, NT], F32).ap()
    bisc = nc.dram_tensor("bisc", [1, NT], F32).ap()
    a2a_in_h = nc.dram_tensor("a2a_in", [D, TPC], BF16)
    a2a_out_h = nc.dram_tensor("a2a_out", [D, TPC], BF16)
    a2a_in = a2a_in_h.ap()
    a2a_out = a2a_out_h.ap()

    dbg = {}
    if DEBUG:
        for nm, shp in _DBG_SHAPES.items():
            dbg[nm] = dram("dbg_" + nm, shp, F32, "ExternalOutput")

    cnt = dict(ds=0, dg=0, ts=0, vs=0, ss=0, cs=0)

    st0 = contextlib.ExitStack()
    with st0:
        EC = st0.enter_context
        identb = EC(nc.sbuf_tensor("identb", [128, 128], BF16))
        ones128b = EC(nc.sbuf_tensor("ones128b", [128, 1], BF16))
        e1tb = EC(nc.sbuf_tensor("e1tb", [128, 128], BF16))
        e1tf = EC(nc.sbuf_tensor("e1tf", [128, 128], F32))
        mck2 = EC(nc.sbuf_tensor("mck2", [1, 1], F32))
        mckr = EC(nc.sbuf_tensor("mckr", [1, 1], F32))
        scal1 = EC(nc.sbuf_tensor("scal1", [1, 8], F32))
        sem_ds = EC(nc.semaphore("sem_ds"))
        sem_dg = EC(nc.semaphore("sem_dg"))
        sem_ts = EC(nc.semaphore("sem_ts"))
        sem_vs = EC(nc.semaphore("sem_vs"))
        sem_ss = EC(nc.semaphore("sem_ss"))
        sem_cs = EC(nc.semaphore("sem_cs"))
        block = EC(nc.Block())
        SEM = dict(ds=sem_ds, dg=sem_dg, ts=sem_ts, vs=sem_vs, ss=sem_ss, cs=sem_cs)
        OWNSEM = dict(sync="ds", gpsimd="dg", tensor="ts", vector="vs", scalar="ss")

        def vdr(e, inst):
            inst.then_inc(SEM["vs"], 1)
            cnt["vs"] += 1
            e.wait_ge(SEM["vs"], cnt["vs"])

        def grp(engname, fn, dman=0):
            waits = [(s, cnt[s]) for s in ("ds", "dg", "ts", "vs", "ss", "cs")
                     if cnt[s] > 0]
            own = OWNSEM[engname]
            n = 16 if engname in ("sync", "gpsimd") else 1

            def body(e):
                for s, vv in waits:
                    e.wait_ge(SEM[s], vv)
                last = fn(e)
                last.then_inc(SEM[own], n)
            getattr(block, engname)(body)
            cnt[own] += n + dman * 16

        def coll_grp(fn):
            waits = [(s, cnt[s]) for s in ("ds", "dg", "ts", "vs", "ss")
                     if cnt[s] > 0]

            def body(e):
                for s, vv in waits:
                    e.wait_ge(SEM[s], vv)
                fn(e).then_inc(SEM["cs"], 1)
            block.gpsimd(body)
            cnt["cs"] += 1

        st1 = contextlib.ExitStack()
        with st1:
            E1 = st1.enter_context
            kT0 = E1(nc.sbuf_tensor("kT0", [128, S], BF16))
            kT1 = E1(nc.sbuf_tensor("kT1", [128, S], BF16))
            rkT0 = E1(nc.sbuf_tensor("rkT0", [128, S], BF16))
            rkT1 = E1(nc.sbuf_tensor("rkT1", [128, S], BF16))
            vT0 = E1(nc.sbuf_tensor("vT0", [128, S], BF16))
            vT1 = E1(nc.sbuf_tensor("vT1", [128, S], BF16))
            vnat0 = E1(nc.sbuf_tensor("vnat0", [128, 32 * 129], BF16))
            vnat1 = E1(nc.sbuf_tensor("vnat1", [128, 32 * 129], BF16))
            vnatS0 = E1(nc.sbuf_tensor("vnatS0", [128, 32 * 129], BF16))
            vnatS1 = E1(nc.sbuf_tensor("vnatS1", [128, 32 * 129], BF16))
            ckT0 = E1(nc.sbuf_tensor("ckT0", [128, 385], BF16))
            ckT1 = E1(nc.sbuf_tensor("ckT1", [128, 385], BF16))
            cv0 = E1(nc.sbuf_tensor("cv0", [128, 3 * 129], BF16))
            cv1 = E1(nc.sbuf_tensor("cv1", [128, 3 * 129], BF16))
            kT = [kT0, kT1]
            rkT = [rkT0, rkT1]
            vT = [vT0, vT1]
            vnat = [vnat0, vnat1]
            vnatS = [vnatS0, vnatS1]
            ckT = [ckT0, ckT1]
            cvn = [cv0, cv1]

            def ld_consts(e):
                e.dma_start(out=identb[:], in_=identc[:, :]).then_inc(sem_dg, 16)
                e.dma_start(out=ones128b[:], in_=ones128[:, :]).then_inc(sem_dg, 16)
                e.dma_start(out=e1tb[:], in_=e1t_h[:, :]).then_inc(sem_dg, 16)
                return e.dma_start(out=e1tf[:], in_=e1t_h[:, :])
            grp("gpsimd", ld_consts, dman=3)

            def init_misc(e):
                e.memset(mck2[:], 0.0)
                for b in range(Bq):
                    e.memset(vnat[b][:], 1.0)
                    e.memset(vnatS[b][:], 1.0)
                    e.memset(ckT[b][:], 0.0)
                    e.memset(cvn[b][:], 0.0)
                return e.memset(scal1[:], 0.0)
            grp("vector", init_misc)

            stA = contextlib.ExitStack()
            with stA:
                EA = stA.enter_context
                wqvb = EA(nc.sbuf_tensor("wqvb", [128, 32 * 768], BF16))
                wgb = EA(nc.sbuf_tensor("wgb", [128, 32 * 12], BF16))
                bgb = EA(nc.sbuf_tensor("bgb", [12, 1], F32))
                cosb = EA(nc.sbuf_tensor("cosb", [64, TCH], BF16))
                sinb = EA(nc.sbuf_tensor("sinb", [64, TCH], BF16))
                xbuf = EA(nc.sbuf_tensor("xbuf", [128, 32 * TCH], BF16))
                qbuf = EA(nc.sbuf_tensor("qbuf", [128, G * TCH], BF16))
                hiA = EA(nc.sbuf_tensor("hiA", [64, 5 * TCH], BF16))
                rlo = EA(nc.sbuf_tensor("rlo", [64, 5 * TCH], BF16))
                rhi = EA(nc.sbuf_tensor("rhi", [64, 5 * TCH], BF16))
                rt1 = EA(nc.sbuf_tensor("rt1", [64, TCH], F32))
                rt2 = EA(nc.sbuf_tensor("rt2", [64, TCH], F32))
                sqb = EA(nc.sbuf_tensor("sqb", [128, TCH], BF16))
                selfb = EA(nc.sbuf_tensor("selfb", [128, TCH], BF16))
                selfrow = EA(nc.sbuf_tensor("selfrow", [1, TCH], F32))
                gbuf = EA(nc.sbuf_tensor("gbuf", [12, TCH], BF16))
                nrow = EA(nc.sbuf_tensor("nrow", [1, TCH], F32))
                n1 = EA(nc.sbuf_tensor("n1", [1, 1], F32))
                pq0 = EA(nc.psum_tensor("pq0", [128, TCH], F32))
                pq1 = EA(nc.psum_tensor("pq1", [128, TCH], F32))
                pq2 = EA(nc.psum_tensor("pq2", [128, TCH], F32))
                pq3 = EA(nc.psum_tensor("pq3", [128, TCH], F32))
                pk = EA(nc.psum_tensor("pk", [128, TCH], F32))
                pv = EA(nc.psum_tensor("pv", [128, TCH], F32))
                pgn = EA(nc.psum_tensor("pgn", [128, TCH], F32))
                ptb = EA(nc.psum_tensor("ptb", [128, 512], BF16))
                pq = [pq0, pq1, pq2, pq3]

                def ld_w(e):
                    src = wqkvT.rearrange("(kd p) f -> p kd f", p=128)
                    dst = wqvb.ap().rearrange("p (kd f) -> p kd f", kd=32)
                    e.dma_start(out=dst[:, :, :], in_=src[:, :, :]).then_inc(sem_dg, 16)
                    src = wgT.rearrange("(kd p) f -> p kd f", p=128)
                    dst = wgb.ap().rearrange("p (kd f) -> p kd f", kd=32)
                    e.dma_start(out=dst[:, :, :], in_=src[:, :, :]).then_inc(sem_dg, 16)
                    return e.dma_start(out=bgb[:], in_=bg[:, :])
                grp("gpsimd", ld_w, dman=2)

                for ci in range(NT // TCH):
                    b, tc = ci // (S // TCH), ci % (S // TCH)
                    t0 = tc * TCH
                    gt0 = b * S + t0

                    def ld_x(e, gt0=gt0, t0=t0):
                        src = xT.rearrange("(kd p) t -> p kd t", p=128)
                        dst = xbuf.ap().rearrange("p (kd t) -> p kd t", kd=32)
                        e.dma_start(out=dst[:, :, :],
                                    in_=src[:, :, gt0:gt0 + TCH]).then_inc(sem_dg, 16)
                        e.dma_start(out=cosb[:], in_=cosT[:, t0:t0 + TCH]).then_inc(sem_dg, 16)
                        return e.dma_start(out=sinb[:], in_=sinT[:, t0:t0 + TCH])
                    grp("gpsimd", ld_x, dman=2)

                    def mm_qkv(e):
                        for m in range(6):
                            dstp = (pq + [pk, pv])[m]
                            for kd in range(32):
                                e.matmul(dstp[:, :],
                                         wqvb[:, kd * 768 + m * 128: kd * 768 + (m + 1) * 128],
                                         xbuf[:, kd * TCH:(kd + 1) * TCH],
                                         start=(kd == 0), stop=(kd == 31),
                                         skip_group_check=True)
                        for kd in range(32):
                            last = e.matmul(pgn[0:12, :],
                                            wgb[:, kd * 12:(kd + 1) * 12],
                                            xbuf[:, kd * TCH:(kd + 1) * TCH],
                                            start=(kd == 0), stop=(kd == 31),
                                            skip_group_check=True)
                        return last
                    grp("tensor", mm_qkv)

                    def ev1(e, b=b, t0=t0):
                        for g in range(G):
                            e.tensor_copy(qbuf[:, g * TCH:(g + 1) * TCH], pq[g][:, :])
                        e.tensor_copy(kT[b][:, t0:t0 + TCH], pk[:, :])
                        return e.tensor_copy(vT[b][:, t0:t0 + TCH], pv[:, :])
                    grp("vector", ev1)

                    def ev_g(e):
                        return e.activation(gbuf[:, :], pgn[0:12, :], ACT.Sigmoid,
                                            bias=bgb[:, :], scale=1.0)
                    grp("scalar", ev_g)

                    def st_g(e, gt0=gt0):
                        return e.dma_start(out=gsc[:, gt0:gt0 + TCH], in_=gbuf[:, :])
                    grp("sync", st_g)

                    def dshift(e, b=b, t0=t0, gt0=gt0):
                        for g in range(G):
                            e.dma_start(out=hiA[:, g * TCH:(g + 1) * TCH],
                                        in_=qbuf[64:128, g * TCH:(g + 1) * TCH]
                                        ).then_inc(sem_ds, 16)
                            e.dma_start(out=qsc[g, :, gt0:gt0 + TCH],
                                        in_=qbuf[:, g * TCH:(g + 1) * TCH]
                                        ).then_inc(sem_ds, 16)
                        return e.dma_start(out=hiA[:, 4 * TCH:5 * TCH],
                                           in_=kT[b][64:128, t0:t0 + TCH])
                    grp("sync", dshift, dman=8)

                    def rope(e, b=b, t0=t0):
                        for u in range(5):
                            lo = (qbuf[0:64, u * TCH:(u + 1) * TCH] if u < 4
                                  else kT[b][0:64, t0:t0 + TCH])
                            hi = hiA[:, u * TCH:(u + 1) * TCH]
                            e.tensor_mul(rt1[:, :], lo, cosb[:, :])
                            e.tensor_mul(rt2[:, :], hi, sinb[:, :])
                            e.tensor_sub(rlo[:, u * TCH:(u + 1) * TCH], rt1[:, :], rt2[:, :])
                            e.tensor_mul(rt1[:, :], hi, cosb[:, :])
                            e.tensor_mul(rt2[:, :], lo, sinb[:, :])
                            e.tensor_add(rhi[:, u * TCH:(u + 1) * TCH], rt1[:, :], rt2[:, :])
                        return e.tensor_copy(rkT[b][0:64, t0:t0 + TCH],
                                             rlo[:, 4 * TCH:5 * TCH])
                    grp("vector", rope)

                    def st_rq(e, b=b, t0=t0, gt0=gt0):
                        for g in range(G):
                            e.dma_start(out=rqsc[g, 0, :, gt0:gt0 + TCH],
                                        in_=rlo[:, g * TCH:(g + 1) * TCH]).then_inc(sem_ds, 16)
                            e.dma_start(out=rqsc[g, 1, :, gt0:gt0 + TCH],
                                        in_=rhi[:, g * TCH:(g + 1) * TCH]).then_inc(sem_ds, 16)
                        return e.dma_start(out=rkT[b][64:128, t0:t0 + TCH],
                                           in_=rhi[:, 4 * TCH:5 * TCH])
                    grp("sync", st_rq, dman=8)

                    def tr_v(e, b=b, t0=t0):
                        for j in range(TCH // 128):
                            last = e.transpose(ptb[:, j * 128:(j + 1) * 128],
                                               vT[b][:, t0 + j * 128: t0 + (j + 1) * 128],
                                               identb[:, :])
                        return last
                    grp("tensor", tr_v)

                    def ev_v(e, b=b, t0=t0):
                        for j in range(TCH // 128):
                            kt2 = t0 // 128 + j
                            last = e.tensor_copy(vnat[b][:, kt2 * 129: kt2 * 129 + 128],
                                                 ptb[:, j * 128:(j + 1) * 128])
                        return last
                    grp("vector", ev_v)

                    for g in range(G):
                        def sq1(e, g=g, b=b, t0=t0):
                            e.tensor_mul(sqb[:, :],
                                         qbuf[:, g * TCH:(g + 1) * TCH],
                                         qbuf[:, g * TCH:(g + 1) * TCH])
                            return e.tensor_mul(selfb[:, :],
                                                qbuf[:, g * TCH:(g + 1) * TCH],
                                                kT[b][:, t0:t0 + TCH])
                        grp("vector", sq1)

                        def nmm(e):
                            e.matmul(pq1[0:1, :], ones128b[:, :], sqb[:, :],
                                     start=True, stop=True, skip_group_check=True)
                            return e.matmul(pq2[0:1, :], ones128b[:, :], selfb[:, :],
                                            start=True, stop=True, skip_group_check=True)
                        grp("tensor", nmm)

                        def nred(e):
                            e.tensor_copy(nrow[:, :], pq1[0:1, :])
                            e.tensor_copy(selfrow[:, :], pq2[0:1, :])
                            return e.tensor_scalar(selfrow[:, :], selfrow[:, :],
                                                   80.0 / SCALE, -1.0, AL.add, AL.mult)
                        grp("vector", nred)

                        def st_n(e, g=g, gt0=gt0):
                            e.dma_start(out=qnsc[g, gt0:gt0 + TCH],
                                        in_=nrow[:, :]).then_inc(sem_ds, 16)
                            return e.dma_start(out=bfsc[g, gt0:gt0 + TCH],
                                               in_=selfrow[:, :])
                        grp("sync", st_n, dman=1)

            stB = contextlib.ExitStack()
            with stB:
                EB = stB.enter_context
                ckin = EB(nc.sbuf_tensor("ckin", [128, 16 * 512], BF16))
                w1b = EB(nc.sbuf_tensor("w1b", [128, 16 * 128], BF16))
                w2b = EB(nc.sbuf_tensor("w2b", [128, 16 * 128], BF16))
                b1b = EB(nc.sbuf_tensor("b1b", [128, 16], F32))
                b2b = EB(nc.sbuf_tensor("b2b", [128, 1], F32))
                vb2b = EB(nc.sbuf_tensor("vb2b", [128, 128], BF16))
                posb = EB(nc.sbuf_tensor("posb", [128, 16], F32))
                reluT = EB(nc.sbuf_tensor("reluT", [128, 16 * RSTR], BF16))
                sqc = EB(nc.sbuf_tensor("sqc", [128, 512], BF16))
                nrow2 = EB(nc.sbuf_tensor("nrow2", [1, 512], F32))
                qrow = EB(nc.sbuf_tensor("qrow", [1, S], F32))
                qrow2 = EB(nc.sbuf_tensor("qrow2", [1, S], F32))
                qacc = EB(nc.sbuf_tensor("qacc", [1, S], F32))
                n2 = EB(nc.sbuf_tensor("n2", [1, 1], F32))
                ph = EB(nc.psum_tensor("ph", [128, 512], F32))
                pck = EB(nc.psum_tensor("pck", [128, 256], F32))
                pcv = EB(nc.psum_tensor("pcv", [128, 128], F32))
                pnn = EB(nc.psum_tensor("pnn", [1, 512], F32))
                pbb = EB(nc.psum_tensor("pbb", [128, 1], F32))
                for kind in range(2):
                    w1ap = kw1T if kind == 0 else vw1T
                    w2ap = kw2T if kind == 0 else vw2T
                    b1ap = kb1 if kind == 0 else vb1
                    posap = kposT if kind == 0 else vposT
                    cache = kT if kind == 0 else vT

                    def ld_small(e, w2ap=w2ap, b1ap=b1ap, posap=posap, kind=kind):
                        src = w2ap.rearrange("(u p) f -> p u f", p=128)
                        dst = w2b.ap().rearrange("p (u f) -> p u f", u=16)
                        e.dma_start(out=dst[:, :, :], in_=src[:, :, :]).then_inc(sem_dg, 16)
                        src = b1ap.rearrange("(u p) one -> p u one", p=128)
                        e.dma_start(out=b1b.ap()[:, :], in_=src[:, :, 0],
                                    allow_slow_non_contiguous=True).then_inc(sem_dg, 16)
                        e.dma_start(out=posb[:, :], in_=posap[:, :]).then_inc(sem_dg, 16)
                        if kind == 0:
                            return e.dma_start(out=b2b[:, :], in_=kb2[:, :])
                        return e.dma_start(out=vb2b[:, :], in_=vb2z[:, :])
                    grp("gpsimd", ld_small, dman=3)

                    def mk_in(e, cache=cache):
                        for p in range(16):
                            for b in range(Bq):
                                src = cache[b].ap().rearrange("d (c p) -> d c p", p=16)
                                e.tensor_scalar_add(
                                    ckin[:, p * 512 + b * 256: p * 512 + (b + 1) * 256],
                                    src[:, :, p], posb[:, p:p + 1])
                        return e.memset(reluT[:], 0.0)
                    grp("vector", mk_in)

                    for u in range(16):
                        def ld_w1(e, u=u, w1ap=w1ap):
                            src = w1ap.rearrange("(p pp) f -> pp p f", pp=128)
                            dst = w1b.ap().rearrange("pp (p f) -> pp p f", p=16)
                            return e.dma_start(out=dst[:, :, :],
                                               in_=src[:, :, u * 128:(u + 1) * 128])
                        grp("gpsimd", ld_w1)

                        def mm_h(e):
                            for p in range(16):
                                last = e.matmul(ph[:, :], w1b[:, p * 128:(p + 1) * 128],
                                                ckin[:, p * 512:(p + 1) * 512],
                                                start=(p == 0), stop=(p == 15),
                                                skip_group_check=True)
                            return last
                        grp("tensor", mm_h)

                        def ev_h(e, u=u):
                            return e.activation(reluT[:, u * RSTR + 1: u * RSTR + 513],
                                                ph[:, :], ACT.Relu,
                                                bias=b1b[:, u:u + 1], scale=1.0)
                        grp("scalar", ev_h)

                    if kind == 0:
                        for b in range(Bq):
                            def mm_ck(e, b=b):
                                for u in range(16):
                                    last = e.matmul(
                                        pck[:, :], w2b[:, u * 128:(u + 1) * 128],
                                        reluT[:, u * RSTR + 1 + b * 256:
                                              u * RSTR + 1 + (b + 1) * 256],
                                        start=(u == 0), stop=(u == 15),
                                        skip_group_check=True)
                                return last
                            grp("tensor", mm_ck)

                            def ev_ck(e, b=b):
                                e.tensor_copy(ckT[b][:, 1:257], pck[:, :])
                                return e.tensor_scalar_add(ckT[b][:, 1:257],
                                                           ckT[b][:, 1:257], b2b[:, :])
                            grp("vector", ev_ck)

                        def ld_mem(e):
                            e.dma_start(out=ckT[0][:, 0:1], in_=memk[:, :]).then_inc(sem_dg, 16)
                            return e.dma_start(out=ckT[1][:, 0:1], in_=memk[:, :])
                        grp("gpsimd", ld_mem, dman=1)
                    else:
                        for b in range(Bq):
                            for ct in range(3):
                                def mm_cv(e, b=b, ct=ct):
                                    for u in range(16):
                                        e.matmul(
                                            pcv[:, :],
                                            reluT[:, u * RSTR + b * 256 + ct * 128:
                                                  u * RSTR + b * 256 + ct * 128 + 128],
                                            w2b[:, u * 128:(u + 1) * 128],
                                            start=(u == 0), stop=False,
                                            skip_group_check=True)
                                    return e.matmul(pcv[:, :], e1tb[:, :], vb2b[:, :],
                                                    start=False, stop=True,
                                                    skip_group_check=True)
                                grp("tensor", mm_cv)

                                def ev_cv(e, b=b, ct=ct):
                                    return e.tensor_copy(
                                        cvn[b][:, ct * 129: ct * 129 + 128], pcv[:, :])
                                grp("vector", ev_cv)

                        def ld_memv(e):
                            e.dma_start(out=cvn[0][0:1, 0:128], in_=memv[:, :]).then_inc(sem_dg, 16)
                            return e.dma_start(out=cvn[1][0:1, 0:128], in_=memv[:, :])
                        grp("gpsimd", ld_memv, dman=1)

                        def fix_cv(e):
                            for b in range(Bq):
                                v = cvn[b].ap().rearrange("p (c t) -> p c t", c=3)
                                e.memset(v[:, :, 128:129], 1.0)
                            return e.memset(sqc[:], 0.0)
                        grp("vector", fix_cv)

                for b in range(Bq):
                    def sq_ck(e, b=b):
                        return e.tensor_mul(sqc[:, 0:385], ckT[b][:, :], ckT[b][:, :])
                    grp("vector", sq_ck)

                    def nmm2(e):
                        return e.matmul(pnn[:, 0:385], ones128b[:, :], sqc[:, 0:385],
                                        start=True, stop=True, skip_group_check=True)
                    grp("tensor", nmm2)

                    def nred2(e):
                        e.tensor_copy(nrow2[:, 0:385], pnn[:, 0:385])
                        e.reduce_max(n2[:, :], nrow2[:, 0:385], axis=AX.X)
                        return e.tensor_max(mck2[:, :], mck2[:, :], n2[:, :])
                    grp("vector", nred2)

                def mk_b1(e):
                    return e.tensor_copy(scal1[:, 0:1], mck2[:, :])
                grp("vector", mk_b1)

                def mk_b2(e):
                    return e.activation(scal1[:, 1:2], scal1[:, 0:1], ACT.Sqrt,
                                        bias=0.0, scale=1.0)
                grp("scalar", mk_b2)

                def mk_b3(e):
                    return e.tensor_scalar_mul(mckr[:, :], scal1[0:1, 1:2], SCALE)
                grp("vector", mk_b3)

                for b in range(Bq):
                    def z_acc(e):
                        return e.memset(qacc[:, :], 0.0)
                    grp("vector", z_acc)
                    for g in range(G):
                        def ld_qn(e, g=g, b=b):
                            return e.dma_start(out=qrow[:, :],
                                               in_=qnsc[g, b * S:(b + 1) * S])
                        grp("sync", ld_qn)

                        def qn_sqrt(e):
                            return e.activation(qrow[:, :], qrow[:, :], ACT.Sqrt,
                                                bias=0.0, scale=1.0)
                        grp("scalar", qn_sqrt)

                        def mk_bc(e):
                            e.tensor_add(qacc[:, :], qacc[:, :], qrow[:, :])
                            e.tensor_scalar(qrow2[:, :], qrow[:, :],
                                            mckr[:, :], -78.0, AL.mult, AL.add)
                            e.tensor_scalar_max(qrow2[:, :], qrow2[:, :], 0.0)
                            return e.tensor_scalar_mul(qrow2[:, :], qrow2[:, :],
                                                       -1.0 / SCALE)
                        grp("vector", mk_bc)

                        def st_bc(e, g=g, b=b):
                            return e.dma_start(out=bcsc[g, b * S:(b + 1) * S],
                                               in_=qrow2[:, :])
                        grp("sync", st_bc)

                    def mk_bi(e):
                        e.tensor_scalar_mul(qacc[:, :], qacc[:, :], 0.25)
                        e.tensor_scalar(qacc[:, :], qacc[:, :],
                                        mckr[:, :], -78.0, AL.mult, AL.add)
                        e.tensor_scalar_max(qacc[:, :], qacc[:, :], 0.0)
                        return e.tensor_scalar_mul(qacc[:, :], qacc[:, :], -1.0)
                    grp("vector", mk_bi)

                    def st_bi(e, b=b):
                        return e.dma_start(out=bisc[0, b * S:(b + 1) * S],
                                           in_=qacc[:, :])
                    grp("sync", st_bi)

                def mk_vs(e):
                    for b in range(Bq):
                        sv = vnat[b].ap().rearrange("p (c t) -> p c t", c=32)
                        dv = vnatS[b].ap().rearrange("p (c t) -> p c t", c=32)
                        e.dma_start(out=dv[0:64, :, :], in_=sv[64:128, :, :]).then_inc(sem_ds, 16)
                        if b == 1:
                            return e.dma_start(out=dv[64:128, 0:31, :], in_=sv[0:64, 1:32, :])
                        e.dma_start(out=dv[64:128, 0:31, :], in_=sv[0:64, 1:32, :]).then_inc(sem_ds, 16)
                grp("sync", mk_vs, dman=3)

            stC = contextlib.ExitStack()
            with stC:
                EE = stC.enter_context
                bst32b = EE(nc.sbuf_tensor("bst32b", [32, 896], BF16))
                bst8b = EE(nc.sbuf_tensor("bst8b", [8, 896], BF16))
                e32b = EE(nc.sbuf_tensor("e32b", [32, 512], BF16))
                e8b = EE(nc.sbuf_tensor("e8b", [8, 128], BF16))
                e8onesb = EE(nc.sbuf_tensor("e8onesb", [8, 1024], BF16))
                brows = EE(nc.sbuf_tensor("brows", [8, 512], BF16))
                bitile = EE(nc.sbuf_tensor("bitile", [128, 4], F32))
                mx4 = EE(nc.sbuf_tensor("mx4", [128, 1], F32))
                ebigtb = EE(nc.sbuf_tensor("ebigtb", [128, 2048], BF16))
                owntb = EE(nc.sbuf_tensor("owntb", [128, 128], BF16))
                sm0b = EE(nc.sbuf_tensor("sm0b", [128, 64], BF16))
                sm1b = EE(nc.sbuf_tensor("sm1b", [128, 64], BF16))
                qTb = EE(nc.sbuf_tensor("qTb", [128, G * 512], BF16))
                rqTb = EE(nc.sbuf_tensor("rqTb", [128, G * 512], BF16))
                qsT = EE(nc.sbuf_tensor("qsT", [128, 512], BF16))
                gch = EE(nc.sbuf_tensor("gch", [12, 512], BF16))
                gnat = EE(nc.sbuf_tensor("gnat", [128, 12], F32))
                pimp = EE(nc.sbuf_tensor("pimp", [128, 384], F32))
                mx8 = EE(nc.sbuf_tensor("mx8", [128, 8], F32))
                zap = EE(nc.sbuf_tensor("zap", [128, 384], F32))
                selw = EE(nc.sbuf_tensor("selw", [128, 384], F32))
                selb = EE(nc.sbuf_tensor("selb", [128, 384], BF16))
                zrow = EE(nc.sbuf_tensor("zrow", [128, 1], F32))
                thr = EE(nc.sbuf_tensor("thr", [128, 1], F32))
                selBIP = EE(nc.sbuf_tensor("selBIP", [128, 3 * 512], BF16))
                pcT = EE(nc.sbuf_tensor("pcT", [128, 3 * 512], BF16))
                pfT = EE(nc.sbuf_tensor("pfT", [128, 512], BF16))
                psT = EE(nc.sbuf_tensor("psT", [128, 512], BF16))
                recb = EE(nc.sbuf_tensor("recb", [128, 1], F32))
                couts = EE(nc.sbuf_tensor("couts", [128, G * 4 * 128], BF16))
                fouts = EE(nc.sbuf_tensor("fouts", [128, G * 4 * 128], BF16))
                souts = EE(nc.sbuf_tensor("souts", [128, G * 4 * 128], BF16))
                mbuf = EE(nc.sbuf_tensor("mbuf", [128, 512], BF16))
                mTb = EE(nc.sbuf_tensor("mTb", [128, 512], BF16))
                bPS = EE(nc.psum_tensor("bPS", [128, 512], F32))
                bPC = EE(nc.psum_tensor("bPC", [128, 512], F32))
                bPV0 = EE(nc.psum_tensor("bPV0", [128, 129], F32))
                bPV1 = EE(nc.psum_tensor("bPV1", [128, 129], F32))
                bPV2 = EE(nc.psum_tensor("bPV2", [128, 129], F32))
                bPV3 = EE(nc.psum_tensor("bPV3", [128, 129], F32))
                bPD = EE(nc.psum_tensor("bPD", [128, 129], F32))
                bPT = EE(nc.psum_tensor("bPT", [128, 512], BF16))
                bPV = [bPV0, bPV1, bPV2, bPV3]

                def ld_bconsts(e):
                    e.dma_start(out=bst32b[:], in_=bst32_h[:, :]).then_inc(sem_dg, 16)
                    e.dma_start(out=bst8b[:], in_=bst8_h[:, :]).then_inc(sem_dg, 16)
                    e.dma_start(out=e32b[:], in_=e32_h[:, :]).then_inc(sem_dg, 16)
                    e.dma_start(out=e8b[:], in_=e8_h[:, :]).then_inc(sem_dg, 16)
                    e.dma_start(out=e8onesb[:], in_=e8ones_h[:, :]).then_inc(sem_dg, 16)
                    e.dma_start(out=ebigtb[:], in_=ebigt_h[:, :]).then_inc(sem_dg, 16)
                    e.dma_start(out=owntb[:], in_=ownt_h[:, :]).then_inc(sem_dg, 16)
                    e.dma_start(out=sm0b[:], in_=sm0_h[:, :]).then_inc(sem_dg, 16)
                    return e.dma_start(out=sm1b[:], in_=sm1_h[:, :])
                grp("gpsimd", ld_bconsts, dman=8)

                for b in range(Bq):
                    for qc in range(8):
                        off = b * S + qc * 512
                        wp = 32 * qc + 33
                        nct = (wp + 127) // 128
                        wpad = nct * 128
                        base = 32 * qc
                        nkt = 4 * (qc + 1)

                        def ld_q(e, off=off):
                            for g in range(G):
                                e.dma_start(out=qTb[:, g * 512:(g + 1) * 512],
                                            in_=qsc[g, :, off:off + 512]).then_inc(sem_ds, 16)
                                e.dma_start(out=rqTb[0:64, g * 512:(g + 1) * 512],
                                            in_=rqsc[g, 0, :, off:off + 512]).then_inc(sem_ds, 16)
                                e.dma_start(out=rqTb[64:128, g * 512:(g + 1) * 512],
                                            in_=rqsc[g, 1, :, off:off + 512]).then_inc(sem_ds, 16)
                            return e.dma_start(out=gch[:, :], in_=gsc[:, off:off + 512])
                        grp("sync", ld_q, dman=12)

                        def ld_br(e, off=off):
                            for g in range(G):
                                e.dma_start(out=brows[g:g + 1, :],
                                            in_=bcsc[g, off:off + 512]).then_inc(sem_dg, 16)
                                e.dma_start(out=brows[4 + g:5 + g, :],
                                            in_=bfsc[g, off:off + 512]).then_inc(sem_dg, 16)
                            for it in range(4):
                                if it == 3:
                                    return e.dma_start(
                                        out=bitile[:, it:it + 1],
                                        in_=bisc[0, off + it * 128: off + it * 128 + 128])
                                e.dma_start(out=bitile[:, it:it + 1],
                                            in_=bisc[0, off + it * 128: off + it * 128 + 128]
                                            ).then_inc(sem_dg, 16)
                        grp("gpsimd", ld_br, dman=11)

                        def mk_qs(e):
                            e.tensor_add(qsT[:, :], qTb[:, 0:512], qTb[:, 512:1024])
                            e.tensor_add(qsT[:, :], qsT[:, :], qTb[:, 1024:1536])
                            return e.tensor_add(qsT[:, :], qsT[:, :], qTb[:, 1536:2048])
                        grp("vector", mk_qs)

                        for it in range(4):
                            ibase = base + 8 * it

                            def mm_imp(e, it=it, ibase=ibase):
                                e.matmul(bPC[:, 0:384], e8b[:, :],
                                         bst8b[:, 257 - ibase: 257 - ibase + 384],
                                         start=True, stop=False, skip_group_check=True)
                                return e.matmul(bPC[:, 0:384],
                                                qsT[:, it * 128:(it + 1) * 128],
                                                ckT[b][:, 1:385],
                                                start=False, stop=True,
                                                skip_group_check=True)
                            grp("tensor", mm_imp)

                            def exp_imp(e, it=it):
                                return e.activation(pimp[:, :], bPC[:, 0:384],
                                                    ACT.Exp, bias=bitile[:, it:it + 1],
                                                    scale=SCALE / 4.0)
                            grp("scalar", exp_imp)

                            def sel(e):
                                e.reduce_sum(zrow[:, :], pimp[:, :], axis=AX.X)
                                vdr(e, e.max(out=mx8[:, :], in_=pimp[:, :]))
                                e.tensor_scalar_mul(thr[:, :], zrow[:, :], 1e-10)
                                vdr(e, e.tensor_copy(mx4[:, :], mx8[:, 3:4]))
                                e.tensor_scalar(selw[:, :], pimp[:, :],
                                                mx4[:, :], None, AL.is_ge)
                                e.tensor_scalar(zap[:, :], pimp[:, :],
                                                thr[:, :], None, AL.is_gt)
                                e.tensor_mul(selw[:, :], selw[:, :], zap[:, :])
                                return e.tensor_copy(selb[:, :], selw[:, :])
                            grp("vector", sel)

                            def tr_sel(e):
                                for ct in range(3):
                                    last = e.transpose(bPT[:, ct * 128:(ct + 1) * 128],
                                                       selb[:, ct * 128:(ct + 1) * 128],
                                                       identb[:, :])
                                return last
                            grp("tensor", tr_sel)

                            def ev_sel(e, it=it):
                                for ct in range(3):
                                    e.tensor_copy(
                                        selBIP[:, ct * 512 + it * 128:
                                               ct * 512 + (it + 1) * 128],
                                        bPT[:, ct * 128:(ct + 1) * 128])
                                    last = e.tensor_scalar_add(
                                        selBIP[:, ct * 512 + it * 128:
                                               ct * 512 + (it + 1) * 128],
                                        selBIP[:, ct * 512 + it * 128:
                                               ct * 512 + (it + 1) * 128], -1.0)
                                return last
                            grp("vector", ev_sel)

                        for g in range(G):
                            qg = qTb[:, g * 512:(g + 1) * 512]
                            rqg = rqTb[:, g * 512:(g + 1) * 512]

                            for ct in range(nct):
                                def mm_c(e, ct=ct, qg=qg, g=g):
                                    col = 256 + 128 * ct - base
                                    e.matmul(bPC[:, :], bst32b[:, col:col + 128],
                                             e32b[:, :], start=True, stop=False,
                                             skip_group_check=True)
                                    e.matmul(bPC[:, :], e8onesb[:, g * 128:(g + 1) * 128],
                                             brows[:, :], start=False, stop=False,
                                             skip_group_check=True)
                                    return e.matmul(bPC[:, :],
                                                    ckT[b][:, ct * 128:(ct + 1) * 128],
                                                    qg, start=False, stop=True,
                                                    skip_group_check=True)
                                grp("tensor", mm_c)

                                def exp_c(e, ct=ct):
                                    return e.activation(pcT[:, ct * 512:(ct + 1) * 512],
                                                        bPC[:, :], ACT.Exp,
                                                        bias=0.0, scale=SCALE)
                                grp("scalar", exp_c)

                            for it in range(4):
                                def mm_cpv(e, it=it, nct=nct):
                                    for ct in range(nct):
                                        last = e.matmul(
                                            bPD[:, :],
                                            pcT[:, ct * 512 + it * 128:
                                                ct * 512 + (it + 1) * 128],
                                            cvn[b][:, ct * 129:(ct + 1) * 129],
                                            start=(ct == 0), stop=(ct == nct - 1),
                                            skip_group_check=True)
                                    return last
                                grp("tensor", mm_cpv)

                                def div_c(e, it=it, g=g):
                                    o = (g * 4 + it) * 128
                                    vdr(e, e.reciprocal(recb[:, :], bPD[:, 128:129]))
                                    e.tensor_copy(couts[:, o:o + 128], bPD[:, 0:128])
                                    return e.tensor_scalar_mul(
                                        couts[:, o:o + 128], couts[:, o:o + 128],
                                        recb[:, :])
                                grp("vector", div_c)

                            for kt in range(nkt):
                                def mm_f(e, kt=kt, rqg=rqg, qc=qc, g=g):
                                    ct = kt // 16
                                    j = kt % 16
                                    e.matmul(bPS[:, :], ebigtb[:, j * 128:(j + 1) * 128],
                                             selBIP[:, ct * 512:(ct + 1) * 512],
                                             start=True, stop=False,
                                             skip_group_check=True)
                                    it2 = kt - 4 * qc
                                    if 0 <= it2 < 4:
                                        e.matmul(bPS[:, it2 * 128:(it2 + 1) * 128],
                                                 identb[:, :], owntb[:, :],
                                                 start=False, stop=False,
                                                 skip_group_check=True)
                                    e.matmul(bPS[:, :],
                                             e8onesb[:, (4 + g) * 128:(5 + g) * 128],
                                             brows[:, :], start=False, stop=False,
                                             skip_group_check=True)
                                    return e.matmul(bPS[:, :],
                                                    rkT[b][:, kt * 128:(kt + 1) * 128],
                                                    rqg, start=False, stop=True,
                                                    skip_group_check=True)
                                grp("tensor", mm_f)

                                def exp_f(e):
                                    return e.activation(pfT[:, :], bPS[:, :], ACT.Exp,
                                                        bias=0.0, scale=SCALE)
                                grp("scalar", exp_f)

                                def mm_fpv(e, kt=kt, qc=qc):
                                    for it2 in range(max(0, kt - 4 * qc), 4):
                                        last = e.matmul(
                                            bPV[it2][:, :],
                                            pfT[:, it2 * 128:(it2 + 1) * 128],
                                            vnat[b][:, kt * 129: kt * 129 + 129],
                                            start=(kt == 0),
                                            stop=(kt == 4 * qc + it2),
                                            skip_group_check=True)
                                    return last
                                grp("tensor", mm_fpv)

                            for it in range(4):
                                def div_f(e, it=it, g=g):
                                    o = (g * 4 + it) * 128
                                    vdr(e, e.reciprocal(recb[:, :], bPV[it][:, 128:129]))
                                    e.tensor_copy(fouts[:, o:o + 128], bPV[it][:, 0:128])
                                    return e.tensor_scalar_mul(
                                        fouts[:, o:o + 128], fouts[:, o:o + 128],
                                        recb[:, :])
                                grp("vector", div_f)

                            def mm_s(e, rqg=rqg, qc=qc, g=g):
                                e.matmul(bPS[:, :],
                                         e8onesb[:, (4 + g) * 128:(5 + g) * 128],
                                         brows[:, :], start=True, stop=False,
                                         skip_group_check=True)
                                for wl in range(8):
                                    w = 8 * qc + wl
                                    ocol = wl * 64
                                    if w == 0:
                                        e.matmul(bPS[:, ocol:ocol + 64], identb[:, :],
                                                 sm0b[:, :], start=False, stop=False,
                                                 skip_group_check=True)
                                        last = e.matmul(bPS[:, ocol:ocol + 64],
                                                        rkT[b][:, 0:128],
                                                        rqg[:, 0:64],
                                                        start=False, stop=True,
                                                        skip_group_check=True)
                                    else:
                                        e.matmul(bPS[:, ocol:ocol + 64], identb[:, :],
                                                 sm1b[:, :], start=False, stop=False,
                                                 skip_group_check=True)
                                        last = e.matmul(bPS[:, ocol:ocol + 64],
                                                        rkT[b][:, 64 * (w - 1): 64 * (w + 1)],
                                                        rqg[:, wl * 64: wl * 64 + 64],
                                                        start=False, stop=True,
                                                        skip_group_check=True)
                                return last
                            grp("tensor", mm_s)

                            def exp_s(e):
                                return e.activation(psT[:, :], bPS[:, :], ACT.Exp,
                                                    bias=0.0, scale=SCALE)
                            grp("scalar", exp_s)

                            def mm_spv(e, qc=qc):
                                for wl in range(8):
                                    w = 8 * qc + wl
                                    it2 = wl // 2
                                    half = wl % 2
                                    if w == 0:
                                        rv = vnat[b][:, 0:129]
                                    elif w % 2 == 1:
                                        kt2 = (w - 1) // 2
                                        rv = vnat[b][:, kt2 * 129: kt2 * 129 + 129]
                                    else:
                                        kt2 = (w - 2) // 2
                                        rv = vnatS[b][:, kt2 * 129: kt2 * 129 + 129]
                                    last = e.matmul(
                                        bPV[it2][64 * half:64 * half + 64, :],
                                        psT[:, wl * 64: wl * 64 + 64], rv,
                                        start=True, stop=True,
                                        tile_position=(0, 64 * half),
                                        skip_group_check=True)
                                return last
                            grp("tensor", mm_spv)

                            for it in range(4):
                                def div_s(e, it=it, g=g):
                                    o = (g * 4 + it) * 128
                                    vdr(e, e.reciprocal(recb[:, :], bPV[it][:, 128:129]))
                                    e.tensor_copy(souts[:, o:o + 128], bPV[it][:, 0:128])
                                    return e.tensor_scalar_mul(
                                        souts[:, o:o + 128], souts[:, o:o + 128],
                                        recb[:, :])
                                grp("vector", div_s)

                        for it in range(4):
                            def tr_g(e, it=it):
                                return e.transpose(bPT[:, 0:12],
                                                   gch[:, it * 128: it * 128 + 128],
                                                   identb[0:12, 0:12])
                            grp("tensor", tr_g)

                            def ev_g2(e):
                                return e.tensor_copy(gnat[:, 0:12], bPT[:, 0:12])
                            grp("vector", ev_g2)

                            def mrg(e, it=it):
                                for g in range(G):
                                    o = (g * 4 + it) * 128
                                    e.tensor_scalar_mul(mbuf[:, g * 128:(g + 1) * 128],
                                                        couts[:, o:o + 128],
                                                        gnat[:, 3 * g: 3 * g + 1])
                                    e.scalar_tensor_tensor(
                                        mbuf[:, g * 128:(g + 1) * 128],
                                        fouts[:, o:o + 128],
                                        gnat[:, 3 * g + 1: 3 * g + 2],
                                        mbuf[:, g * 128:(g + 1) * 128], AL.mult, AL.add)
                                    last = e.scalar_tensor_tensor(
                                        mbuf[:, g * 128:(g + 1) * 128],
                                        souts[:, o:o + 128],
                                        gnat[:, 3 * g + 2: 3 * g + 3],
                                        mbuf[:, g * 128:(g + 1) * 128], AL.mult, AL.add)
                                return last
                            grp("vector", mrg)

                            def tr_m(e):
                                for j in range(4):
                                    last = e.transpose(bPT[:, j * 128:(j + 1) * 128],
                                                       mbuf[:, j * 128:(j + 1) * 128],
                                                       identb[:, :])
                                return last
                            grp("tensor", tr_m)

                            def ev_m(e):
                                return e.tensor_copy(mTb[:, :], bPT[:, :])
                            grp("vector", ev_m)

                            def st_m(e, b=b, qc=qc, it=it):
                                tglob = b * S + qc * 512 + it * 128
                                owner = tglob // TPC
                                tcol = tglob % TPC
                                dst = a2a_in[owner * 512:(owner + 1) * 512,
                                             tcol:tcol + 128].rearrange(
                                                 "(j p) t -> p j t", p=128)
                                src = mTb.ap().rearrange("p (j t) -> p j t", j=4)
                                return e.dma_start(out=dst[:, :, :], in_=src[:, :, :])
                            grp("sync", st_m)

                        if DEBUG and (b, qc) in getattr(build_nc, "_dbg_bq", ()):
                            def dbg_dump(e, b=b, qc=qc):
                                key = "bq_%d_%d" % (b, qc)
                                e.dma_start(out=dbg[key + "_c"][:, :], in_=couts[:, :]).then_inc(sem_dg, 16)
                                e.dma_start(out=dbg[key + "_f"][:, :], in_=fouts[:, :]).then_inc(sem_dg, 16)
                                e.dma_start(out=dbg[key + "_s"][:, :], in_=souts[:, :]).then_inc(sem_dg, 16)
                                return e.dma_start(out=dbg[key + "_sel"][:, :],
                                                   in_=selBIP[:, 0:1024])
                            grp("gpsimd", dbg_dump, dman=3)

        def coll(e):
            with allow_ici_all_to_all():
                return e.collective_compute(
                    "AllToAll", AL.bypass,
                    replica_groups=[list(range(NCORE))],
                    ins=[a2a_in_h.ap().opt()],
                    outs=[a2a_out_h.ap().opt()])
        coll_grp(coll)

        xTown = dram("xTown", (D, TPC))
        stD = contextlib.ExitStack()
        with stD:
            ED = stD.enter_context
            bigA = ED(nc.sbuf_tensor("bigA", [128, 32 * TPC], BF16))
            bigB = ED(nc.sbuf_tensor("bigB", [128, 32 * TPC], BF16))
            wstr = ED(nc.sbuf_tensor("wstr", [128, 32 * 128], BF16))
            xcb = ED(nc.sbuf_tensor("xcb", [128, TPC], F32))
            sqC = ED(nc.sbuf_tensor("sqC", [128, 512], BF16))
            stat = ED(nc.sbuf_tensor("stat", [1, 4 * TPC], F32))
            nmB = ED(nc.sbuf_tensor("nmB", [128, TPC], F32))
            rsB = ED(nc.sbuf_tensor("rsB", [128, TPC], F32))
            zrowC = ED(nc.sbuf_tensor("zrowC", [128, 512], F32))
            laTb = ED(nc.sbuf_tensor("laTb", [128, 32 * 128], BF16))
            lbTb = ED(nc.sbuf_tensor("lbTb", [128, D], BF16))
            bfbb = ED(nc.sbuf_tensor("bfbb", [128, 32], F32))
            coefb = ED(nc.sbuf_tensor("coefb", [128, 1], F32))
            hTb = ED(nc.sbuf_tensor("hTb", [128, TPC], BF16))
            youtb = ED(nc.sbuf_tensor("youtb", [128, TPC], F32))
            cPA = ED(nc.psum_tensor("cPA", [128, 512], F32))
            cSM0 = ED(nc.psum_tensor("cSM0", [1, 512], F32))
            cSM1 = ED(nc.psum_tensor("cSM1", [1, 512], F32))
            cSS0 = ED(nc.psum_tensor("cSS0", [1, 512], F32))
            cSS1 = ED(nc.psum_tensor("cSS1", [1, 512], F32))
            cBC = ED(nc.psum_tensor("cBC", [128, 512], F32))
            cSM = [cSM0, cSM1]
            cSS = [cSS0, cSS1]

            def ld_c(e):
                src = a2a_out.rearrange("(kt p) t -> p kt t", p=128)
                dst = bigA.ap().rearrange("p (kt t) -> p kt t", kt=32)
                e.dma_start(out=dst[:, :, :], in_=src[:, :, :]).then_inc(sem_ds, 16)
                return e.dma_start(out=gnat_dummy_never, in_=None)
            def ld_c(e):
                src = a2a_out.rearrange("(kt p) t -> p kt t", p=128)
                dst = bigA.ap().rearrange("p (kt t) -> p kt t", kt=32)
                return e.dma_start(out=dst[:, :, :], in_=src[:, :, :])
            grp("sync", ld_c)

            def ld_c2(e):
                src = laT.rearrange("(kt p) r -> p kt r", p=128)
                dst = laTb.ap().rearrange("p (kt r) -> p kt r", kt=32)
                e.dma_start(out=dst[:, :, :], in_=src[:, :, :]).then_inc(sem_dg, 16)
                e.dma_start(out=lbTb[:], in_=lbT[:, :]).then_inc(sem_dg, 16)
                e.dma_start(out=bfbb[:], in_=bfb_h[:, :]).then_inc(sem_dg, 16)
                return e.dma_start(out=coefb[:], in_=lw128[:, :])
            grp("gpsimd", ld_c2, dman=3)

            def mk_coef(e):
                return e.tensor_scalar_mul(coefb[:, :], coefb[:, :], 2.0)
            grp("vector", mk_coef)

            for phase in range(2):
                wsrc = woT if phase == 0 else wfT
                for ft in range(32):
                    def ld_wt(e, ft=ft, wsrc=wsrc):
                        src = wsrc.rearrange("(kt p) f -> p kt f", p=128)
                        dst = wstr.ap().rearrange("p (kt f) -> p kt f", kt=32)
                        return e.dma_start(out=dst[:, :, :],
                                           in_=src[:, :, ft * 128:(ft + 1) * 128])
                    grp("gpsimd", ld_wt)

                    if phase == 0:
                        def ld_xc(e, ft=ft):
                            return e.dma_start(out=xcb[:, :],
                                               in_=xTown[ft * 128:(ft + 1) * 128, :])
                        grp("sync", ld_xc)

                    for tc2 in range(TPC // 512):
                        def mm_o(e, ft=ft, tc2=tc2, phase=phase):
                            for kt in range(32):
                                last = e.matmul(
                                    cPA[:, :], wstr[:, kt * 128:(kt + 1) * 128],
                                    bigA[:, kt * TPC + tc2 * 512: kt * TPC + tc2 * 512 + 512],
                                    start=(kt == 0), stop=False,
                                    skip_group_check=True)
                            if phase == 1:
                                last = e.matmul(
                                    cPA[:, :], lbTb[:, ft * 128:(ft + 1) * 128],
                                    hTb[:, tc2 * 512:(tc2 + 1) * 512],
                                    start=False, stop=True, skip_group_check=True)
                            else:
                                last = e.matmul(
                                    cPA[0:1, 0:1], wstr[0:1, 0:1], bigA[0:1, 0:1],
                                    start=False, stop=True, skip_group_check=True) \
                                    if False else last
                            return last
                        def mm_o0(e, ft=ft, tc2=tc2):
                            for kt in range(32):
                                last = e.matmul(
                                    cPA[:, :], wstr[:, kt * 128:(kt + 1) * 128],
                                    bigA[:, kt * TPC + tc2 * 512: kt * TPC + tc2 * 512 + 512],
                                    start=(kt == 0), stop=(kt == 31),
                                    skip_group_check=True)
                            return last
                        grp("tensor", mm_o0 if phase == 0 else mm_o)

                        def ev_o(e, ft=ft, tc2=tc2, phase=phase):
                            dsttile = bigB[:, ft * TPC + tc2 * 512: ft * TPC + tc2 * 512 + 512]
                            if phase == 0:
                                e.tensor_add(dsttile, cPA[:, :],
                                             xcb[:, tc2 * 512:(tc2 + 1) * 512])
                            else:
                                e.tensor_copy(zrowC[:, :], cPA[:, :])
                                e.tensor_scalar_add(zrowC[:, :], zrowC[:, :],
                                                    bfbb[:, ft:ft + 1])
                                e.tensor_add(dsttile, zrowC[:, :],
                                             bigA[:, ft * TPC + tc2 * 512:
                                                  ft * TPC + tc2 * 512 + 512])
                            return e.tensor_mul(sqC[:, :], dsttile, dsttile)
                        grp("vector", ev_o)

                        def mm_st(e, ft=ft, tc2=tc2):
                            e.matmul(cSM[tc2][:, :], ones128b[:, :],
                                     bigB[:, ft * TPC + tc2 * 512: ft * TPC + tc2 * 512 + 512],
                                     start=(ft == 0), stop=(ft == 31),
                                     skip_group_check=True)
                            return e.matmul(cSS[tc2][:, :], ones128b[:, :], sqC[:, :],
                                            start=(ft == 0), stop=(ft == 31),
                                            skip_group_check=True)
                        grp("tensor", mm_st)

                def ev_st(e):
                    for tc2 in range(2):
                        e.tensor_copy(stat[:, tc2 * 512:(tc2 + 1) * 512], cSM[tc2][:, :])
                        e.tensor_copy(stat[:, TPC + tc2 * 512: TPC + (tc2 + 1) * 512],
                                      cSS[tc2][:, :])
                    e.tensor_scalar_mul(stat[:, 2 * TPC:3 * TPC], stat[:, 0:TPC],
                                        1.0 / D)
                    e.tensor_scalar_mul(stat[:, TPC:2 * TPC], stat[:, TPC:2 * TPC],
                                        1.0 / D)
                    e.tensor_mul(stat[:, 3 * TPC:4 * TPC], stat[:, 2 * TPC:3 * TPC],
                                 stat[:, 2 * TPC:3 * TPC])
                    e.tensor_sub(stat[:, TPC:2 * TPC], stat[:, TPC:2 * TPC],
                                 stat[:, 3 * TPC:4 * TPC])
                    return e.tensor_scalar_add(stat[:, TPC:2 * TPC],
                                               stat[:, TPC:2 * TPC], EPS)
                grp("vector", ev_st)

                def st_sqrt(e):
                    return e.activation(stat[:, 3 * TPC:4 * TPC], stat[:, TPC:2 * TPC],
                                        ACT.Sqrt, bias=0.0, scale=1.0)
                grp("scalar", st_sqrt)

                def st_fin(e):
                    e.reciprocal(stat[:, TPC:2 * TPC], stat[:, 3 * TPC:4 * TPC])
                    return e.tensor_scalar_mul(stat[:, 2 * TPC:3 * TPC],
                                               stat[:, 2 * TPC:3 * TPC], -1.0)
                grp("vector", st_fin)

                for which in range(2):
                    dstb = nmB if which == 0 else rsB
                    srcoff = 2 * TPC if which == 0 else TPC
                    for tc2 in range(2):
                        def bc_prep(e, srcoff=srcoff, tc2=tc2):
                            e.memset(zrowC[:], 0.0)
                            return e.tensor_copy(
                                zrowC[0:1, :],
                                stat[:, srcoff + tc2 * 512: srcoff + (tc2 + 1) * 512])
                        grp("vector", bc_prep)

                        def bc_mm(e):
                            return e.matmul(cBC[:, :], e1tf[:, :], zrowC[:, :],
                                            start=True, stop=True, skip_group_check=True)
                        grp("tensor", bc_mm)

                        def bc_ev(e, dstb=dstb, tc2=tc2):
                            return e.tensor_copy(dstb[:, tc2 * 512:(tc2 + 1) * 512],
                                                 cBC[:, :])
                        grp("vector", bc_ev)

                if phase == 0:
                    for ft in range(32):
                        def nrm(e, ft=ft):
                            sl = slice(ft * TPC, (ft + 1) * TPC)
                            e.tensor_add(bigA[:, sl], bigB[:, sl], nmB[:, :])
                            return e.tensor_mul(bigA[:, sl], bigA[:, sl], rsB[:, :])
                        grp("vector", nrm)

                    for tc2 in range(2):
                        def mm_h2(e, tc2=tc2):
                            for kt in range(32):
                                last = e.matmul(
                                    cPA[:, :], laTb[:, kt * 128:(kt + 1) * 128],
                                    bigA[:, kt * TPC + tc2 * 512: kt * TPC + tc2 * 512 + 512],
                                    start=(kt == 0), stop=(kt == 31),
                                    skip_group_check=True)
                            return last
                        grp("tensor", mm_h2)

                        def ev_h2(e, tc2=tc2):
                            e.tensor_copy(hTb[:, tc2 * 512:(tc2 + 1) * 512], cPA[:, :])
                            return e.tensor_scalar_mul(
                                hTb[:, tc2 * 512:(tc2 + 1) * 512],
                                hTb[:, tc2 * 512:(tc2 + 1) * 512], coefb[:, :])
                        grp("vector", ev_h2)
                else:
                    for ft in range(32):
                        def nrm2(e, ft=ft):
                            sl = slice(ft * TPC, (ft + 1) * TPC)
                            e.tensor_add(youtb[:, :], bigB[:, sl], nmB[:, :])
                            return e.tensor_mul(youtb[:, :], youtb[:, :], rsB[:, :])
                        grp("vector", nrm2)

                        def st_y(e, ft=ft):
                            return e.dma_start(out=y[ft * 128:(ft + 1) * 128, :],
                                               in_=youtb[:, :])
                        grp("sync", st_y)

            def fin(e):
                return e.nop()
            grp("sync", fin)

    return nc


def kernel(**inputs):
    shared = _host_prep_shared(inputs)
    consts_keys = ["kw1T", "vw1T", "kb1", "vb1", "kw2T", "vw2T", "kb2", "vb2z",
                   "woT", "wfT", "bfb", "laT", "lbT", "lw128", "identc", "ones128",
                   "e1t", "bst32", "bst8", "e32", "e8", "e8ones", "ebigt", "ownt",
                   "smaskt0", "smaskt1", "cosT", "sinT", "xT"]
    in_maps = []
    for c in range(NCORE):
        d = _host_prep_core(inputs, c)
        for k in consts_keys:
            d[k] = shared[k]
        d["xTown"] = np.ascontiguousarray(shared["xT"][:, c * TPC:(c + 1) * TPC])
        in_maps.append(d)
    nc = build_nc()
    res = run_bass_kernel_spmd(nc, in_maps, core_ids=list(range(NCORE)))
    out = np.concatenate(
        [np.ascontiguousarray(res.results[c]["yT"].T) for c in range(NCORE)], axis=0)
    return out.reshape(Bq, S, D)


# version 07
# speedup vs baseline: 1.1019x; 1.1019x over previous
import contextlib
import numpy as np
import ml_dtypes
import concourse.bass as bass
import concourse.mybir as mybir
from concourse.bass_utils import run_bass_kernel_spmd
from concourse.aot_env import allow_ici_all_to_all

F32 = mybir.dt.float32
BF16 = mybir.dt.bfloat16
AL = mybir.AluOpType
AX = mybir.AxisListType
ACT = mybir.ActivationFunctionType

NCORE = 8
H, KVH, DH, G = 32, 8, 128, 4
Bq, S, D = 2, 4096, 4096
CB = 16
WIN = 64
SCALE = float(DH) ** -0.5
TCH = 256
NT = Bq * S
TPC = NT // NCORE
EPS = 1e-5
BIG = 1e30
RSTR = 648

DEBUG = False
_DBG_SHAPES = {}


def _host_prep_core(inp, core):
    P = np.concatenate([np.arange(0, DH, 2), np.arange(1, DH, 2)])
    d = {}
    wqkv = inp["L0_attn_w_qkv"]
    qrows = np.concatenate(
        [wqkv[(4 * core + g) * DH + P] for g in range(G)], axis=0)
    krows = wqkv[H * DH + core * DH + P]
    vrows = wqkv[H * DH + KVH * DH + core * DH: H * DH + KVH * DH + (core + 1) * DH]
    d["wqkvT"] = np.ascontiguousarray(
        np.concatenate([qrows, krows, vrows], axis=0).T.astype(np.float32))
    wg = inp["L0_attn_w_gate"]
    grows = np.stack([wg[3 * (4 * core + g) + j] for g in range(G) for j in range(3)])
    d["wgT"] = np.ascontiguousarray(grows.T.astype(np.float32))
    bgv = inp["L0_attn_b_gate"]
    d["bg"] = np.ascontiguousarray(
        np.array([bgv[3 * (4 * core + g) + j] for g in range(G) for j in range(3)],
                 np.float32).reshape(12, 1))
    d["kposT"] = np.ascontiguousarray(inp["L0_attn_k_pos"][core][:, P].T.astype(np.float32))
    d["vposT"] = np.ascontiguousarray(inp["L0_attn_v_pos"][core].T.astype(np.float32))
    d["memk"] = np.ascontiguousarray(
        inp["L0_attn_mem_kv"][0, core, 0][P].reshape(128, 1).astype(np.float32))
    d["memv"] = np.ascontiguousarray(
        inp["L0_attn_mem_kv"][1, core, 0].reshape(1, 128).astype(np.float32))
    return d


def _host_prep_shared(inp):
    P = np.concatenate([np.arange(0, DH, 2), np.arange(1, DH, 2)])
    d = {}
    X = np.ascontiguousarray(inp["L0_attn_hidden_states"].reshape(NT, D))
    d["xT"] = np.ascontiguousarray(X.T)
    perm2048 = (np.arange(CB)[:, None] * DH + P[None, :]).reshape(-1)
    d["kw1T"] = np.ascontiguousarray(inp["L0_attn_k_w1"].T[perm2048].astype(np.float32))
    d["vw1T"] = np.ascontiguousarray(inp["L0_attn_v_w1"].T.astype(np.float32))
    d["kb1"] = np.ascontiguousarray(inp["L0_attn_k_b1"].reshape(2048, 1).astype(np.float32))
    d["vb1"] = np.ascontiguousarray(inp["L0_attn_v_b1"].reshape(2048, 1).astype(np.float32))
    d["kw2T"] = np.ascontiguousarray(inp["L0_attn_k_w2"].T[:, P].astype(np.float32))
    d["vw2T"] = np.ascontiguousarray(inp["L0_attn_v_w2"].T.astype(np.float32))
    d["kb2"] = np.ascontiguousarray(inp["L0_attn_k_b2"][P].reshape(128, 1).astype(np.float32))
    vb2z = np.zeros((128, 128), np.float32)
    vb2z[0, :] = inp["L0_attn_v_b2"]
    d["vb2z"] = vb2z
    d["woT"] = np.ascontiguousarray(inp["L0_attn_w_o"].T.astype(np.float32))
    d["wfT"] = np.ascontiguousarray(inp["L0_ffn_W"].T.astype(np.float32))
    d["bfb"] = np.ascontiguousarray(
        inp["L0_ffn_bias"].reshape(32, 128).T.astype(np.float32))
    d["laT"] = np.ascontiguousarray(
        inp["L0_ffn_lora_A"].reshape(128, D).T.astype(np.float32))
    d["lbT"] = np.ascontiguousarray(
        inp["L0_ffn_lora_B"].transpose(0, 2, 1).reshape(128, D).astype(np.float32))
    d["lw128"] = np.ascontiguousarray(
        np.repeat(np.asarray(inp["L0_ffn_lora_weights"]), 16).reshape(128, 1).astype(np.float32))
    d["identc"] = np.eye(128, dtype=np.float32)
    d["ones128"] = np.ones((128, 1), np.float32)
    e1t = np.zeros((128, 128), np.float32)
    e1t[0, :] = 1.0
    d["e1t"] = e1t
    xs = np.arange(896) - 256
    bst32 = np.zeros((32, 896), np.float32)
    for r in range(32):
        bst32[r, xs >= r + 1] = -BIG
    d["bst32"] = bst32
    bst8 = np.zeros((8, 896), np.float32)
    for r in range(8):
        bst8[r, xs >= r + 1] = -BIG
    d["bst8"] = bst8
    e32 = np.zeros((32, 512), np.float32)
    for q in range(512):
        e32[q // 16, q] = 1.0
    d["e32"] = e32
    e8 = np.zeros((8, 128), np.float32)
    for q in range(128):
        e8[q // 16, q] = 1.0
    d["e8"] = e8
    e8o = np.zeros((8, 1024), np.float32)
    for r in range(8):
        e8o[r, r * 128:(r + 1) * 128] = 1.0
    d["e8ones"] = e8o
    ebig = np.zeros((128, 16 * 128), np.float32)
    for j in range(16):
        for m in range(128):
            ebig[8 * j + m // 16, j * 128 + m] = BIG
    d["ebigt"] = ebig
    ownt = np.zeros((128, 128), np.float32)
    for k in range(128):
        for q in range(128):
            if k // 16 == q // 16 and k % 16 <= q % 16:
                ownt[k, q] = BIG
    d["ownt"] = ownt
    sm0 = np.full((128, 64), -BIG, np.float32)
    sm1 = np.full((128, 64), -BIG, np.float32)
    for k in range(128):
        for q in range(64):
            if k <= q:
                sm0[k, q] = 0.0
            if q <= k <= q + 64:
                sm1[k, q] = 0.0
    d["smaskt0"] = sm0
    d["smaskt1"] = sm1
    inv = (1.0 / 10000.0 ** (np.arange(0, DH, 2, dtype=np.float64) / DH))
    ang = inv[:, None] * np.arange(S, dtype=np.float64)[None, :]
    d["cosT"] = np.cos(ang).astype(np.float32)
    d["sinT"] = np.sin(ang).astype(np.float32)
    return d


def build_nc():
    nc = bass.Bass("TRN2", target_bir_lowering=False)

    def dram(name, shape, dt=F32, kind="ExternalInput"):
        return nc.dram_tensor(name, list(shape), dt, kind=kind).ap()

    xT = dram("xT", (D, NT))
    wqkvT = dram("wqkvT", (D, 768))
    wgT = dram("wgT", (D, 12))
    bg = dram("bg", (12, 1))
    kw1T = dram("kw1T", (2048, 2048))
    vw1T = dram("vw1T", (2048, 2048))
    kb1 = dram("kb1", (2048, 1))
    vb1 = dram("vb1", (2048, 1))
    kw2T = dram("kw2T", (2048, 128))
    vw2T = dram("vw2T", (2048, 128))
    kb2 = dram("kb2", (128, 1))
    vb2z = dram("vb2z", (128, 128))
    kposT = dram("kposT", (128, 16))
    vposT = dram("vposT", (128, 16))
    memk = dram("memk", (128, 1))
    memv = dram("memv", (1, 128))
    woT = dram("woT", (D, D))
    wfT = dram("wfT", (D, D))
    bfb_h = dram("bfb", (128, 32))
    laT = dram("laT", (D, 128))
    lbT = dram("lbT", (128, D))
    lw128 = dram("lw128", (128, 1))
    identc = dram("identc", (128, 128))
    ones128 = dram("ones128", (128, 1))
    e1t_h = dram("e1t", (128, 128))
    bst32_h = dram("bst32", (32, 896))
    bst8_h = dram("bst8", (8, 896))
    e32_h = dram("e32", (32, 512))
    e8_h = dram("e8", (8, 128))
    e8ones_h = dram("e8ones", (8, 1024))
    ebigt_h = dram("ebigt", (128, 2048))
    ownt_h = dram("ownt", (128, 128))
    sm0_h = dram("smaskt0", (128, 64))
    sm1_h = dram("smaskt1", (128, 64))
    cosT = dram("cosT", (64, S))
    sinT = dram("sinT", (64, S))

    y = dram("yT", (D, TPC), F32, "ExternalOutput")

    qsc = dram("qsc", (G, 128, NT), BF16, None) if False else \
        nc.dram_tensor("qsc", [G, 128, NT], BF16).ap()
    rqsc = nc.dram_tensor("rqsc", [G, 2, 64, NT], BF16).ap()
    gsc = nc.dram_tensor("gsc", [12, NT], BF16).ap()
    bfsc = nc.dram_tensor("bfsc", [G, NT], F32).ap()
    qnsc = nc.dram_tensor("qnsc", [G, NT], F32).ap()
    bcsc = nc.dram_tensor("bcsc", [G, NT], F32).ap()
    bisc = nc.dram_tensor("bisc", [1, NT], F32).ap()
    a2a_in_h = nc.dram_tensor("a2a_in", [D, TPC], BF16)
    a2a_out_h = nc.dram_tensor("a2a_out", [D, TPC], BF16)
    a2a_in = a2a_in_h.ap()
    a2a_out = a2a_out_h.ap()

    dbg = {}
    if DEBUG:
        for nm, shp in _DBG_SHAPES.items():
            dbg[nm] = dram("dbg_" + nm, shp, F32, "ExternalOutput")

    cnt = dict(ds=0, dg=0, ts=0, vs=0, ss=0, cs=0)

    st0 = contextlib.ExitStack()
    with st0:
        EC = st0.enter_context
        identb = EC(nc.sbuf_tensor("identb", [128, 128], BF16))
        ones128b = EC(nc.sbuf_tensor("ones128b", [128, 1], BF16))
        e1tb = EC(nc.sbuf_tensor("e1tb", [128, 128], BF16))
        e1tf = EC(nc.sbuf_tensor("e1tf", [128, 128], F32))
        mck2 = EC(nc.sbuf_tensor("mck2", [1, 1], F32))
        mckr = EC(nc.sbuf_tensor("mckr", [1, 1], F32))
        scal1 = EC(nc.sbuf_tensor("scal1", [1, 8], F32))
        sem_ds = EC(nc.semaphore("sem_ds"))
        sem_dg = EC(nc.semaphore("sem_dg"))
        sem_ts = EC(nc.semaphore("sem_ts"))
        sem_vs = EC(nc.semaphore("sem_vs"))
        sem_ss = EC(nc.semaphore("sem_ss"))
        sem_cs = EC(nc.semaphore("sem_cs"))
        block = EC(nc.Block())
        SEM = dict(ds=sem_ds, dg=sem_dg, ts=sem_ts, vs=sem_vs, ss=sem_ss, cs=sem_cs)
        OWNSEM = dict(sync="ds", gpsimd="dg", tensor="ts", vector="vs", scalar="ss")

        def vdr(e, inst):
            inst.then_inc(SEM["vs"], 1)
            cnt["vs"] += 1
            e.wait_ge(SEM["vs"], cnt["vs"])

        def grp(engname, fn, dman=0):
            waits = [(s, cnt[s]) for s in ("ds", "dg", "ts", "vs", "ss", "cs")
                     if cnt[s] > 0]
            own = OWNSEM[engname]
            n = 16 if engname in ("sync", "gpsimd") else 1

            def body(e):
                for s, vv in waits:
                    e.wait_ge(SEM[s], vv)
                last = fn(e)
                last.then_inc(SEM[own], n)
            getattr(block, engname)(body)
            cnt[own] += n + dman * 16

        def coll_grp(fn):
            waits = [(s, cnt[s]) for s in ("ds", "dg", "ts", "vs", "ss")
                     if cnt[s] > 0]

            def body(e):
                for s, vv in waits:
                    e.wait_ge(SEM[s], vv)
                fn(e).then_inc(SEM["cs"], 1)
            block.gpsimd(body)
            cnt["cs"] += 1

        st1 = contextlib.ExitStack()
        with st1:
            E1 = st1.enter_context
            kT0 = E1(nc.sbuf_tensor("kT0", [128, S], BF16))
            kT1 = E1(nc.sbuf_tensor("kT1", [128, S], BF16))
            rkT0 = E1(nc.sbuf_tensor("rkT0", [128, S], BF16))
            rkT1 = E1(nc.sbuf_tensor("rkT1", [128, S], BF16))
            vT0 = E1(nc.sbuf_tensor("vT0", [128, S], BF16))
            vT1 = E1(nc.sbuf_tensor("vT1", [128, S], BF16))
            vnat0 = E1(nc.sbuf_tensor("vnat0", [128, 32 * 129], BF16))
            vnat1 = E1(nc.sbuf_tensor("vnat1", [128, 32 * 129], BF16))
            vnatS0 = E1(nc.sbuf_tensor("vnatS0", [128, 32 * 129], BF16))
            vnatS1 = E1(nc.sbuf_tensor("vnatS1", [128, 32 * 129], BF16))
            ckT0 = E1(nc.sbuf_tensor("ckT0", [128, 385], BF16))
            ckT1 = E1(nc.sbuf_tensor("ckT1", [128, 385], BF16))
            cv0 = E1(nc.sbuf_tensor("cv0", [128, 3 * 129], BF16))
            cv1 = E1(nc.sbuf_tensor("cv1", [128, 3 * 129], BF16))
            kT = [kT0, kT1]
            rkT = [rkT0, rkT1]
            vT = [vT0, vT1]
            vnat = [vnat0, vnat1]
            vnatS = [vnatS0, vnatS1]
            ckT = [ckT0, ckT1]
            cvn = [cv0, cv1]

            def ld_consts(e):
                e.dma_start(out=identb[:], in_=identc[:, :]).then_inc(sem_dg, 16)
                e.dma_start(out=ones128b[:], in_=ones128[:, :]).then_inc(sem_dg, 16)
                e.dma_start(out=e1tb[:], in_=e1t_h[:, :]).then_inc(sem_dg, 16)
                return e.dma_start(out=e1tf[:], in_=e1t_h[:, :])
            grp("gpsimd", ld_consts, dman=3)

            def init_misc(e):
                e.memset(mck2[:], 0.0)
                for b in range(Bq):
                    e.memset(vnat[b][:], 1.0)
                    e.memset(vnatS[b][:], 1.0)
                    e.memset(ckT[b][:], 0.0)
                    e.memset(cvn[b][:], 0.0)
                return e.memset(scal1[:], 0.0)
            grp("vector", init_misc)

            stA = contextlib.ExitStack()
            with stA:
                EA = stA.enter_context
                wqvb = EA(nc.sbuf_tensor("wqvb", [128, 32 * 768], BF16))
                wgb = EA(nc.sbuf_tensor("wgb", [128, 32 * 12], BF16))
                bgb = EA(nc.sbuf_tensor("bgb", [12, 1], F32))
                cosb = EA(nc.sbuf_tensor("cosb", [64, TCH], BF16))
                sinb = EA(nc.sbuf_tensor("sinb", [64, TCH], BF16))
                xbuf = EA(nc.sbuf_tensor("xbuf", [128, 32 * TCH], BF16))
                qbuf = EA(nc.sbuf_tensor("qbuf", [128, G * TCH], BF16))
                hiA = EA(nc.sbuf_tensor("hiA", [64, 5 * TCH], BF16))
                rlo = EA(nc.sbuf_tensor("rlo", [64, 5 * TCH], BF16))
                rhi = EA(nc.sbuf_tensor("rhi", [64, 5 * TCH], BF16))
                rt1 = EA(nc.sbuf_tensor("rt1", [64, TCH], F32))
                rt2 = EA(nc.sbuf_tensor("rt2", [64, TCH], F32))
                sqb = EA(nc.sbuf_tensor("sqb", [128, TCH], BF16))
                selfb = EA(nc.sbuf_tensor("selfb", [128, TCH], BF16))
                selfrow = EA(nc.sbuf_tensor("selfrow", [1, TCH], F32))
                gbuf = EA(nc.sbuf_tensor("gbuf", [12, TCH], BF16))
                nrow = EA(nc.sbuf_tensor("nrow", [1, TCH], F32))
                n1 = EA(nc.sbuf_tensor("n1", [1, 1], F32))
                pq0 = EA(nc.psum_tensor("pq0", [128, TCH], F32))
                pq1 = EA(nc.psum_tensor("pq1", [128, TCH], F32))
                pq2 = EA(nc.psum_tensor("pq2", [128, TCH], F32))
                pq3 = EA(nc.psum_tensor("pq3", [128, TCH], F32))
                pk = EA(nc.psum_tensor("pk", [128, TCH], F32))
                pv = EA(nc.psum_tensor("pv", [128, TCH], F32))
                pgn = EA(nc.psum_tensor("pgn", [128, TCH], F32))
                ptb = EA(nc.psum_tensor("ptb", [128, 512], BF16))
                pq = [pq0, pq1, pq2, pq3]

                def ld_w(e):
                    src = wqkvT.rearrange("(kd p) f -> p kd f", p=128)
                    dst = wqvb.ap().rearrange("p (kd f) -> p kd f", kd=32)
                    e.dma_start(out=dst[:, :, :], in_=src[:, :, :]).then_inc(sem_dg, 16)
                    src = wgT.rearrange("(kd p) f -> p kd f", p=128)
                    dst = wgb.ap().rearrange("p (kd f) -> p kd f", kd=32)
                    e.dma_start(out=dst[:, :, :], in_=src[:, :, :]).then_inc(sem_dg, 16)
                    return e.dma_start(out=bgb[:], in_=bg[:, :])
                grp("gpsimd", ld_w, dman=2)

                for ci in range(NT // TCH):
                    b, tc = ci // (S // TCH), ci % (S // TCH)
                    t0 = tc * TCH
                    gt0 = b * S + t0

                    def ld_x(e, gt0=gt0, t0=t0):
                        src = xT.rearrange("(kd p) t -> p kd t", p=128)
                        dst = xbuf.ap().rearrange("p (kd t) -> p kd t", kd=32)
                        e.dma_start(out=dst[:, :, :],
                                    in_=src[:, :, gt0:gt0 + TCH]).then_inc(sem_dg, 16)
                        e.dma_start(out=cosb[:], in_=cosT[:, t0:t0 + TCH]).then_inc(sem_dg, 16)
                        return e.dma_start(out=sinb[:], in_=sinT[:, t0:t0 + TCH])
                    grp("gpsimd", ld_x, dman=2)

                    def mm_qkv(e):
                        for m in range(6):
                            dstp = (pq + [pk, pv])[m]
                            for kd in range(32):
                                e.matmul(dstp[:, :],
                                         wqvb[:, kd * 768 + m * 128: kd * 768 + (m + 1) * 128],
                                         xbuf[:, kd * TCH:(kd + 1) * TCH],
                                         start=(kd == 0), stop=(kd == 31),
                                         skip_group_check=True)
                        for kd in range(32):
                            last = e.matmul(pgn[0:12, :],
                                            wgb[:, kd * 12:(kd + 1) * 12],
                                            xbuf[:, kd * TCH:(kd + 1) * TCH],
                                            start=(kd == 0), stop=(kd == 31),
                                            skip_group_check=True)
                        return last
                    grp("tensor", mm_qkv)

                    def ev1(e, b=b, t0=t0):
                        for g in range(G):
                            e.tensor_copy(qbuf[:, g * TCH:(g + 1) * TCH], pq[g][:, :])
                        e.tensor_copy(kT[b][:, t0:t0 + TCH], pk[:, :])
                        return e.tensor_copy(vT[b][:, t0:t0 + TCH], pv[:, :])
                    grp("vector", ev1)

                    def ev_g(e):
                        return e.activation(gbuf[:, :], pgn[0:12, :], ACT.Sigmoid,
                                            bias=bgb[:, :], scale=1.0)
                    grp("scalar", ev_g)

                    def st_g(e, gt0=gt0):
                        return e.dma_start(out=gsc[:, gt0:gt0 + TCH], in_=gbuf[:, :])
                    grp("sync", st_g)

                    def dshift(e, b=b, t0=t0, gt0=gt0):
                        for g in range(G):
                            e.dma_start(out=hiA[:, g * TCH:(g + 1) * TCH],
                                        in_=qbuf[64:128, g * TCH:(g + 1) * TCH]
                                        ).then_inc(sem_ds, 16)
                            e.dma_start(out=qsc[g, :, gt0:gt0 + TCH],
                                        in_=qbuf[:, g * TCH:(g + 1) * TCH]
                                        ).then_inc(sem_ds, 16)
                        return e.dma_start(out=hiA[:, 4 * TCH:5 * TCH],
                                           in_=kT[b][64:128, t0:t0 + TCH])
                    grp("sync", dshift, dman=8)

                    def rope(e, b=b, t0=t0):
                        for u in range(5):
                            lo = (qbuf[0:64, u * TCH:(u + 1) * TCH] if u < 4
                                  else kT[b][0:64, t0:t0 + TCH])
                            hi = hiA[:, u * TCH:(u + 1) * TCH]
                            e.tensor_mul(rt1[:, :], lo, cosb[:, :])
                            e.tensor_mul(rt2[:, :], hi, sinb[:, :])
                            e.tensor_sub(rlo[:, u * TCH:(u + 1) * TCH], rt1[:, :], rt2[:, :])
                            e.tensor_mul(rt1[:, :], hi, cosb[:, :])
                            e.tensor_mul(rt2[:, :], lo, sinb[:, :])
                            e.tensor_add(rhi[:, u * TCH:(u + 1) * TCH], rt1[:, :], rt2[:, :])
                        return e.tensor_copy(rkT[b][0:64, t0:t0 + TCH],
                                             rlo[:, 4 * TCH:5 * TCH])
                    grp("vector", rope)

                    def st_rq(e, b=b, t0=t0, gt0=gt0):
                        for g in range(G):
                            e.dma_start(out=rqsc[g, 0, :, gt0:gt0 + TCH],
                                        in_=rlo[:, g * TCH:(g + 1) * TCH]).then_inc(sem_ds, 16)
                            e.dma_start(out=rqsc[g, 1, :, gt0:gt0 + TCH],
                                        in_=rhi[:, g * TCH:(g + 1) * TCH]).then_inc(sem_ds, 16)
                        return e.dma_start(out=rkT[b][64:128, t0:t0 + TCH],
                                           in_=rhi[:, 4 * TCH:5 * TCH])
                    grp("sync", st_rq, dman=8)

                    def tr_v(e, b=b, t0=t0):
                        for j in range(TCH // 128):
                            last = e.transpose(ptb[:, j * 128:(j + 1) * 128],
                                               vT[b][:, t0 + j * 128: t0 + (j + 1) * 128],
                                               identb[:, :])
                        return last
                    grp("tensor", tr_v)

                    def ev_v(e, b=b, t0=t0):
                        for j in range(TCH // 128):
                            kt2 = t0 // 128 + j
                            last = e.tensor_copy(vnat[b][:, kt2 * 129: kt2 * 129 + 128],
                                                 ptb[:, j * 128:(j + 1) * 128])
                        return last
                    grp("vector", ev_v)

            stB = contextlib.ExitStack()
            with stB:
                EB = stB.enter_context
                ckin = EB(nc.sbuf_tensor("ckin", [128, 16 * 512], BF16))
                w1b = EB(nc.sbuf_tensor("w1b", [128, 16 * 128], BF16))
                w2b = EB(nc.sbuf_tensor("w2b", [128, 16 * 128], BF16))
                b1b = EB(nc.sbuf_tensor("b1b", [128, 16], F32))
                b2b = EB(nc.sbuf_tensor("b2b", [128, 1], F32))
                vb2b = EB(nc.sbuf_tensor("vb2b", [128, 128], BF16))
                posb = EB(nc.sbuf_tensor("posb", [128, 16], F32))
                reluT = EB(nc.sbuf_tensor("reluT", [128, 16 * RSTR], BF16))
                sqc = EB(nc.sbuf_tensor("sqc", [128, 512], BF16))
                nrow2 = EB(nc.sbuf_tensor("nrow2", [1, 512], F32))
                qrow = EB(nc.sbuf_tensor("qrow", [1, S], F32))
                qrow2 = EB(nc.sbuf_tensor("qrow2", [1, S], F32))
                qacc = EB(nc.sbuf_tensor("qacc", [1, S], F32))
                n2 = EB(nc.sbuf_tensor("n2", [1, 1], F32))
                ph = EB(nc.psum_tensor("ph", [128, 512], F32))
                pck = EB(nc.psum_tensor("pck", [128, 256], F32))
                pcv = EB(nc.psum_tensor("pcv", [128, 128], F32))
                pnn = EB(nc.psum_tensor("pnn", [1, 512], F32))
                pbb = EB(nc.psum_tensor("pbb", [128, 1], F32))
                for kind in range(2):
                    w1ap = kw1T if kind == 0 else vw1T
                    w2ap = kw2T if kind == 0 else vw2T
                    b1ap = kb1 if kind == 0 else vb1
                    posap = kposT if kind == 0 else vposT
                    cache = kT if kind == 0 else vT

                    def ld_small(e, w2ap=w2ap, b1ap=b1ap, posap=posap, kind=kind):
                        src = w2ap.rearrange("(u p) f -> p u f", p=128)
                        dst = w2b.ap().rearrange("p (u f) -> p u f", u=16)
                        e.dma_start(out=dst[:, :, :], in_=src[:, :, :]).then_inc(sem_dg, 16)
                        src = b1ap.rearrange("(u p) one -> p u one", p=128)
                        e.dma_start(out=b1b.ap()[:, :], in_=src[:, :, 0],
                                    allow_slow_non_contiguous=True).then_inc(sem_dg, 16)
                        e.dma_start(out=posb[:, :], in_=posap[:, :]).then_inc(sem_dg, 16)
                        if kind == 0:
                            return e.dma_start(out=b2b[:, :], in_=kb2[:, :])
                        return e.dma_start(out=vb2b[:, :], in_=vb2z[:, :])
                    grp("gpsimd", ld_small, dman=3)

                    def mk_in(e, cache=cache):
                        for p in range(16):
                            for b in range(Bq):
                                src = cache[b].ap().rearrange("d (c p) -> d c p", p=16)
                                e.tensor_scalar_add(
                                    ckin[:, p * 512 + b * 256: p * 512 + (b + 1) * 256],
                                    src[:, :, p], posb[:, p:p + 1])
                        return e.memset(reluT[:], 0.0)
                    grp("vector", mk_in)

                    for u in range(16):
                        def ld_w1(e, u=u, w1ap=w1ap):
                            src = w1ap.rearrange("(p pp) f -> pp p f", pp=128)
                            dst = w1b.ap().rearrange("pp (p f) -> pp p f", p=16)
                            return e.dma_start(out=dst[:, :, :],
                                               in_=src[:, :, u * 128:(u + 1) * 128])
                        grp("gpsimd", ld_w1)

                        def mm_h(e):
                            for p in range(16):
                                last = e.matmul(ph[:, :], w1b[:, p * 128:(p + 1) * 128],
                                                ckin[:, p * 512:(p + 1) * 512],
                                                start=(p == 0), stop=(p == 15),
                                                skip_group_check=True)
                            return last
                        grp("tensor", mm_h)

                        def ev_h(e, u=u):
                            return e.activation(reluT[:, u * RSTR + 1: u * RSTR + 513],
                                                ph[:, :], ACT.Relu,
                                                bias=b1b[:, u:u + 1], scale=1.0)
                        grp("scalar", ev_h)

                    if kind == 0:
                        for b in range(Bq):
                            def mm_ck(e, b=b):
                                for u in range(16):
                                    last = e.matmul(
                                        pck[:, :], w2b[:, u * 128:(u + 1) * 128],
                                        reluT[:, u * RSTR + 1 + b * 256:
                                              u * RSTR + 1 + (b + 1) * 256],
                                        start=(u == 0), stop=(u == 15),
                                        skip_group_check=True)
                                return last
                            grp("tensor", mm_ck)

                            def ev_ck(e, b=b):
                                e.tensor_copy(ckT[b][:, 1:257], pck[:, :])
                                return e.tensor_scalar_add(ckT[b][:, 1:257],
                                                           ckT[b][:, 1:257], b2b[:, :])
                            grp("vector", ev_ck)

                        def ld_mem(e):
                            e.dma_start(out=ckT[0][:, 0:1], in_=memk[:, :]).then_inc(sem_dg, 16)
                            return e.dma_start(out=ckT[1][:, 0:1], in_=memk[:, :])
                        grp("gpsimd", ld_mem, dman=1)
                    else:
                        for b in range(Bq):
                            for ct in range(3):
                                def mm_cv(e, b=b, ct=ct):
                                    for u in range(16):
                                        e.matmul(
                                            pcv[:, :],
                                            reluT[:, u * RSTR + b * 256 + ct * 128:
                                                  u * RSTR + b * 256 + ct * 128 + 128],
                                            w2b[:, u * 128:(u + 1) * 128],
                                            start=(u == 0), stop=False,
                                            skip_group_check=True)
                                    return e.matmul(pcv[:, :], e1tb[:, :], vb2b[:, :],
                                                    start=False, stop=True,
                                                    skip_group_check=True)
                                grp("tensor", mm_cv)

                                def ev_cv(e, b=b, ct=ct):
                                    return e.tensor_copy(
                                        cvn[b][:, ct * 129: ct * 129 + 128], pcv[:, :])
                                grp("vector", ev_cv)

                        def ld_memv(e):
                            e.dma_start(out=cvn[0][0:1, 0:128], in_=memv[:, :]).then_inc(sem_dg, 16)
                            return e.dma_start(out=cvn[1][0:1, 0:128], in_=memv[:, :])
                        grp("gpsimd", ld_memv, dman=1)

                        def fix_cv(e):
                            v0 = cvn[0].ap().rearrange("p (c t) -> p c t", c=3)
                            e.memset(v0[:, :, 128:129], 1.0)
                            v1 = cvn[1].ap().rearrange("p (c t) -> p c t", c=3)
                            return e.memset(v1[:, :, 128:129], 1.0)
                        grp("vector", fix_cv)

                def mk_vs(e):
                    for b in range(Bq):
                        sv = vnat[b].ap().rearrange("p (c t) -> p c t", c=32)
                        dv = vnatS[b].ap().rearrange("p (c t) -> p c t", c=32)
                        e.dma_start(out=dv[0:64, :, :], in_=sv[64:128, :, :]).then_inc(sem_ds, 16)
                        if b == 1:
                            return e.dma_start(out=dv[64:128, 0:31, :], in_=sv[0:64, 1:32, :])
                        e.dma_start(out=dv[64:128, 0:31, :], in_=sv[0:64, 1:32, :]).then_inc(sem_ds, 16)
                grp("sync", mk_vs, dman=3)

            stC = contextlib.ExitStack()
            with stC:
                EE = stC.enter_context
                bst32b = EE(nc.sbuf_tensor("bst32b", [32, 896], BF16))
                bst8b = EE(nc.sbuf_tensor("bst8b", [8, 896], BF16))
                e32b = EE(nc.sbuf_tensor("e32b", [32, 512], BF16))
                e8b = EE(nc.sbuf_tensor("e8b", [8, 128], BF16))
                e8onesb = EE(nc.sbuf_tensor("e8onesb", [8, 1024], BF16))
                brows = EE(nc.sbuf_tensor("brows", [8, 512], BF16))
                bitile = EE(nc.sbuf_tensor("bitile", [128, 4], F32))
                mx4 = EE(nc.sbuf_tensor("mx4", [128, 1], F32))
                ebigtb = EE(nc.sbuf_tensor("ebigtb", [128, 2048], BF16))
                owntb = EE(nc.sbuf_tensor("owntb", [128, 128], BF16))
                sm0b = EE(nc.sbuf_tensor("sm0b", [128, 64], BF16))
                sm1b = EE(nc.sbuf_tensor("sm1b", [128, 64], BF16))
                qTb = EE(nc.sbuf_tensor("qTb", [128, G * 512], BF16))
                rqTb = EE(nc.sbuf_tensor("rqTb", [128, G * 512], BF16))
                qsT = EE(nc.sbuf_tensor("qsT", [128, 512], BF16))
                gch = EE(nc.sbuf_tensor("gch", [12, 512], BF16))
                gnat = EE(nc.sbuf_tensor("gnat", [128, 12], F32))
                pimp = EE(nc.sbuf_tensor("pimp", [128, 384], F32))
                mx8 = EE(nc.sbuf_tensor("mx8", [128, 8], F32))
                zap = EE(nc.sbuf_tensor("zap", [128, 384], F32))
                selw = EE(nc.sbuf_tensor("selw", [128, 384], F32))
                selb = EE(nc.sbuf_tensor("selb", [128, 384], BF16))
                zrow = EE(nc.sbuf_tensor("zrow", [128, 1], F32))
                thr = EE(nc.sbuf_tensor("thr", [128, 1], F32))
                selBIP = EE(nc.sbuf_tensor("selBIP", [128, 3 * 512], BF16))
                pcT = EE(nc.sbuf_tensor("pcT", [128, 3 * 512], BF16))
                pfT = EE(nc.sbuf_tensor("pfT", [128, 512], BF16))
                psT = EE(nc.sbuf_tensor("psT", [128, 512], BF16))
                recb = EE(nc.sbuf_tensor("recb", [128, 1], F32))
                couts = EE(nc.sbuf_tensor("couts", [128, G * 4 * 128], BF16))
                fouts = EE(nc.sbuf_tensor("fouts", [128, G * 4 * 128], BF16))
                souts = EE(nc.sbuf_tensor("souts", [128, G * 4 * 128], BF16))
                mbuf = EE(nc.sbuf_tensor("mbuf", [128, 512], BF16))
                mTb = EE(nc.sbuf_tensor("mTb", [128, 512], BF16))
                bPS = EE(nc.psum_tensor("bPS", [128, 512], F32))
                bPC = EE(nc.psum_tensor("bPC", [128, 512], F32))
                bPV0 = EE(nc.psum_tensor("bPV0", [128, 129], F32))
                bPV1 = EE(nc.psum_tensor("bPV1", [128, 129], F32))
                bPV2 = EE(nc.psum_tensor("bPV2", [128, 129], F32))
                bPV3 = EE(nc.psum_tensor("bPV3", [128, 129], F32))
                bPD = EE(nc.psum_tensor("bPD", [128, 129], F32))
                bPT = EE(nc.psum_tensor("bPT", [128, 512], BF16))
                bPV = [bPV0, bPV1, bPV2, bPV3]

                def ld_bconsts(e):
                    e.dma_start(out=bst32b[:], in_=bst32_h[:, :]).then_inc(sem_dg, 16)
                    e.dma_start(out=bst8b[:], in_=bst8_h[:, :]).then_inc(sem_dg, 16)
                    e.dma_start(out=e32b[:], in_=e32_h[:, :]).then_inc(sem_dg, 16)
                    e.dma_start(out=e8b[:], in_=e8_h[:, :]).then_inc(sem_dg, 16)
                    e.dma_start(out=e8onesb[:], in_=e8ones_h[:, :]).then_inc(sem_dg, 16)
                    e.dma_start(out=ebigtb[:], in_=ebigt_h[:, :]).then_inc(sem_dg, 16)
                    e.dma_start(out=owntb[:], in_=ownt_h[:, :]).then_inc(sem_dg, 16)
                    e.dma_start(out=sm0b[:], in_=sm0_h[:, :]).then_inc(sem_dg, 16)
                    return e.dma_start(out=sm1b[:], in_=sm1_h[:, :])
                grp("gpsimd", ld_bconsts, dman=8)

                for b in range(Bq):
                    for qc in range(8):
                        off = b * S + qc * 512
                        wp = 32 * qc + 33
                        nct = (wp + 127) // 128
                        wpad = nct * 128
                        base = 32 * qc
                        nkt = 4 * (qc + 1)

                        def ld_q(e, off=off):
                            for g in range(G):
                                e.dma_start(out=qTb[:, g * 512:(g + 1) * 512],
                                            in_=qsc[g, :, off:off + 512]).then_inc(sem_ds, 16)
                                e.dma_start(out=rqTb[0:64, g * 512:(g + 1) * 512],
                                            in_=rqsc[g, 0, :, off:off + 512]).then_inc(sem_ds, 16)
                                e.dma_start(out=rqTb[64:128, g * 512:(g + 1) * 512],
                                            in_=rqsc[g, 1, :, off:off + 512]).then_inc(sem_ds, 16)
                            return e.dma_start(out=gch[:, :], in_=gsc[:, off:off + 512])
                        grp("sync", ld_q, dman=12)

                        def mk_qs(e):
                            e.tensor_add(qsT[:, :], qTb[:, 0:512], qTb[:, 512:1024])
                            e.tensor_add(qsT[:, :], qsT[:, :], qTb[:, 1024:1536])
                            return e.tensor_add(qsT[:, :], qsT[:, :], qTb[:, 1536:2048])
                        grp("vector", mk_qs)

                        for it in range(4):
                            ibase = base + 8 * it

                            def mm_imp(e, it=it, ibase=ibase):
                                e.matmul(bPC[:, 0:384], e8b[:, :],
                                         bst8b[:, 257 - ibase: 257 - ibase + 384],
                                         start=True, stop=False, skip_group_check=True)
                                return e.matmul(bPC[:, 0:384],
                                                qsT[:, it * 128:(it + 1) * 128],
                                                ckT[b][:, 1:385],
                                                start=False, stop=True,
                                                skip_group_check=True)
                            grp("tensor", mm_imp)

                            def exp_imp(e, it=it):
                                return e.activation(pimp[:, :], bPC[:, 0:384],
                                                    ACT.Exp, bias=0.0,
                                                    scale=SCALE / 4.0)
                            grp("scalar", exp_imp)

                            def sel(e):
                                e.reduce_sum(zrow[:, :], pimp[:, :], axis=AX.X)
                                vdr(e, e.max(out=mx8[:, :], in_=pimp[:, :]))
                                e.tensor_scalar_mul(thr[:, :], zrow[:, :], 1e-10)
                                vdr(e, e.tensor_copy(mx4[:, :], mx8[:, 3:4]))
                                e.tensor_scalar(selw[:, :], pimp[:, :],
                                                mx4[:, :], None, AL.is_ge)
                                e.tensor_scalar(zap[:, :], pimp[:, :],
                                                thr[:, :], None, AL.is_gt)
                                e.tensor_mul(selw[:, :], selw[:, :], zap[:, :])
                                return e.tensor_copy(selb[:, :], selw[:, :])
                            grp("vector", sel)

                            def tr_sel(e):
                                for ct in range(3):
                                    last = e.transpose(bPT[:, ct * 128:(ct + 1) * 128],
                                                       selb[:, ct * 128:(ct + 1) * 128],
                                                       identb[:, :])
                                return last
                            grp("tensor", tr_sel)

                            def ev_sel(e, it=it):
                                for ct in range(3):
                                    e.tensor_copy(
                                        selBIP[:, ct * 512 + it * 128:
                                               ct * 512 + (it + 1) * 128],
                                        bPT[:, ct * 128:(ct + 1) * 128])
                                    last = e.tensor_scalar_add(
                                        selBIP[:, ct * 512 + it * 128:
                                               ct * 512 + (it + 1) * 128],
                                        selBIP[:, ct * 512 + it * 128:
                                               ct * 512 + (it + 1) * 128], -1.0)
                                return last
                            grp("vector", ev_sel)

                        for g in range(G):
                            qg = qTb[:, g * 512:(g + 1) * 512]
                            rqg = rqTb[:, g * 512:(g + 1) * 512]

                            for ct in range(nct):
                                def mm_c(e, ct=ct, qg=qg, g=g):
                                    col = 256 + 128 * ct - base
                                    e.matmul(bPC[:, :], bst32b[:, col:col + 128],
                                             e32b[:, :], start=True, stop=False,
                                             skip_group_check=True)
                                    return e.matmul(bPC[:, :],
                                                    ckT[b][:, ct * 128:(ct + 1) * 128],
                                                    qg, start=False, stop=True,
                                                    skip_group_check=True)
                                grp("tensor", mm_c)

                                def exp_c(e, ct=ct):
                                    return e.activation(pcT[:, ct * 512:(ct + 1) * 512],
                                                        bPC[:, :], ACT.Exp,
                                                        bias=0.0, scale=SCALE)
                                grp("scalar", exp_c)

                            for it in range(4):
                                def mm_cpv(e, it=it, nct=nct):
                                    for ct in range(nct):
                                        last = e.matmul(
                                            bPD[:, :],
                                            pcT[:, ct * 512 + it * 128:
                                                ct * 512 + (it + 1) * 128],
                                            cvn[b][:, ct * 129:(ct + 1) * 129],
                                            start=(ct == 0), stop=(ct == nct - 1),
                                            skip_group_check=True)
                                    return last
                                grp("tensor", mm_cpv)

                                def div_c(e, it=it, g=g):
                                    o = (g * 4 + it) * 128
                                    vdr(e, e.reciprocal(recb[:, :], bPD[:, 128:129]))
                                    e.tensor_copy(couts[:, o:o + 128], bPD[:, 0:128])
                                    return e.tensor_scalar_mul(
                                        couts[:, o:o + 128], couts[:, o:o + 128],
                                        recb[:, :])
                                grp("vector", div_c)

                            for kt in range(nkt):
                                def mm_f(e, kt=kt, rqg=rqg, qc=qc, g=g):
                                    ct = kt // 16
                                    j = kt % 16
                                    e.matmul(bPS[:, :], ebigtb[:, j * 128:(j + 1) * 128],
                                             selBIP[:, ct * 512:(ct + 1) * 512],
                                             start=True, stop=False,
                                             skip_group_check=True)
                                    it2 = kt - 4 * qc
                                    if 0 <= it2 < 4:
                                        e.matmul(bPS[:, it2 * 128:(it2 + 1) * 128],
                                                 identb[:, :], owntb[:, :],
                                                 start=False, stop=False,
                                                 skip_group_check=True)
                                    return e.matmul(bPS[:, :],
                                                    rkT[b][:, kt * 128:(kt + 1) * 128],
                                                    rqg, start=False, stop=True,
                                                    skip_group_check=True)
                                grp("tensor", mm_f)

                                def exp_f(e):
                                    return e.activation(pfT[:, :], bPS[:, :], ACT.Exp,
                                                        bias=0.0, scale=SCALE)
                                grp("scalar", exp_f)

                                def mm_fpv(e, kt=kt, qc=qc):
                                    for it2 in range(max(0, kt - 4 * qc), 4):
                                        last = e.matmul(
                                            bPV[it2][:, :],
                                            pfT[:, it2 * 128:(it2 + 1) * 128],
                                            vnat[b][:, kt * 129: kt * 129 + 129],
                                            start=(kt == 0),
                                            stop=(kt == 4 * qc + it2),
                                            skip_group_check=True)
                                    return last
                                grp("tensor", mm_fpv)

                            for it in range(4):
                                def div_f(e, it=it, g=g):
                                    o = (g * 4 + it) * 128
                                    vdr(e, e.reciprocal(recb[:, :], bPV[it][:, 128:129]))
                                    e.tensor_copy(fouts[:, o:o + 128], bPV[it][:, 0:128])
                                    return e.tensor_scalar_mul(
                                        fouts[:, o:o + 128], fouts[:, o:o + 128],
                                        recb[:, :])
                                grp("vector", div_f)

                            def mm_s(e, rqg=rqg, qc=qc, g=g):
                                for wl in range(8):
                                    w = 8 * qc + wl
                                    ocol = wl * 64
                                    if w == 0:
                                        e.matmul(bPS[:, ocol:ocol + 64], identb[:, :],
                                                 sm0b[:, :], start=True, stop=False,
                                                 skip_group_check=True)
                                        last = e.matmul(bPS[:, ocol:ocol + 64],
                                                        rkT[b][:, 0:128],
                                                        rqg[:, 0:64],
                                                        start=False, stop=True,
                                                        skip_group_check=True)
                                    else:
                                        e.matmul(bPS[:, ocol:ocol + 64], identb[:, :],
                                                 sm1b[:, :], start=True, stop=False,
                                                 skip_group_check=True)
                                        last = e.matmul(bPS[:, ocol:ocol + 64],
                                                        rkT[b][:, 64 * (w - 1): 64 * (w + 1)],
                                                        rqg[:, wl * 64: wl * 64 + 64],
                                                        start=False, stop=True,
                                                        skip_group_check=True)
                                return last
                            grp("tensor", mm_s)

                            def exp_s(e):
                                return e.activation(psT[:, :], bPS[:, :], ACT.Exp,
                                                    bias=0.0, scale=SCALE)
                            grp("scalar", exp_s)

                            def mm_spv(e, qc=qc):
                                for wl in range(8):
                                    w = 8 * qc + wl
                                    it2 = wl // 2
                                    half = wl % 2
                                    if w == 0:
                                        rv = vnat[b][:, 0:129]
                                    elif w % 2 == 1:
                                        kt2 = (w - 1) // 2
                                        rv = vnat[b][:, kt2 * 129: kt2 * 129 + 129]
                                    else:
                                        kt2 = (w - 2) // 2
                                        rv = vnatS[b][:, kt2 * 129: kt2 * 129 + 129]
                                    last = e.matmul(
                                        bPV[it2][64 * half:64 * half + 64, :],
                                        psT[:, wl * 64: wl * 64 + 64], rv,
                                        start=True, stop=True,
                                        tile_position=(0, 64 * half),
                                        skip_group_check=True)
                                return last
                            grp("tensor", mm_spv)

                            for it in range(4):
                                def div_s(e, it=it, g=g):
                                    o = (g * 4 + it) * 128
                                    vdr(e, e.reciprocal(recb[:, :], bPV[it][:, 128:129]))
                                    e.tensor_copy(souts[:, o:o + 128], bPV[it][:, 0:128])
                                    return e.tensor_scalar_mul(
                                        souts[:, o:o + 128], souts[:, o:o + 128],
                                        recb[:, :])
                                grp("vector", div_s)

                        for it in range(4):
                            def tr_g(e, it=it):
                                return e.transpose(bPT[:, 0:12],
                                                   gch[:, it * 128: it * 128 + 128],
                                                   identb[0:12, 0:12])
                            grp("tensor", tr_g)

                            def ev_g2(e):
                                return e.tensor_copy(gnat[:, 0:12], bPT[:, 0:12])
                            grp("vector", ev_g2)

                            def mrg(e, it=it):
                                for g in range(G):
                                    o = (g * 4 + it) * 128
                                    e.tensor_scalar_mul(mbuf[:, g * 128:(g + 1) * 128],
                                                        couts[:, o:o + 128],
                                                        gnat[:, 3 * g: 3 * g + 1])
                                    e.scalar_tensor_tensor(
                                        mbuf[:, g * 128:(g + 1) * 128],
                                        fouts[:, o:o + 128],
                                        gnat[:, 3 * g + 1: 3 * g + 2],
                                        mbuf[:, g * 128:(g + 1) * 128], AL.mult, AL.add)
                                    last = e.scalar_tensor_tensor(
                                        mbuf[:, g * 128:(g + 1) * 128],
                                        souts[:, o:o + 128],
                                        gnat[:, 3 * g + 2: 3 * g + 3],
                                        mbuf[:, g * 128:(g + 1) * 128], AL.mult, AL.add)
                                return last
                            grp("vector", mrg)

                            def tr_m(e):
                                for j in range(4):
                                    last = e.transpose(bPT[:, j * 128:(j + 1) * 128],
                                                       mbuf[:, j * 128:(j + 1) * 128],
                                                       identb[:, :])
                                return last
                            grp("tensor", tr_m)

                            def ev_m(e):
                                return e.tensor_copy(mTb[:, :], bPT[:, :])
                            grp("vector", ev_m)

                            def st_m(e, b=b, qc=qc, it=it):
                                tglob = b * S + qc * 512 + it * 128
                                owner = tglob // TPC
                                tcol = tglob % TPC
                                dst = a2a_in[owner * 512:(owner + 1) * 512,
                                             tcol:tcol + 128].rearrange(
                                                 "(j p) t -> p j t", p=128)
                                src = mTb.ap().rearrange("p (j t) -> p j t", j=4)
                                return e.dma_start(out=dst[:, :, :], in_=src[:, :, :])
                            grp("sync", st_m)

                        if DEBUG and (b, qc) in getattr(build_nc, "_dbg_bq", ()):
                            def dbg_dump(e, b=b, qc=qc):
                                key = "bq_%d_%d" % (b, qc)
                                e.dma_start(out=dbg[key + "_c"][:, :], in_=couts[:, :]).then_inc(sem_dg, 16)
                                e.dma_start(out=dbg[key + "_f"][:, :], in_=fouts[:, :]).then_inc(sem_dg, 16)
                                e.dma_start(out=dbg[key + "_s"][:, :], in_=souts[:, :]).then_inc(sem_dg, 16)
                                return e.dma_start(out=dbg[key + "_sel"][:, :],
                                                   in_=selBIP[:, 0:1024])
                            grp("gpsimd", dbg_dump, dman=3)

        def coll(e):
            with allow_ici_all_to_all():
                return e.collective_compute(
                    "AllToAll", AL.bypass,
                    replica_groups=[list(range(NCORE))],
                    ins=[a2a_in_h.ap().opt()],
                    outs=[a2a_out_h.ap().opt()])
        coll_grp(coll)

        xTown = dram("xTown", (D, TPC))
        stD = contextlib.ExitStack()
        with stD:
            ED = stD.enter_context
            bigA = ED(nc.sbuf_tensor("bigA", [128, 32 * TPC], BF16))
            bigB = ED(nc.sbuf_tensor("bigB", [128, 32 * TPC], BF16))
            wstr = ED(nc.sbuf_tensor("wstr", [128, 32 * 128], BF16))
            xcb = ED(nc.sbuf_tensor("xcb", [128, TPC], F32))
            sqC = ED(nc.sbuf_tensor("sqC", [128, 512], BF16))
            stat = ED(nc.sbuf_tensor("stat", [1, 4 * TPC], F32))
            nmB = ED(nc.sbuf_tensor("nmB", [128, TPC], F32))
            rsB = ED(nc.sbuf_tensor("rsB", [128, TPC], F32))
            zrowC = ED(nc.sbuf_tensor("zrowC", [128, 512], F32))
            laTb = ED(nc.sbuf_tensor("laTb", [128, 32 * 128], BF16))
            lbTb = ED(nc.sbuf_tensor("lbTb", [128, D], BF16))
            bfbb = ED(nc.sbuf_tensor("bfbb", [128, 32], F32))
            coefb = ED(nc.sbuf_tensor("coefb", [128, 1], F32))
            hTb = ED(nc.sbuf_tensor("hTb", [128, TPC], BF16))
            youtb = ED(nc.sbuf_tensor("youtb", [128, TPC], F32))
            cPA = ED(nc.psum_tensor("cPA", [128, 512], F32))
            cSM0 = ED(nc.psum_tensor("cSM0", [1, 512], F32))
            cSM1 = ED(nc.psum_tensor("cSM1", [1, 512], F32))
            cSS0 = ED(nc.psum_tensor("cSS0", [1, 512], F32))
            cSS1 = ED(nc.psum_tensor("cSS1", [1, 512], F32))
            cBC = ED(nc.psum_tensor("cBC", [128, 512], F32))
            cSM = [cSM0, cSM1]
            cSS = [cSS0, cSS1]

            def ld_c(e):
                src = a2a_out.rearrange("(kt p) t -> p kt t", p=128)
                dst = bigA.ap().rearrange("p (kt t) -> p kt t", kt=32)
                e.dma_start(out=dst[:, :, :], in_=src[:, :, :]).then_inc(sem_ds, 16)
                return e.dma_start(out=gnat_dummy_never, in_=None)
            def ld_c(e):
                src = a2a_out.rearrange("(kt p) t -> p kt t", p=128)
                dst = bigA.ap().rearrange("p (kt t) -> p kt t", kt=32)
                return e.dma_start(out=dst[:, :, :], in_=src[:, :, :])
            grp("sync", ld_c)

            def ld_c2(e):
                src = laT.rearrange("(kt p) r -> p kt r", p=128)
                dst = laTb.ap().rearrange("p (kt r) -> p kt r", kt=32)
                e.dma_start(out=dst[:, :, :], in_=src[:, :, :]).then_inc(sem_dg, 16)
                e.dma_start(out=lbTb[:], in_=lbT[:, :]).then_inc(sem_dg, 16)
                e.dma_start(out=bfbb[:], in_=bfb_h[:, :]).then_inc(sem_dg, 16)
                return e.dma_start(out=coefb[:], in_=lw128[:, :])
            grp("gpsimd", ld_c2, dman=3)

            def mk_coef(e):
                return e.tensor_scalar_mul(coefb[:, :], coefb[:, :], 2.0)
            grp("vector", mk_coef)

            for phase in range(2):
                wsrc = woT if phase == 0 else wfT
                for ft in range(32):
                    def ld_wt(e, ft=ft, wsrc=wsrc):
                        src = wsrc.rearrange("(kt p) f -> p kt f", p=128)
                        dst = wstr.ap().rearrange("p (kt f) -> p kt f", kt=32)
                        return e.dma_start(out=dst[:, :, :],
                                           in_=src[:, :, ft * 128:(ft + 1) * 128])
                    grp("gpsimd", ld_wt)

                    if phase == 0:
                        def ld_xc(e, ft=ft):
                            return e.dma_start(out=xcb[:, :],
                                               in_=xTown[ft * 128:(ft + 1) * 128, :])
                        grp("sync", ld_xc)

                    for tc2 in range(TPC // 512):
                        def mm_o(e, ft=ft, tc2=tc2, phase=phase):
                            for kt in range(32):
                                last = e.matmul(
                                    cPA[:, :], wstr[:, kt * 128:(kt + 1) * 128],
                                    bigA[:, kt * TPC + tc2 * 512: kt * TPC + tc2 * 512 + 512],
                                    start=(kt == 0), stop=False,
                                    skip_group_check=True)
                            if phase == 1:
                                last = e.matmul(
                                    cPA[:, :], lbTb[:, ft * 128:(ft + 1) * 128],
                                    hTb[:, tc2 * 512:(tc2 + 1) * 512],
                                    start=False, stop=True, skip_group_check=True)
                            else:
                                last = e.matmul(
                                    cPA[0:1, 0:1], wstr[0:1, 0:1], bigA[0:1, 0:1],
                                    start=False, stop=True, skip_group_check=True) \
                                    if False else last
                            return last
                        def mm_o0(e, ft=ft, tc2=tc2):
                            for kt in range(32):
                                last = e.matmul(
                                    cPA[:, :], wstr[:, kt * 128:(kt + 1) * 128],
                                    bigA[:, kt * TPC + tc2 * 512: kt * TPC + tc2 * 512 + 512],
                                    start=(kt == 0), stop=(kt == 31),
                                    skip_group_check=True)
                            return last
                        grp("tensor", mm_o0 if phase == 0 else mm_o)

                        def ev_o(e, ft=ft, tc2=tc2, phase=phase):
                            dsttile = bigB[:, ft * TPC + tc2 * 512: ft * TPC + tc2 * 512 + 512]
                            if phase == 0:
                                e.tensor_add(dsttile, cPA[:, :],
                                             xcb[:, tc2 * 512:(tc2 + 1) * 512])
                            else:
                                e.tensor_copy(zrowC[:, :], cPA[:, :])
                                e.tensor_scalar_add(zrowC[:, :], zrowC[:, :],
                                                    bfbb[:, ft:ft + 1])
                                e.tensor_add(dsttile, zrowC[:, :],
                                             bigA[:, ft * TPC + tc2 * 512:
                                                  ft * TPC + tc2 * 512 + 512])
                            return e.tensor_mul(sqC[:, :], dsttile, dsttile)
                        grp("vector", ev_o)

                        def mm_st(e, ft=ft, tc2=tc2):
                            e.matmul(cSM[tc2][:, :], ones128b[:, :],
                                     bigB[:, ft * TPC + tc2 * 512: ft * TPC + tc2 * 512 + 512],
                                     start=(ft == 0), stop=(ft == 31),
                                     skip_group_check=True)
                            return e.matmul(cSS[tc2][:, :], ones128b[:, :], sqC[:, :],
                                            start=(ft == 0), stop=(ft == 31),
                                            skip_group_check=True)
                        grp("tensor", mm_st)

                def ev_st(e):
                    for tc2 in range(2):
                        e.tensor_copy(stat[:, tc2 * 512:(tc2 + 1) * 512], cSM[tc2][:, :])
                        e.tensor_copy(stat[:, TPC + tc2 * 512: TPC + (tc2 + 1) * 512],
                                      cSS[tc2][:, :])
                    e.tensor_scalar_mul(stat[:, 2 * TPC:3 * TPC], stat[:, 0:TPC],
                                        1.0 / D)
                    e.tensor_scalar_mul(stat[:, TPC:2 * TPC], stat[:, TPC:2 * TPC],
                                        1.0 / D)
                    e.tensor_mul(stat[:, 3 * TPC:4 * TPC], stat[:, 2 * TPC:3 * TPC],
                                 stat[:, 2 * TPC:3 * TPC])
                    e.tensor_sub(stat[:, TPC:2 * TPC], stat[:, TPC:2 * TPC],
                                 stat[:, 3 * TPC:4 * TPC])
                    return e.tensor_scalar_add(stat[:, TPC:2 * TPC],
                                               stat[:, TPC:2 * TPC], EPS)
                grp("vector", ev_st)

                def st_sqrt(e):
                    return e.activation(stat[:, 3 * TPC:4 * TPC], stat[:, TPC:2 * TPC],
                                        ACT.Sqrt, bias=0.0, scale=1.0)
                grp("scalar", st_sqrt)

                def st_fin(e):
                    e.reciprocal(stat[:, TPC:2 * TPC], stat[:, 3 * TPC:4 * TPC])
                    return e.tensor_scalar_mul(stat[:, 2 * TPC:3 * TPC],
                                               stat[:, 2 * TPC:3 * TPC], -1.0)
                grp("vector", st_fin)

                for which in range(2):
                    dstb = nmB if which == 0 else rsB
                    srcoff = 2 * TPC if which == 0 else TPC
                    for tc2 in range(2):
                        def bc_prep(e, srcoff=srcoff, tc2=tc2):
                            e.memset(zrowC[:], 0.0)
                            return e.tensor_copy(
                                zrowC[0:1, :],
                                stat[:, srcoff + tc2 * 512: srcoff + (tc2 + 1) * 512])
                        grp("vector", bc_prep)

                        def bc_mm(e):
                            return e.matmul(cBC[:, :], e1tf[:, :], zrowC[:, :],
                                            start=True, stop=True, skip_group_check=True)
                        grp("tensor", bc_mm)

                        def bc_ev(e, dstb=dstb, tc2=tc2):
                            return e.tensor_copy(dstb[:, tc2 * 512:(tc2 + 1) * 512],
                                                 cBC[:, :])
                        grp("vector", bc_ev)

                if phase == 0:
                    for ft in range(32):
                        def nrm(e, ft=ft):
                            sl = slice(ft * TPC, (ft + 1) * TPC)
                            e.tensor_add(bigA[:, sl], bigB[:, sl], nmB[:, :])
                            return e.tensor_mul(bigA[:, sl], bigA[:, sl], rsB[:, :])
                        grp("vector", nrm)

                    for tc2 in range(2):
                        def mm_h2(e, tc2=tc2):
                            for kt in range(32):
                                last = e.matmul(
                                    cPA[:, :], laTb[:, kt * 128:(kt + 1) * 128],
                                    bigA[:, kt * TPC + tc2 * 512: kt * TPC + tc2 * 512 + 512],
                                    start=(kt == 0), stop=(kt == 31),
                                    skip_group_check=True)
                            return last
                        grp("tensor", mm_h2)

                        def ev_h2(e, tc2=tc2):
                            e.tensor_copy(hTb[:, tc2 * 512:(tc2 + 1) * 512], cPA[:, :])
                            return e.tensor_scalar_mul(
                                hTb[:, tc2 * 512:(tc2 + 1) * 512],
                                hTb[:, tc2 * 512:(tc2 + 1) * 512], coefb[:, :])
                        grp("vector", ev_h2)
                else:
                    for ft in range(32):
                        def nrm2(e, ft=ft):
                            sl = slice(ft * TPC, (ft + 1) * TPC)
                            e.tensor_add(youtb[:, :], bigB[:, sl], nmB[:, :])
                            return e.tensor_mul(youtb[:, :], youtb[:, :], rsB[:, :])
                        grp("vector", nrm2)

                        def st_y(e, ft=ft):
                            return e.dma_start(out=y[ft * 128:(ft + 1) * 128, :],
                                               in_=youtb[:, :])
                        grp("sync", st_y)

            def fin(e):
                return e.nop()
            grp("sync", fin)

    return nc


def kernel(**inputs):
    shared = _host_prep_shared(inputs)
    consts_keys = ["kw1T", "vw1T", "kb1", "vb1", "kw2T", "vw2T", "kb2", "vb2z",
                   "woT", "wfT", "bfb", "laT", "lbT", "lw128", "identc", "ones128",
                   "e1t", "bst32", "bst8", "e32", "e8", "e8ones", "ebigt", "ownt",
                   "smaskt0", "smaskt1", "cosT", "sinT", "xT"]
    in_maps = []
    for c in range(NCORE):
        d = _host_prep_core(inputs, c)
        for k in consts_keys:
            d[k] = shared[k]
        d["xTown"] = np.ascontiguousarray(shared["xT"][:, c * TPC:(c + 1) * TPC])
        in_maps.append(d)
    nc = build_nc()
    res = run_bass_kernel_spmd(nc, in_maps, core_ids=list(range(NCORE)))
    out = np.concatenate(
        [np.ascontiguousarray(res.results[c]["yT"].T) for c in range(NCORE)], axis=0)
    return out.reshape(Bq, S, D)


# version 10
# speedup vs baseline: 2.0360x; 1.8477x over previous
import contextlib
import numpy as np
import ml_dtypes
import concourse.bass as bass
import concourse.mybir as mybir
from concourse.bass_utils import run_bass_kernel_spmd
from concourse.aot_env import allow_ici_all_to_all

F32 = mybir.dt.float32
BF16 = mybir.dt.bfloat16
AL = mybir.AluOpType
AX = mybir.AxisListType
ACT = mybir.ActivationFunctionType

NCORE = 8
H, KVH, DH, G = 32, 8, 128, 4
Bq, S, D = 2, 4096, 4096
CB = 16
WIN = 64
SCALE = float(DH) ** -0.5
TCH = 256
NT = Bq * S
TPC = NT // NCORE
EPS = 1e-5
BIG = 1e30
RSTR = 648

DEBUG = False
FORCE_BARRIER = False
A2A_OUT = False
_DBG_SHAPES = {}


def _host_prep_core(inp, core):
    P = np.concatenate([np.arange(0, DH, 2), np.arange(1, DH, 2)])
    d = {}
    wqkv = inp["L0_attn_w_qkv"]
    qrows = np.concatenate(
        [wqkv[(4 * core + g) * DH + P] for g in range(G)], axis=0)
    krows = wqkv[H * DH + core * DH + P]
    vrows = wqkv[H * DH + KVH * DH + core * DH: H * DH + KVH * DH + (core + 1) * DH]
    d["wqkvT"] = np.ascontiguousarray(
        np.concatenate([qrows, krows, vrows], axis=0).T.astype(np.float32))
    wg = inp["L0_attn_w_gate"]
    grows = np.stack([wg[3 * (4 * core + g) + j] for g in range(G) for j in range(3)])
    d["wgT"] = np.ascontiguousarray(grows.T.astype(np.float32))
    bgv = inp["L0_attn_b_gate"]
    d["bg"] = np.ascontiguousarray(
        np.array([bgv[3 * (4 * core + g) + j] for g in range(G) for j in range(3)],
                 np.float32).reshape(12, 1))
    d["kposT"] = np.ascontiguousarray(inp["L0_attn_k_pos"][core][:, P].T.astype(np.float32))
    d["vposT"] = np.ascontiguousarray(inp["L0_attn_v_pos"][core].T.astype(np.float32))
    d["memk"] = np.ascontiguousarray(
        inp["L0_attn_mem_kv"][0, core, 0][P].reshape(128, 1).astype(np.float32))
    d["memv"] = np.ascontiguousarray(
        inp["L0_attn_mem_kv"][1, core, 0].reshape(1, 128).astype(np.float32))
    return d


def _host_prep_shared(inp):
    P = np.concatenate([np.arange(0, DH, 2), np.arange(1, DH, 2)])
    d = {}
    X = np.ascontiguousarray(inp["L0_attn_hidden_states"].reshape(NT, D))
    d["xT"] = np.ascontiguousarray(X.T)
    perm2048 = (np.arange(CB)[:, None] * DH + P[None, :]).reshape(-1)
    d["kw1T"] = np.ascontiguousarray(inp["L0_attn_k_w1"].T[perm2048].astype(np.float32))
    d["vw1T"] = np.ascontiguousarray(inp["L0_attn_v_w1"].T.astype(np.float32))
    d["kb1"] = np.ascontiguousarray(inp["L0_attn_k_b1"].reshape(2048, 1).astype(np.float32))
    d["vb1"] = np.ascontiguousarray(inp["L0_attn_v_b1"].reshape(2048, 1).astype(np.float32))
    d["kw2T"] = np.ascontiguousarray(inp["L0_attn_k_w2"].T[:, P].astype(np.float32))
    d["vw2T"] = np.ascontiguousarray(inp["L0_attn_v_w2"].T.astype(np.float32))
    d["kb2"] = np.ascontiguousarray(inp["L0_attn_k_b2"][P].reshape(128, 1).astype(np.float32))
    vb2z = np.zeros((128, 128), np.float32)
    vb2z[0, :] = inp["L0_attn_v_b2"]
    d["vb2z"] = vb2z
    d["woT"] = np.ascontiguousarray(inp["L0_attn_w_o"].T.astype(np.float32))
    d["wfT"] = np.ascontiguousarray(inp["L0_ffn_W"].T.astype(np.float32))
    d["bfb"] = np.ascontiguousarray(
        inp["L0_ffn_bias"].reshape(32, 128).T.astype(np.float32))
    d["laT"] = np.ascontiguousarray(
        inp["L0_ffn_lora_A"].reshape(128, D).T.astype(np.float32))
    d["lbT"] = np.ascontiguousarray(
        inp["L0_ffn_lora_B"].transpose(0, 2, 1).reshape(128, D).astype(np.float32))
    d["lw128"] = np.ascontiguousarray(
        np.repeat(np.asarray(inp["L0_ffn_lora_weights"]), 16).reshape(128, 1).astype(np.float32))
    d["identc"] = np.eye(128, dtype=np.float32)
    d["ones128"] = np.ones((128, 1), np.float32)
    e1t = np.zeros((128, 128), np.float32)
    e1t[0, :] = 1.0
    d["e1t"] = e1t
    xs = np.arange(896) - 256
    bst32 = np.zeros((32, 896), np.float32)
    for r in range(32):
        bst32[r, xs >= r + 1] = -BIG
    d["bst32"] = bst32
    bst8 = np.zeros((8, 896), np.float32)
    for r in range(8):
        bst8[r, xs >= r + 1] = -BIG
    d["bst8"] = bst8
    e32 = np.zeros((32, 512), np.float32)
    for q in range(512):
        e32[q // 16, q] = 1.0
    d["e32"] = e32
    e8 = np.zeros((8, 128), np.float32)
    for q in range(128):
        e8[q // 16, q] = 1.0
    d["e8"] = e8
    e8o = np.zeros((8, 1024), np.float32)
    for r in range(8):
        e8o[r, r * 128:(r + 1) * 128] = 1.0
    d["e8ones"] = e8o
    ebig = np.zeros((128, 16 * 128), np.float32)
    for j in range(16):
        for m in range(128):
            ebig[8 * j + m // 16, j * 128 + m] = BIG
    d["ebigt"] = ebig
    ownt = np.zeros((128, 128), np.float32)
    for k in range(128):
        for q in range(128):
            if k // 16 == q // 16 and k % 16 <= q % 16:
                ownt[k, q] = BIG
    d["ownt"] = ownt
    sm0 = np.full((128, 64), -BIG, np.float32)
    sm1 = np.full((128, 64), -BIG, np.float32)
    for k in range(128):
        for q in range(64):
            if k <= q:
                sm0[k, q] = 0.0
            if q <= k <= q + 64:
                sm1[k, q] = 0.0
    d["smaskt0"] = sm0
    d["smaskt1"] = sm1
    inv = (1.0 / 10000.0 ** (np.arange(0, DH, 2, dtype=np.float64) / DH))
    ang = inv[:, None] * np.arange(S, dtype=np.float64)[None, :]
    d["cosT"] = np.cos(ang).astype(np.float32)
    d["sinT"] = np.sin(ang).astype(np.float32)
    return d


def build_nc():
    nc = bass.Bass("TRN2", target_bir_lowering=False)

    def dram(name, shape, dt=F32, kind="ExternalInput"):
        return nc.dram_tensor(name, list(shape), dt, kind=kind).ap()

    xT = dram("xT", (D, NT))
    wqkvT = dram("wqkvT", (D, 768))
    wgT = dram("wgT", (D, 12))
    bg = dram("bg", (12, 1))
    kw1T = dram("kw1T", (2048, 2048))
    vw1T = dram("vw1T", (2048, 2048))
    kb1 = dram("kb1", (2048, 1))
    vb1 = dram("vb1", (2048, 1))
    kw2T = dram("kw2T", (2048, 128))
    vw2T = dram("vw2T", (2048, 128))
    kb2 = dram("kb2", (128, 1))
    vb2z = dram("vb2z", (128, 128))
    kposT = dram("kposT", (128, 16))
    vposT = dram("vposT", (128, 16))
    memk = dram("memk", (128, 1))
    memv = dram("memv", (1, 128))
    woT = dram("woT", (D, D))
    wfT = dram("wfT", (D, D))
    bfb_h = dram("bfb", (128, 32))
    laT = dram("laT", (D, 128))
    lbT = dram("lbT", (128, D))
    lw128 = dram("lw128", (128, 1))
    identc = dram("identc", (128, 128))
    ones128 = dram("ones128", (128, 1))
    e1t_h = dram("e1t", (128, 128))
    bst32_h = dram("bst32", (32, 896))
    bst8_h = dram("bst8", (8, 896))
    e32_h = dram("e32", (32, 512))
    e8_h = dram("e8", (8, 128))
    e8ones_h = dram("e8ones", (8, 1024))
    ebigt_h = dram("ebigt", (128, 2048))
    ownt_h = dram("ownt", (128, 128))
    sm0_h = dram("smaskt0", (128, 64))
    sm1_h = dram("smaskt1", (128, 64))
    cosT = dram("cosT", (64, S))
    sinT = dram("sinT", (64, S))

    y = dram("yT", (D, TPC), F32, "ExternalOutput")

    qsc = dram("qsc", (G, 128, NT), BF16, None) if False else \
        nc.dram_tensor("qsc", [G, 128, NT], BF16).ap()
    rqsc = nc.dram_tensor("rqsc", [G, 2, 64, NT], BF16).ap()
    gsc = nc.dram_tensor("gsc", [12, NT], BF16).ap()
    bfsc = nc.dram_tensor("bfsc", [G, NT], F32).ap()
    qnsc = nc.dram_tensor("qnsc", [G, NT], F32).ap()
    bcsc = nc.dram_tensor("bcsc", [G, NT], F32).ap()
    bisc = nc.dram_tensor("bisc", [1, NT], F32).ap()
    a2a_in_h = nc.dram_tensor("a2a_in", [D, TPC], BF16)
    a2a_out_h = nc.dram_tensor("a2a_out", [D, TPC], BF16)
    a2a_in = a2a_in_h.ap()
    a2a_out = a2a_out_h.ap()

    dbg = {}
    if DEBUG:
        for nm, shp in _DBG_SHAPES.items():
            dbg[nm] = dram("dbg_" + nm, shp, F32, "ExternalOutput")

    cnt = dict(ds=0, dg=0, ts=0, vs=0, ss=0, cs=0)

    st0 = contextlib.ExitStack()
    with st0:
        EC = st0.enter_context
        identb = EC(nc.sbuf_tensor("identb", [128, 128], BF16))
        ones128b = EC(nc.sbuf_tensor("ones128b", [128, 1], BF16))
        e1tb = EC(nc.sbuf_tensor("e1tb", [128, 128], BF16))
        e1tf = EC(nc.sbuf_tensor("e1tf", [128, 128], F32))
        mck2 = EC(nc.sbuf_tensor("mck2", [1, 1], F32))
        mckr = EC(nc.sbuf_tensor("mckr", [1, 1], F32))
        scal1 = EC(nc.sbuf_tensor("scal1", [1, 8], F32))
        sem_ds = EC(nc.semaphore("sem_ds"))
        sem_dg = EC(nc.semaphore("sem_dg"))
        sem_ts = EC(nc.semaphore("sem_ts"))
        sem_vs = EC(nc.semaphore("sem_vs"))
        sem_ss = EC(nc.semaphore("sem_ss"))
        sem_cs = EC(nc.semaphore("sem_cs"))
        block = EC(nc.Block())
        SEM = dict(ds=sem_ds, dg=sem_dg, ts=sem_ts, vs=sem_vs, ss=sem_ss, cs=sem_cs)
        OWNSEM = dict(sync="ds", gpsimd="dg", tensor="ts", vector="vs", scalar="ss")

        def vdr(e, inst):
            inst.then_inc(SEM["vs"], 1)
            cnt["vs"] += 1
            e.wait_ge(SEM["vs"], cnt["vs"])

        def grp(engname, fn, dman=0, after=None):
            own = OWNSEM[engname]
            n = 16 if engname in ("sync", "gpsimd") else 1
            if FORCE_BARRIER:
                after = None
            if after is None:
                waits = [(s, cnt[s]) for s in ("ds", "dg", "ts", "vs", "ss", "cs")
                         if cnt[s] > 0]
            else:
                bys = {}
                for t in after:
                    if t is None:
                        continue
                    s, vv = t
                    if vv > bys.get(s, -1):
                        bys[s] = vv
                waits = [(s, vv) for s, vv in sorted(bys.items()) if vv > 0]

            def body(e):
                for s, vv in waits:
                    e.wait_ge(SEM[s], vv)
                last = fn(e)
                last.then_inc(SEM[own], n)
            getattr(block, engname)(body)
            cnt[own] += n + dman * 16
            return (own, cnt[own])

        def coll_grp(fn):
            waits = [(s, cnt[s]) for s in ("ds", "dg", "ts", "vs", "ss")
                     if cnt[s] > 0]

            def body(e):
                for s, vv in waits:
                    e.wait_ge(SEM[s], vv)
                fn(e).then_inc(SEM["cs"], 1)
            block.gpsimd(body)
            cnt["cs"] += 1

        st1 = contextlib.ExitStack()
        with st1:
            E1 = st1.enter_context
            kT0 = E1(nc.sbuf_tensor("kT0", [128, S], BF16))
            kT1 = E1(nc.sbuf_tensor("kT1", [128, S], BF16))
            rkT0 = E1(nc.sbuf_tensor("rkT0", [128, S], BF16))
            rkT1 = E1(nc.sbuf_tensor("rkT1", [128, S], BF16))
            vT0 = E1(nc.sbuf_tensor("vT0", [128, S], BF16))
            vT1 = E1(nc.sbuf_tensor("vT1", [128, S], BF16))
            vnat0 = E1(nc.sbuf_tensor("vnat0", [128, 32 * 129], BF16))
            vnat1 = E1(nc.sbuf_tensor("vnat1", [128, 32 * 129], BF16))
            vnatS0 = E1(nc.sbuf_tensor("vnatS0", [128, 32 * 129], BF16))
            vnatS1 = E1(nc.sbuf_tensor("vnatS1", [128, 32 * 129], BF16))
            ckT0 = E1(nc.sbuf_tensor("ckT0", [128, 385], BF16))
            ckT1 = E1(nc.sbuf_tensor("ckT1", [128, 385], BF16))
            cv0 = E1(nc.sbuf_tensor("cv0", [128, 3 * 129], BF16))
            cv1 = E1(nc.sbuf_tensor("cv1", [128, 3 * 129], BF16))
            kT = [kT0, kT1]
            rkT = [rkT0, rkT1]
            vT = [vT0, vT1]
            vnat = [vnat0, vnat1]
            vnatS = [vnatS0, vnatS1]
            ckT = [ckT0, ckT1]
            cvn = [cv0, cv1]

            def ld_consts(e):
                e.dma_start(out=identb[:], in_=identc[:, :]).then_inc(sem_dg, 16)
                e.dma_start(out=ones128b[:], in_=ones128[:, :]).then_inc(sem_dg, 16)
                e.dma_start(out=e1tb[:], in_=e1t_h[:, :]).then_inc(sem_dg, 16)
                return e.dma_start(out=e1tf[:], in_=e1t_h[:, :])
            grp("gpsimd", ld_consts, dman=3)

            def init_misc(e):
                e.memset(mck2[:], 0.0)
                for b in range(Bq):
                    e.memset(vnat[b][:], 1.0)
                    e.memset(vnatS[b][:], 1.0)
                    e.memset(ckT[b][:], 0.0)
                    e.memset(cvn[b][:], 0.0)
                return e.memset(scal1[:], 0.0)
            grp("vector", init_misc)

            stA = contextlib.ExitStack()
            with stA:
                EA = stA.enter_context
                wqvb = EA(nc.sbuf_tensor("wqvb", [128, 32 * 768], BF16))
                wgb = EA(nc.sbuf_tensor("wgb", [128, 32 * 12], BF16))
                bgb = EA(nc.sbuf_tensor("bgb", [12, 1], F32))
                cosb = [EA(nc.sbuf_tensor("cosb%d" % p, [64, TCH], BF16)) for p in range(2)]
                sinb = [EA(nc.sbuf_tensor("sinb%d" % p, [64, TCH], BF16)) for p in range(2)]
                xbuf = [EA(nc.sbuf_tensor("xbuf%d" % p, [128, 32 * TCH], BF16)) for p in range(2)]
                qbuf = [EA(nc.sbuf_tensor("qbuf%d" % p, [128, G * TCH], BF16)) for p in range(2)]
                hiA = [EA(nc.sbuf_tensor("hiA%d" % p, [64, 5 * TCH], BF16)) for p in range(2)]
                rlo = [EA(nc.sbuf_tensor("rlo%d" % p, [64, 5 * TCH], BF16)) for p in range(2)]
                rhi = [EA(nc.sbuf_tensor("rhi%d" % p, [64, 5 * TCH], BF16)) for p in range(2)]
                rt1 = EA(nc.sbuf_tensor("rt1", [64, TCH], F32))
                rt2 = EA(nc.sbuf_tensor("rt2", [64, TCH], F32))
                gbuf = [EA(nc.sbuf_tensor("gbuf%d" % p, [12, TCH], BF16)) for p in range(2)]
                pq01 = [EA(nc.psum_tensor("pq01_%d" % p, [128, 512], F32)) for p in range(2)]
                pq23 = [EA(nc.psum_tensor("pq23_%d" % p, [128, 512], F32)) for p in range(2)]
                pkv = [EA(nc.psum_tensor("pkv%d" % p, [128, 512], F32)) for p in range(2)]
                pmx = EA(nc.psum_tensor("pmx", [128, TCH], F32))
                ptb = EA(nc.psum_tensor("ptb", [128, TCH], BF16))

                def ld_w(e):
                    src = wqkvT.rearrange("(kd p) f -> p kd f", p=128)
                    dst = wqvb.ap().rearrange("p (kd f) -> p kd f", kd=32)
                    e.dma_start(out=dst[:, :, :], in_=src[:, :, :]).then_inc(sem_dg, 16)
                    src = wgT.rearrange("(kd p) f -> p kd f", p=128)
                    dst = wgb.ap().rearrange("p (kd f) -> p kd f", kd=32)
                    e.dma_start(out=dst[:, :, :], in_=src[:, :, :]).then_inc(sem_dg, 16)
                    return e.dma_start(out=bgb[:], in_=bg[:, :])
                grp("gpsimd", ld_w, dman=2)

                NC1 = NT // TCH
                t_ldg = {}; t_lds = {}; t_mm = {}; t_ev1 = {}; t_evg = {}
                t_dsh = {}; t_rp = {}; t_srq = {}; t_trv = {}; t_evv = {}
                t_stg = {}

                def emit_trv(ci):
                    b, tc = ci // (S // TCH), ci % (S // TCH)
                    t0 = tc * TCH
                    p = ci % 2

                    def tr_v(e, b=b, t0=t0, p=p):
                        for j in range(TCH // 128):
                            last = e.transpose(
                                ptb[:, j * 128:(j + 1) * 128],
                                vT[b][:, t0 + j * 128: t0 + (j + 1) * 128],
                                identb[:, :])
                        return last
                    t_trv[ci] = grp("tensor", tr_v,
                                    after=[t_ev1[ci], t_evv.get(ci - 1)])

                    def ev_v(e, b=b, t0=t0, p=p):
                        for j in range(TCH // 128):
                            kt2 = t0 // 128 + j
                            last = e.tensor_copy(
                                vnat[b][:, kt2 * 129: kt2 * 129 + 128],
                                ptb[:, j * 128:(j + 1) * 128])
                        return last
                    t_evv[ci] = grp("vector", ev_v, after=[t_trv[ci]])

                for ci in range(NC1):
                    b, tc = ci // (S // TCH), ci % (S // TCH)
                    t0 = tc * TCH
                    gt0 = b * S + t0
                    p = ci % 2

                    def ld_x_g(e, gt0=gt0, t0=t0, p=p):
                        src = xT.rearrange("(kd p) t -> p kd t", p=128)
                        dst = xbuf[p].ap().rearrange("p (kd t) -> p kd t", kd=32)
                        e.dma_start(out=dst[:, :, :],
                                    in_=src[:, :, gt0:gt0 + TCH]).then_inc(sem_dg, 16)
                        e.dma_start(out=cosb[p][:], in_=cosT[:, t0:t0 + TCH]).then_inc(sem_dg, 16)
                        return e.dma_start(out=sinb[p][:], in_=sinT[:, t0:t0 + TCH])
                    t_ldg[ci] = grp("gpsimd", ld_x_g, dman=2,
                                    after=[t_mm.get(ci - 2), t_rp.get(ci - 2)])

                    def mm_qkv(e, p=p):
                        tgt = [(pq01[p], 0), (pq01[p], 256), (pq23[p], 0),
                               (pq23[p], 256), (pkv[p], 0), (pkv[p], 256)]
                        for m in range(6):
                            dstp, c0 = tgt[m]
                            for kd in range(32):
                                e.matmul(dstp[:, c0:c0 + TCH],
                                         wqvb[:, kd * 768 + m * 128: kd * 768 + (m + 1) * 128],
                                         xbuf[p][:, kd * TCH:(kd + 1) * TCH],
                                         start=(kd == 0), stop=(kd == 31),
                                         skip_group_check=True)
                        for kd in range(32):
                            last = e.matmul(pmx[0:12, 0:TCH],
                                            wgb[:, kd * 12:(kd + 1) * 12],
                                            xbuf[p][:, kd * TCH:(kd + 1) * TCH],
                                            start=(kd == 0), stop=(kd == 31),
                                            skip_group_check=True)
                        return last
                    t_mm[ci] = grp("tensor", mm_qkv,
                                   after=[t_ldg[ci], t_ev1.get(ci - 2),
                                          t_evg.get(ci - 1)])

                    def ev1(e, b=b, t0=t0, p=p):
                        for g in range(G):
                            src = [pq01[p], pq01[p], pq23[p], pq23[p]][g]
                            c0 = [0, 256, 0, 256][g]
                            e.tensor_copy(qbuf[p][:, g * TCH:(g + 1) * TCH],
                                          src[:, c0:c0 + TCH])
                        e.tensor_copy(kT[b][:, t0:t0 + TCH], pkv[p][:, 0:TCH])
                        return e.tensor_copy(vT[b][:, t0:t0 + TCH], pkv[p][:, 256:256 + TCH])
                    t_ev1[ci] = grp("vector", ev1,
                                    after=[t_mm[ci], t_dsh.get(ci - 2)])

                    def ev_g(e, gt0=gt0, p=p):
                        return e.activation(gbuf[p][:, :], pmx[0:12, 0:TCH], ACT.Sigmoid,
                                            bias=bgb[:, :], scale=1.0)
                    t_evg[ci] = grp("scalar", ev_g,
                                    after=[t_mm[ci], t_stg.get(ci - 2)])

                    def st_g(e, gt0=gt0, p=p):
                        return e.dma_start(out=gsc[:, gt0:gt0 + TCH],
                                           in_=gbuf[p][:, :])
                    t_stg[ci] = grp("sync", st_g, after=[t_evg[ci]])

                    def dshift(e, b=b, t0=t0, gt0=gt0, p=p):
                        for g in range(G):
                            e.dma_start(out=hiA[p][:, g * TCH:(g + 1) * TCH],
                                        in_=qbuf[p][64:128, g * TCH:(g + 1) * TCH]
                                        ).then_inc(sem_ds, 16)
                            e.dma_start(out=qsc[g, :, gt0:gt0 + TCH],
                                        in_=qbuf[p][:, g * TCH:(g + 1) * TCH]
                                        ).then_inc(sem_ds, 16)
                        return e.dma_start(out=hiA[p][:, 4 * TCH:5 * TCH],
                                           in_=kT[b][64:128, t0:t0 + TCH])
                    t_dsh[ci] = grp("sync", dshift, dman=8,
                                    after=[t_ev1[ci], t_rp.get(ci - 2)])

                    def rope(e, b=b, t0=t0, p=p):
                        for u in range(5):
                            lo = (qbuf[p][0:64, u * TCH:(u + 1) * TCH] if u < 4
                                  else kT[b][0:64, t0:t0 + TCH])
                            hi = hiA[p][:, u * TCH:(u + 1) * TCH]
                            e.tensor_mul(rt1[:, :], lo, cosb[p][:, :])
                            e.tensor_mul(rt2[:, :], hi, sinb[p][:, :])
                            e.tensor_sub(rlo[p][:, u * TCH:(u + 1) * TCH], rt1[:, :], rt2[:, :])
                            e.tensor_mul(rt1[:, :], hi, cosb[p][:, :])
                            e.tensor_mul(rt2[:, :], lo, sinb[p][:, :])
                            e.tensor_add(rhi[p][:, u * TCH:(u + 1) * TCH], rt1[:, :], rt2[:, :])
                        return e.tensor_copy(rkT[b][0:64, t0:t0 + TCH],
                                             rlo[p][:, 4 * TCH:5 * TCH])
                    t_rp[ci] = grp("vector", rope,
                                   after=[t_dsh[ci], t_ldg[ci], t_srq.get(ci - 2)])

                    def st_rq(e, b=b, t0=t0, gt0=gt0, p=p):
                        for g in range(G):
                            e.dma_start(out=rqsc[g, 0, :, gt0:gt0 + TCH],
                                        in_=rlo[p][:, g * TCH:(g + 1) * TCH]).then_inc(sem_ds, 16)
                            e.dma_start(out=rqsc[g, 1, :, gt0:gt0 + TCH],
                                        in_=rhi[p][:, g * TCH:(g + 1) * TCH]).then_inc(sem_ds, 16)
                        return e.dma_start(out=rkT[b][64:128, t0:t0 + TCH],
                                           in_=rhi[p][:, 4 * TCH:5 * TCH])
                    t_srq[ci] = grp("sync", st_rq, dman=8, after=[t_rp[ci]])

                    if ci >= 1:
                        emit_trv(ci - 1)
                emit_trv(NC1 - 1)

            stB = contextlib.ExitStack()
            with stB:
                EB = stB.enter_context
                ckin = EB(nc.sbuf_tensor("ckin", [128, 16 * 512], BF16))
                w1b = EB(nc.sbuf_tensor("w1b", [128, 16 * 128], BF16))
                w2b = EB(nc.sbuf_tensor("w2b", [128, 16 * 128], BF16))
                b1b = EB(nc.sbuf_tensor("b1b", [128, 16], F32))
                b2b = EB(nc.sbuf_tensor("b2b", [128, 1], F32))
                vb2b = EB(nc.sbuf_tensor("vb2b", [128, 128], BF16))
                posb = EB(nc.sbuf_tensor("posb", [128, 16], F32))
                reluT = EB(nc.sbuf_tensor("reluT", [128, 16 * RSTR], BF16))
                sqc = EB(nc.sbuf_tensor("sqc", [128, 512], BF16))
                nrow2 = EB(nc.sbuf_tensor("nrow2", [1, 512], F32))
                qrow = EB(nc.sbuf_tensor("qrow", [1, S], F32))
                qrow2 = EB(nc.sbuf_tensor("qrow2", [1, S], F32))
                qacc = EB(nc.sbuf_tensor("qacc", [1, S], F32))
                n2 = EB(nc.sbuf_tensor("n2", [1, 1], F32))
                ph = EB(nc.psum_tensor("ph", [128, 512], F32))
                pck = EB(nc.psum_tensor("pck", [128, 256], F32))
                pcv = EB(nc.psum_tensor("pcv", [128, 128], F32))
                pnn = EB(nc.psum_tensor("pnn", [1, 512], F32))
                pbb = EB(nc.psum_tensor("pbb", [128, 1], F32))
                for kind in range(2):
                    w1ap = kw1T if kind == 0 else vw1T
                    w2ap = kw2T if kind == 0 else vw2T
                    b1ap = kb1 if kind == 0 else vb1
                    posap = kposT if kind == 0 else vposT
                    cache = kT if kind == 0 else vT

                    def ld_small(e, w2ap=w2ap, b1ap=b1ap, posap=posap, kind=kind):
                        src = w2ap.rearrange("(u p) f -> p u f", p=128)
                        dst = w2b.ap().rearrange("p (u f) -> p u f", u=16)
                        e.dma_start(out=dst[:, :, :], in_=src[:, :, :]).then_inc(sem_dg, 16)
                        src = b1ap.rearrange("(u p) one -> p u one", p=128)
                        e.dma_start(out=b1b.ap()[:, :], in_=src[:, :, 0],
                                    allow_slow_non_contiguous=True).then_inc(sem_dg, 16)
                        e.dma_start(out=posb[:, :], in_=posap[:, :]).then_inc(sem_dg, 16)
                        if kind == 0:
                            return e.dma_start(out=b2b[:, :], in_=kb2[:, :])
                        return e.dma_start(out=vb2b[:, :], in_=vb2z[:, :])
                    grp("gpsimd", ld_small, dman=3)

                    def mk_in(e, cache=cache):
                        for p in range(16):
                            for b in range(Bq):
                                src = cache[b].ap().rearrange("d (c p) -> d c p", p=16)
                                e.tensor_scalar_add(
                                    ckin[:, p * 512 + b * 256: p * 512 + (b + 1) * 256],
                                    src[:, :, p], posb[:, p:p + 1])
                        return e.memset(reluT[:], 0.0)
                    grp("vector", mk_in)

                    for u in range(16):
                        def ld_w1(e, u=u, w1ap=w1ap):
                            src = w1ap.rearrange("(p pp) f -> pp p f", pp=128)
                            dst = w1b.ap().rearrange("pp (p f) -> pp p f", p=16)
                            return e.dma_start(out=dst[:, :, :],
                                               in_=src[:, :, u * 128:(u + 1) * 128])
                        grp("gpsimd", ld_w1)

                        def mm_h(e):
                            for p in range(16):
                                last = e.matmul(ph[:, :], w1b[:, p * 128:(p + 1) * 128],
                                                ckin[:, p * 512:(p + 1) * 512],
                                                start=(p == 0), stop=(p == 15),
                                                skip_group_check=True)
                            return last
                        grp("tensor", mm_h)

                        def ev_h(e, u=u):
                            return e.activation(reluT[:, u * RSTR + 1: u * RSTR + 513],
                                                ph[:, :], ACT.Relu,
                                                bias=b1b[:, u:u + 1], scale=1.0)
                        grp("scalar", ev_h)

                    if kind == 0:
                        for b in range(Bq):
                            def mm_ck(e, b=b):
                                for u in range(16):
                                    last = e.matmul(
                                        pck[:, :], w2b[:, u * 128:(u + 1) * 128],
                                        reluT[:, u * RSTR + 1 + b * 256:
                                              u * RSTR + 1 + (b + 1) * 256],
                                        start=(u == 0), stop=(u == 15),
                                        skip_group_check=True)
                                return last
                            grp("tensor", mm_ck)

                            def ev_ck(e, b=b):
                                e.tensor_copy(ckT[b][:, 1:257], pck[:, :])
                                return e.tensor_scalar_add(ckT[b][:, 1:257],
                                                           ckT[b][:, 1:257], b2b[:, :])
                            grp("vector", ev_ck)

                        def ld_mem(e):
                            e.dma_start(out=ckT[0][:, 0:1], in_=memk[:, :]).then_inc(sem_dg, 16)
                            return e.dma_start(out=ckT[1][:, 0:1], in_=memk[:, :])
                        grp("gpsimd", ld_mem, dman=1)
                    else:
                        for b in range(Bq):
                            for ct in range(3):
                                def mm_cv(e, b=b, ct=ct):
                                    for u in range(16):
                                        e.matmul(
                                            pcv[:, :],
                                            reluT[:, u * RSTR + b * 256 + ct * 128:
                                                  u * RSTR + b * 256 + ct * 128 + 128],
                                            w2b[:, u * 128:(u + 1) * 128],
                                            start=(u == 0), stop=False,
                                            skip_group_check=True)
                                    return e.matmul(pcv[:, :], e1tb[:, :], vb2b[:, :],
                                                    start=False, stop=True,
                                                    skip_group_check=True)
                                grp("tensor", mm_cv)

                                def ev_cv(e, b=b, ct=ct):
                                    return e.tensor_copy(
                                        cvn[b][:, ct * 129: ct * 129 + 128], pcv[:, :])
                                grp("vector", ev_cv)

                        def ld_memv(e):
                            e.dma_start(out=cvn[0][0:1, 0:128], in_=memv[:, :]).then_inc(sem_dg, 16)
                            return e.dma_start(out=cvn[1][0:1, 0:128], in_=memv[:, :])
                        grp("gpsimd", ld_memv, dman=1)

                        def fix_cv(e):
                            v0 = cvn[0].ap().rearrange("p (c t) -> p c t", c=3)
                            e.memset(v0[:, :, 128:129], 1.0)
                            v1 = cvn[1].ap().rearrange("p (c t) -> p c t", c=3)
                            return e.memset(v1[:, :, 128:129], 1.0)
                        grp("vector", fix_cv)

                def mk_vs(e):
                    for b in range(Bq):
                        sv = vnat[b].ap().rearrange("p (c t) -> p c t", c=32)
                        dv = vnatS[b].ap().rearrange("p (c t) -> p c t", c=32)
                        e.dma_start(out=dv[0:64, :, :], in_=sv[64:128, :, :]).then_inc(sem_ds, 16)
                        if b == 1:
                            return e.dma_start(out=dv[64:128, 0:31, :], in_=sv[0:64, 1:32, :])
                        e.dma_start(out=dv[64:128, 0:31, :], in_=sv[0:64, 1:32, :]).then_inc(sem_ds, 16)
                grp("sync", mk_vs, dman=3)

            stC = contextlib.ExitStack()
            with stC:
                EE = stC.enter_context
                bst32b = EE(nc.sbuf_tensor("bst32b", [32, 896], BF16))
                bst8b = EE(nc.sbuf_tensor("bst8b", [8, 896], BF16))
                e32b = EE(nc.sbuf_tensor("e32b", [32, 512], BF16))
                e8b = EE(nc.sbuf_tensor("e8b", [8, 128], BF16))
                mx4 = EE(nc.sbuf_tensor("mx4", [128, 1], F32))
                ebigtb = EE(nc.sbuf_tensor("ebigtb", [128, 2048], BF16))
                owntb = EE(nc.sbuf_tensor("owntb", [128, 128], BF16))
                sm0b = EE(nc.sbuf_tensor("sm0b", [128, 64], BF16))
                sm1b = EE(nc.sbuf_tensor("sm1b", [128, 64], BF16))
                qTb = [EE(nc.sbuf_tensor("qTbP%d" % p, [128, G * 512], BF16)) for p in range(2)]
                rqTb = [EE(nc.sbuf_tensor("rqTbP%d" % p, [128, G * 512], BF16)) for p in range(2)]
                qsT = [EE(nc.sbuf_tensor("qsTP%d" % p, [128, 512], BF16)) for p in range(2)]
                gnatb = [EE(nc.sbuf_tensor("gnatbP%d" % p, [128, 48], F32)) for p in range(2)]
                pimp = EE(nc.sbuf_tensor("pimp", [128, 4 * 384], F32))
                mx8 = EE(nc.sbuf_tensor("mx8", [128, 8], F32))
                zap = EE(nc.sbuf_tensor("zap", [128, 384], F32))
                selw = EE(nc.sbuf_tensor("selw", [128, 384], F32))
                selb = [EE(nc.sbuf_tensor("selbP%d" % p, [128, 384], BF16)) for p in range(4)]
                zrow = EE(nc.sbuf_tensor("zrow", [128, 1], F32))
                thr = EE(nc.sbuf_tensor("thr", [128, 1], F32))
                selBIP = [EE(nc.sbuf_tensor("selBIPP%d" % p, [128, 3 * 512], BF16)) for p in range(2)]
                pcT = EE(nc.sbuf_tensor("pcT", [128, 3 * 512], BF16))
                pfT = EE(nc.sbuf_tensor("pfT", [128, 4 * 512], BF16))
                psT = [EE(nc.sbuf_tensor("psTP%d" % p, [128, 512], BF16)) for p in range(2)]
                recb = EE(nc.sbuf_tensor("recb", [128, 1], F32))
                couts = EE(nc.sbuf_tensor("couts", [128, G * 4 * 128], BF16))
                fouts = EE(nc.sbuf_tensor("fouts", [128, G * 4 * 128], BF16))
                souts = EE(nc.sbuf_tensor("souts", [128, G * 4 * 128], BF16))
                mbuf = EE(nc.sbuf_tensor("mbuf", [128, 512], BF16))
                mTb = [EE(nc.sbuf_tensor("mTbP%d" % p, [128, 512], BF16)) for p in range(2)]
                pool = [EE(nc.psum_tensor("pool%d" % s, [128, 512], F32))
                        for s in range(3)]
                pvt = [EE(nc.psum_tensor("pvt%d" % s, [128, 129], F32))
                       for s in range(4)]
                bPT = EE(nc.psum_tensor("bPT", [128, 512], BF16))

                def pv(it):
                    return pvt[it][:, :]

                def pvp(it, p0, p1):
                    return pvt[it][p0:p1, :]

                slot_rd = [None] * 3
                slot_ctr = [0]

                def take_slots(n):
                    assert n <= 3
                    out = []
                    for _ in range(n):
                        out.append(slot_ctr[0] % 3)
                        slot_ctr[0] += 1
                    return out

                def ld_bconsts(e):
                    e.dma_start(out=bst32b[:], in_=bst32_h[:, :]).then_inc(sem_dg, 16)
                    e.dma_start(out=bst8b[:], in_=bst8_h[:, :]).then_inc(sem_dg, 16)
                    e.dma_start(out=e32b[:], in_=e32_h[:, :]).then_inc(sem_dg, 16)
                    e.dma_start(out=e8b[:], in_=e8_h[:, :]).then_inc(sem_dg, 16)
                    e.dma_start(out=ebigtb[:], in_=ebigt_h[:, :]).then_inc(sem_dg, 16)
                    e.dma_start(out=owntb[:], in_=ownt_h[:, :]).then_inc(sem_dg, 16)
                    e.dma_start(out=sm0b[:], in_=sm0_h[:, :]).then_inc(sem_dg, 16)
                    return e.dma_start(out=sm1b[:], in_=sm1_h[:, :])
                grp("gpsimd", ld_bconsts, dman=7)

                T = {}
                for b in range(Bq):
                    for qc in range(8):
                        i = b * 8 + qc
                        P = i % 2
                        off = b * S + qc * 512
                        base = 32 * qc
                        nct = (32 * qc + 33 + 127) // 128
                        nkt = 4 * (qc + 1)
                        first = (i == 0)

                        def ld_q(e, off=off, P=P):
                            for g in range(G):
                                e.dma_start(out=qTb[P][:, g * 512:(g + 1) * 512],
                                            in_=qsc[g, :, off:off + 512]).then_inc(sem_ds, 16)
                                e.dma_start(out=rqTb[P][0:64, g * 512:(g + 1) * 512],
                                            in_=rqsc[g, 0, :, off:off + 512]).then_inc(sem_ds, 16)
                                if g == G - 1:
                                    return e.dma_start(
                                        out=rqTb[P][64:128, g * 512:(g + 1) * 512],
                                        in_=rqsc[g, 1, :, off:off + 512])
                                e.dma_start(out=rqTb[P][64:128, g * 512:(g + 1) * 512],
                                            in_=rqsc[g, 1, :, off:off + 512]).then_inc(sem_ds, 16)
                        T[('ldq', i)] = grp("sync", ld_q, dman=11,
                                            after=(None if first else
                                                   [T.get(('end', i - 2)),
                                                    T.get(('endv', i - 2))]))

                        def ld_gn(e, off=off, P=P):
                            for it in range(4):
                                s2 = gsc[:, off + it * 128: off + it * 128 + 128]
                                s2 = s2.rearrange("g p -> p g")
                                d2 = gnatb[P].ap()[:, it * 12:(it + 1) * 12]
                                if it == 3:
                                    return e.dma_start(out=d2, in_=s2,
                                                       allow_slow_non_contiguous=True)
                                e.dma_start(out=d2, in_=s2,
                                            allow_slow_non_contiguous=True
                                            ).then_inc(sem_dg, 16)
                        T[('gn', i)] = grp("gpsimd", ld_gn, dman=3,
                                           after=(None if first else
                                                  [T.get(('endv', i - 2))]))

                        def mk_qs(e, P=P):
                            e.tensor_add(qsT[P][:, :], qTb[P][:, 0:512], qTb[P][:, 512:1024])
                            e.tensor_add(qsT[P][:, :], qsT[P][:, :], qTb[P][:, 1024:1536])
                            return e.tensor_add(qsT[P][:, :], qsT[P][:, :], qTb[P][:, 1536:2048])
                        T[('mkqs', i)] = grp("vector", mk_qs,
                                             after=[T[('ldq', i)], T.get(('end', i - 2))])

                        for h in range(2):
                            s_imp = take_slots(2)

                            def mm_imp(e, base=base, b=b, P=P, h=h,
                                       s_imp=tuple(s_imp)):
                                for x in range(2):
                                    it = 2 * h + x
                                    ibase = base + 8 * it
                                    e.matmul(pool[s_imp[x]][:, 0:384], e8b[:, :],
                                             bst8b[:, 257 - ibase: 257 - ibase + 384],
                                             start=True, stop=False, skip_group_check=True)
                                    last = e.matmul(pool[s_imp[x]][:, 0:384],
                                                    qsT[P][:, it * 128:(it + 1) * 128],
                                                    ckT[b][:, 1:385],
                                                    start=False, stop=True,
                                                    skip_group_check=True)
                                return last
                            T[('mmimp', i, h)] = grp("tensor", mm_imp,
                                                     after=[T[('mkqs', i)]]
                                                     + [slot_rd[s] for s in s_imp])

                            def exp_imp(e, h=h, s_imp=tuple(s_imp)):
                                for x in range(2):
                                    it = 2 * h + x
                                    last = e.activation(pimp[:, it * 384:(it + 1) * 384],
                                                        pool[s_imp[x]][:, 0:384],
                                                        ACT.Exp, bias=0.0,
                                                        scale=SCALE / 4.0)
                                return last
                            T[('expimp', i, h)] = grp("scalar", exp_imp,
                                                      after=[T[('mmimp', i, h)]]
                                                      + [T.get(('sel', i - 1, it))
                                                         for it in range(4)])
                            for s in s_imp:
                                slot_rd[s] = T[('expimp', i, h)]

                        for it in range(4):
                            def sel(e, it=it):
                                pp = pimp[:, it * 384:(it + 1) * 384]
                                e.reduce_sum(zrow[:, :], pp, axis=AX.X)
                                vdr(e, e.max(out=mx8[:, :], in_=pp))
                                e.tensor_scalar_mul(thr[:, :], zrow[:, :], 1e-10)
                                vdr(e, e.tensor_copy(mx4[:, :], mx8[:, 3:4]))
                                e.tensor_scalar(selw[:, :], pp,
                                                mx4[:, :], None, AL.is_ge)
                                e.tensor_scalar(zap[:, :], pp,
                                                thr[:, :], None, AL.is_gt)
                                e.tensor_mul(selw[:, :], selw[:, :], zap[:, :])
                                return e.tensor_copy(selb[it][:, :], selw[:, :])
                            T[('sel', i, it)] = grp(
                                "vector", sel,
                                after=[T[('expimp', i, it // 2)],
                                       T.get(('trsel', i - 1, it))])

                        bpt_rd = [T.get(('evm', i - 1, 3))]

                        for g in range(G):
                            qg = qTb[P][:, g * 512:(g + 1) * 512]
                            s_c = take_slots(nct)

                            def mm_c(e, g=g, qg=qg, b=b, base=base, nct=nct,
                                     s_c=tuple(s_c)):
                                for ct in range(nct):
                                    col = 256 + 128 * ct - base
                                    e.matmul(pool[s_c[ct]][:, :], bst32b[:, col:col + 128],
                                             e32b[:, :], start=True, stop=False,
                                             skip_group_check=True)
                                    last = e.matmul(pool[s_c[ct]][:, :],
                                                    ckT[b][:, ct * 128:(ct + 1) * 128],
                                                    qg, start=False, stop=True,
                                                    skip_group_check=True)
                                return last
                            T[('mmc', i, g)] = grp("tensor", mm_c,
                                                   after=[T[('ldq', i)]]
                                                   + [slot_rd[s] for s in s_c])

                            def exp_c(e, nct=nct, s_c=tuple(s_c)):
                                for ct in range(nct):
                                    last = e.activation(pcT[:, ct * 512:(ct + 1) * 512],
                                                        pool[s_c[ct]][:, :], ACT.Exp,
                                                        bias=0.0, scale=SCALE)
                                return last
                            T[('expc', i, g)] = grp("scalar", exp_c,
                                                    after=[T[('mmc', i, g)],
                                                           T.get(('cpv', i, g - 1))])
                            for s in s_c:
                                slot_rd[s] = T[('expc', i, g)]

                            def mm_cpv(e, b=b, nct=nct):
                                for it in range(4):
                                    for ct in range(nct):
                                        last = e.matmul(
                                            pv(it),
                                            pcT[:, ct * 512 + it * 128:
                                                ct * 512 + (it + 1) * 128],
                                            cvn[b][:, ct * 129:(ct + 1) * 129],
                                            start=(ct == 0), stop=(ct == nct - 1),
                                            skip_group_check=True)
                                return last
                            pv_war = (T.get(('divf', i - 1, 3)) if g == 0
                                      else T.get(('divc', i, g - 1)))
                            T[('cpv', i, g)] = grp("tensor", mm_cpv,
                                                   after=[T[('expc', i, g)], pv_war])

                            def tr_sel(e, it=g, P=P):
                                for ct in range(3):
                                    last = e.transpose(bPT[:, ct * 128:(ct + 1) * 128],
                                                       selb[it][:, ct * 128:(ct + 1) * 128],
                                                       identb[:, :])
                                return last
                            T[('trsel', i, g)] = grp("tensor", tr_sel,
                                                     after=[T[('sel', i, g)]] + bpt_rd)

                            def ev_sel(e, it=g, P=P):
                                for ct in range(3):
                                    e.tensor_copy(
                                        selBIP[P][:, ct * 512 + it * 128:
                                                  ct * 512 + (it + 1) * 128],
                                        bPT[:, ct * 128:(ct + 1) * 128])
                                    last = e.tensor_scalar_add(
                                        selBIP[P][:, ct * 512 + it * 128:
                                                  ct * 512 + (it + 1) * 128],
                                        selBIP[P][:, ct * 512 + it * 128:
                                                  ct * 512 + (it + 1) * 128], -1.0)
                                return last
                            T[('evsel', i, g)] = grp("vector", ev_sel,
                                                     after=[T[('trsel', i, g)],
                                                            T.get(('end', i - 2))])
                            bpt_rd = [T[('evsel', i, g)]]

                            def div_c(e, g=g):
                                for it in range(4):
                                    o = (g * 4 + it) * 128
                                    vdr(e, e.reciprocal(recb[:, :], pv(it)[:, 128:129]))
                                    e.tensor_copy(couts[:, o:o + 128], pv(it)[:, 0:128])
                                    last = e.tensor_scalar_mul(
                                        couts[:, o:o + 128], couts[:, o:o + 128],
                                        recb[:, :])
                                return last
                            T[('divc', i, g)] = grp("vector", div_c,
                                                    after=[T[('cpv', i, g)]])

                        for g in range(G):
                            rqg = rqTb[P][:, g * 512:(g + 1) * 512]
                            s_s = take_slots(1)[0]

                            def mm_s(e, rqg=rqg, qc=qc, b=b, s_s=s_s):
                                sl = pool[s_s]
                                for wl in range(8):
                                    w = 8 * qc + wl
                                    ocol = wl * 64
                                    if w == 0:
                                        e.matmul(sl[:, ocol:ocol + 64], identb[:, :],
                                                 sm0b[:, :], start=True, stop=False,
                                                 skip_group_check=True)
                                        last = e.matmul(sl[:, ocol:ocol + 64],
                                                        rkT[b][:, 0:128],
                                                        rqg[:, 0:64],
                                                        start=False, stop=True,
                                                        skip_group_check=True)
                                    else:
                                        e.matmul(sl[:, ocol:ocol + 64], identb[:, :],
                                                 sm1b[:, :], start=True, stop=False,
                                                 skip_group_check=True)
                                        last = e.matmul(sl[:, ocol:ocol + 64],
                                                        rkT[b][:, 64 * (w - 1): 64 * (w + 1)],
                                                        rqg[:, wl * 64: wl * 64 + 64],
                                                        start=False, stop=True,
                                                        skip_group_check=True)
                                return last
                            T[('mms', i, g)] = grp("tensor", mm_s,
                                                   after=[T[('ldq', i)], slot_rd[s_s]])

                            def exp_s(e, g=g, s_s=s_s):
                                return e.activation(psT[g % 2][:, :], pool[s_s][:, :],
                                                    ACT.Exp, bias=0.0, scale=SCALE)
                            T[('exps', i, g)] = grp("scalar", exp_s,
                                                    after=[T[('mms', i, g)],
                                                           T.get(('spv', i, g - 2))])
                            slot_rd[s_s] = T[('exps', i, g)]

                            def mm_spv(e, qc=qc, b=b, g=g):
                                for wl in range(8):
                                    w = 8 * qc + wl
                                    it2 = wl // 2
                                    half = wl % 2
                                    if w == 0:
                                        rv = vnat[b][:, 0:129]
                                    elif w % 2 == 1:
                                        kt2 = (w - 1) // 2
                                        rv = vnat[b][:, kt2 * 129: kt2 * 129 + 129]
                                    else:
                                        kt2 = (w - 2) // 2
                                        rv = vnatS[b][:, kt2 * 129: kt2 * 129 + 129]
                                    last = e.matmul(
                                        pvp(it2, 64 * half, 64 * half + 64),
                                        psT[g % 2][:, wl * 64: wl * 64 + 64], rv,
                                        start=True, stop=True,
                                        tile_position=(0, 64 * half),
                                        skip_group_check=True)
                                return last
                            sv_war = (T.get(('divc', i, 3)) if g == 0
                                      else T.get(('divs', i, g - 1)))
                            T[('spv', i, g)] = grp("tensor", mm_spv,
                                                   after=[T[('exps', i, g)], sv_war])

                            def div_s(e, g=g):
                                for it in range(4):
                                    o = (g * 4 + it) * 128
                                    vdr(e, e.reciprocal(recb[:, :], pv(it)[:, 128:129]))
                                    e.tensor_copy(souts[:, o:o + 128], pv(it)[:, 0:128])
                                    last = e.tensor_scalar_mul(
                                        souts[:, o:o + 128], souts[:, o:o + 128],
                                        recb[:, :])
                                return last
                            T[('divs', i, g)] = grp("vector", div_s,
                                                    after=[T[('spv', i, g)]])

                        for g in range(G):
                            rqg = rqTb[P][:, g * 512:(g + 1) * 512]
                            nbt = (nkt + 2) // 3
                            for j in range(nbt):
                                kts = list(range(3 * j, min(nkt, 3 * j + 3)))
                                s_f = take_slots(len(kts))

                                def mm_f(e, kts=tuple(kts), s_f=tuple(s_f),
                                         rqg=rqg, qc=qc, b=b, P=P):
                                    for x, kt in enumerate(kts):
                                        sl = pool[s_f[x]]
                                        ct = kt // 16
                                        jj = kt % 16
                                        e.matmul(sl[:, :], ebigtb[:, jj * 128:(jj + 1) * 128],
                                                 selBIP[P][:, ct * 512:(ct + 1) * 512],
                                                 start=True, stop=False,
                                                 skip_group_check=True)
                                        it2 = kt - 4 * qc
                                        if 0 <= it2 < 4:
                                            e.matmul(sl[:, it2 * 128:(it2 + 1) * 128],
                                                     identb[:, :], owntb[:, :],
                                                     start=False, stop=False,
                                                     skip_group_check=True)
                                        last = e.matmul(sl[:, :],
                                                        rkT[b][:, kt * 128:(kt + 1) * 128],
                                                        rqg, start=False, stop=True,
                                                        skip_group_check=True)
                                    return last
                                T[('mmf', i, g, j)] = grp(
                                    "tensor", mm_f,
                                    after=[T[('ldq', i)]]
                                    + [T[('evsel', i, it)] for it in range(4)]
                                    + [slot_rd[s] for s in s_f])

                                def exp_f(e, kts=tuple(kts), s_f=tuple(s_f)):
                                    for x in range(len(kts)):
                                        last = e.activation(
                                            pfT[:, x * 512:(x + 1) * 512],
                                            pool[s_f[x]][:, :], ACT.Exp,
                                            bias=0.0, scale=SCALE)
                                    return last
                                T[('expf', i, g, j)] = grp(
                                    "scalar", exp_f,
                                    after=[T[('mmf', i, g, j)],
                                           T.get(('fpv', i, g, j - 1)),
                                           (T.get(('fpv', i, g - 1,
                                                   (nkt + 2) // 3 - 1)) if j == 0
                                            else None)])
                                for s in s_f:
                                    slot_rd[s] = T[('expf', i, g, j)]

                                def mm_fpv(e, kts=tuple(kts), qc=qc, b=b):
                                    for x, kt in enumerate(kts):
                                        for it2 in range(max(0, kt - 4 * qc), 4):
                                            last = e.matmul(
                                                pv(it2),
                                                pfT[:, x * 512 + it2 * 128:
                                                    x * 512 + (it2 + 1) * 128],
                                                vnat[b][:, kt * 129: kt * 129 + 129],
                                                start=(kt == 0),
                                                stop=(kt == 4 * qc + it2),
                                                skip_group_check=True)
                                    return last
                                fv_war = ((T.get(('divs', i, 3)) if g == 0
                                           else T.get(('divf', i, g - 1)))
                                          if j == 0 else None)
                                T[('fpv', i, g, j)] = grp(
                                    "tensor", mm_fpv,
                                    after=[T[('expf', i, g, j)], fv_war])

                            def div_f(e, g=g):
                                for it in range(4):
                                    o = (g * 4 + it) * 128
                                    vdr(e, e.reciprocal(recb[:, :], pv(it)[:, 128:129]))
                                    e.tensor_copy(fouts[:, o:o + 128], pv(it)[:, 0:128])
                                    last = e.tensor_scalar_mul(
                                        fouts[:, o:o + 128], fouts[:, o:o + 128],
                                        recb[:, :])
                                return last
                            T[('divf', i, g)] = grp("vector", div_f,
                                                    after=[T[('fpv', i, g, nbt - 1)]])

                        for it in range(4):
                            def mrg(e, it=it, P=P):
                                for g in range(G):
                                    o = (g * 4 + it) * 128
                                    gn = gnatb[P]
                                    e.tensor_scalar_mul(mbuf[:, g * 128:(g + 1) * 128],
                                                        couts[:, o:o + 128],
                                                        gn[:, it * 12 + 3 * g:
                                                           it * 12 + 3 * g + 1])
                                    e.scalar_tensor_tensor(
                                        mbuf[:, g * 128:(g + 1) * 128],
                                        fouts[:, o:o + 128],
                                        gn[:, it * 12 + 3 * g + 1: it * 12 + 3 * g + 2],
                                        mbuf[:, g * 128:(g + 1) * 128], AL.mult, AL.add)
                                    last = e.scalar_tensor_tensor(
                                        mbuf[:, g * 128:(g + 1) * 128],
                                        souts[:, o:o + 128],
                                        gn[:, it * 12 + 3 * g + 2: it * 12 + 3 * g + 3],
                                        mbuf[:, g * 128:(g + 1) * 128], AL.mult, AL.add)
                                return last
                            T[('mrg', i, it)] = grp("vector", mrg,
                                                    after=[T[('gn', i)],
                                                           T[('divf', i, 3)],
                                                           T.get(('trm', i, it - 1))])

                            def tr_m(e, it=it):
                                for j in range(4):
                                    last = e.transpose(bPT[:, j * 128:(j + 1) * 128],
                                                       mbuf[:, j * 128:(j + 1) * 128],
                                                       identb[:, :])
                                return last
                            T[('trm', i, it)] = grp("tensor", tr_m,
                                                    after=[T[('mrg', i, it)]] + bpt_rd)

                            def ev_m(e, it=it):
                                return e.tensor_copy(mTb[it % 2][:, :], bPT[:, :])
                            T[('evm', i, it)] = grp("vector", ev_m,
                                                    after=[T[('trm', i, it)],
                                                           T.get(('stm', i, it - 2))])
                            bpt_rd = [T[('evm', i, it)]]

                            def st_m(e, b=b, qc=qc, it=it):
                                tglob = b * S + qc * 512 + it * 128
                                owner = tglob // TPC
                                tcol = tglob % TPC
                                dst = a2a_in[owner * 512:(owner + 1) * 512,
                                             tcol:tcol + 128].rearrange(
                                                 "(j p) t -> p j t", p=128)
                                src = mTb[it % 2].ap().rearrange("p (j t) -> p j t", j=4)
                                return e.dma_start(out=dst[:, :, :], in_=src[:, :, :])
                            T[('stm', i, it)] = grp("sync", st_m,
                                                    after=[T[('evm', i, it)]])

                        T[('end', i)] = T[('trm', i, 3)]
                        T[('endv', i)] = T[('evm', i, 3)]

                        if DEBUG and (b, qc) in getattr(build_nc, "_dbg_bq", ()):
                            def dbg_dump(e, b=b, qc=qc, P=P):
                                key = "bq_%d_%d" % (b, qc)
                                e.dma_start(out=dbg[key + "_c"][:, :], in_=couts[:, :]).then_inc(sem_dg, 16)
                                e.dma_start(out=dbg[key + "_f"][:, :], in_=fouts[:, :]).then_inc(sem_dg, 16)
                                e.dma_start(out=dbg[key + "_s"][:, :], in_=souts[:, :]).then_inc(sem_dg, 16)
                                return e.dma_start(out=dbg[key + "_sel"][:, :],
                                                   in_=selBIP[P][:, 0:1024])
                            grp("gpsimd", dbg_dump, dman=3,
                                after=[T[('divf', i, 3)], T[('evsel', i, 3)]])

        def coll(e):
            with allow_ici_all_to_all():
                return e.collective_compute(
                    "AllToAll", AL.bypass,
                    replica_groups=[list(range(NCORE))],
                    ins=[a2a_in_h.ap().opt()],
                    outs=[a2a_out_h.ap().opt()])
        coll_grp(coll)

        xTown = dram("xTown", (D, TPC))
        stD = contextlib.ExitStack()
        with stD:
            ED = stD.enter_context
            bigA = ED(nc.sbuf_tensor("bigA", [128, 32 * TPC], BF16))
            bigB = ED(nc.sbuf_tensor("bigB", [128, 32 * TPC], BF16))
            wstr = [ED(nc.sbuf_tensor("wstr%d" % p, [128, 32 * 128], BF16)) for p in range(2)]
            xcb = [ED(nc.sbuf_tensor("xcb%d" % p, [128, TPC], BF16)) for p in range(2)]
            sqC = ED(nc.sbuf_tensor("sqC", [128, 2 * TPC], BF16))
            stat = ED(nc.sbuf_tensor("stat", [1, 4 * TPC], F32))
            nmB = ED(nc.sbuf_tensor("nmB", [128, TPC], F32))
            rsB = ED(nc.sbuf_tensor("rsB", [128, TPC], F32))
            laTb = ED(nc.sbuf_tensor("laTb", [128, 32 * 128], BF16))
            lbTb = ED(nc.sbuf_tensor("lbTb", [128, D], BF16))
            bfbb = ED(nc.sbuf_tensor("bfbb", [128, 32], F32))
            coefb = ED(nc.sbuf_tensor("coefb", [128, 1], F32))
            hTb = ED(nc.sbuf_tensor("hTb", [128, TPC], BF16))
            youtb = [ED(nc.sbuf_tensor("youtb%d" % p, [128, TPC], F32)) for p in range(2)]
            zrowC = youtb[0].ap()[:, 0:512]
            cPA = [ED(nc.psum_tensor("cPA%d" % p, [128, 512], F32)) for p in range(2)]
            cSM0 = ED(nc.psum_tensor("cSM0", [1, 512], F32))
            cSM1 = ED(nc.psum_tensor("cSM1", [1, 512], F32))
            cSS0 = ED(nc.psum_tensor("cSS0", [1, 512], F32))
            cSS1 = ED(nc.psum_tensor("cSS1", [1, 512], F32))
            cBC = ED(nc.psum_tensor("cBC", [128, 512], F32))
            cSM = [cSM0, cSM1]
            cSS = [cSS0, cSS1]

            def ld_c(e):
                src = a2a_out.rearrange("(kt p) t -> p kt t", p=128)
                dst = bigA.ap().rearrange("p (kt t) -> p kt t", kt=32)
                return e.dma_start(out=dst[:, 0:16, :], in_=src[:, 0:16, :])
            grp("sync", ld_c)

            def ld_cb(e):
                src = a2a_out.rearrange("(kt p) t -> p kt t", p=128)
                dst = bigA.ap().rearrange("p (kt t) -> p kt t", kt=32)
                return e.dma_start(out=dst[:, 16:32, :], in_=src[:, 16:32, :])
            grp("gpsimd", ld_cb)

            def ld_c2(e):
                src = laT.rearrange("(kt p) r -> p kt r", p=128)
                dst = laTb.ap().rearrange("p (kt r) -> p kt r", kt=32)
                e.dma_start(out=dst[:, :, :], in_=src[:, :, :]).then_inc(sem_dg, 16)
                e.dma_start(out=lbTb[:], in_=lbT[:, :]).then_inc(sem_dg, 16)
                e.dma_start(out=bfbb[:], in_=bfb_h[:, :]).then_inc(sem_dg, 16)
                return e.dma_start(out=coefb[:], in_=lw128[:, :])
            grp("gpsimd", ld_c2, dman=3)

            def mk_coef(e):
                return e.tensor_scalar_mul(coefb[:, :], coefb[:, :], 2.0)
            grp("vector", mk_coef)

            if DEBUG and "bigA0" in dbg:
                def dump_bigA(e):
                    return e.dma_start(out=dbg["bigA0"][:, :], in_=bigA[:, :])
                grp("gpsimd", dump_bigA)

            for phase in range(2):
                if DEBUG and phase == 1 and "x1d" in dbg:
                    def dump_x1(e):
                        e.dma_start(out=dbg["x1d"][:, :], in_=bigA[:, :]).then_inc(sem_dg, 16)
                        e.dma_start(out=dbg["htbd"][:, :], in_=hTb[:, :]).then_inc(sem_dg, 16)
                        return e.dma_start(out=dbg["statd"][:, :], in_=stat[:, :])
                    grp("gpsimd", dump_x1, dman=2)
                wsrc = woT if phase == 0 else wfT
                c_w = {}; c_x = {}; c_mm = {}; c_ev = {}; c_st = {}

                def emit_ldw(ft, first, wsrc=wsrc):
                    p = ft % 2

                    def ld_wt(e, ft=ft, wsrc=wsrc, p=p):
                        src = wsrc.rearrange("(kt p) f -> p kt f", p=128)
                        dst = wstr[p].ap().rearrange("p (kt f) -> p kt f", kt=32)
                        return e.dma_start(out=dst[:, :, :],
                                           in_=src[:, :, ft * 128:(ft + 1) * 128])
                    c_w[ft] = grp("gpsimd", ld_wt,
                                  after=(None if first else [c_mm.get((ft - 2, 1))]))
                    if phase == 0:
                        def ld_xc(e, ft=ft, p=p):
                            return e.dma_start(out=xcb[p][:, :],
                                               in_=xTown[ft * 128:(ft + 1) * 128, :])
                        c_x[ft] = grp("gpsimd", ld_xc,
                                      after=(None if first else
                                             [c_ev.get((ft - 2, 0)), c_ev.get((ft - 2, 1))]))

                def emit_mmst(ft):
                    def mm_st(e, ft=ft):
                        sq0 = (ft % 2) * TPC
                        for tc2 in range(2):
                            e.matmul(cSM[tc2][:, :], ones128b[:, :],
                                     bigB[:, ft * TPC + tc2 * 512: ft * TPC + tc2 * 512 + 512],
                                     start=(ft == 0), stop=(ft == 31),
                                     skip_group_check=True)
                            last = e.matmul(cSS[tc2][:, :], ones128b[:, :],
                                            sqC[:, sq0 + tc2 * 512: sq0 + (tc2 + 1) * 512],
                                            start=(ft == 0), stop=(ft == 31),
                                            skip_group_check=True)
                        return last
                    c_st[ft] = grp("tensor", mm_st,
                                   after=[c_ev[(ft, 0)], c_ev[(ft, 1)]])

                emit_ldw(0, True)
                emit_ldw(1, False)
                for ft in range(32):
                    p = ft % 2
                    for tc2 in range(TPC // 512):
                        def mm_o(e, ft=ft, tc2=tc2, phase=phase, p=p):
                            for kt in range(32):
                                last = e.matmul(
                                    cPA[tc2][:, :], wstr[p][:, kt * 128:(kt + 1) * 128],
                                    bigA[:, kt * TPC + tc2 * 512: kt * TPC + tc2 * 512 + 512],
                                    start=(kt == 0),
                                    stop=(kt == 31 and phase == 0),
                                    skip_group_check=True)
                            if phase == 1:
                                last = e.matmul(
                                    cPA[tc2][:, :], lbTb[:, ft * 128:(ft + 1) * 128],
                                    hTb[:, tc2 * 512:(tc2 + 1) * 512],
                                    start=False, stop=True, skip_group_check=True)
                            return last
                        aft = [c_w[ft], c_ev.get((ft - 1, tc2))]
                        if ft == 0 and tc2 == 0:
                            aft = None
                        c_mm[(ft, tc2)] = grp("tensor", mm_o, after=aft)
                        if tc2 == 1 and ft >= 1:
                            emit_mmst(ft - 1)

                        def ev_o(e, ft=ft, tc2=tc2, phase=phase, p=p):
                            dsttile = bigB[:, ft * TPC + tc2 * 512: ft * TPC + tc2 * 512 + 512]
                            if phase == 0:
                                e.tensor_add(dsttile, cPA[tc2][:, :],
                                             xcb[p][:, tc2 * 512:(tc2 + 1) * 512])
                            else:
                                e.tensor_scalar_add(zrowC[:, :], cPA[tc2][:, :],
                                                    bfbb[:, ft:ft + 1])
                                e.tensor_add(dsttile, zrowC[:, :],
                                             bigA[:, ft * TPC + tc2 * 512:
                                                  ft * TPC + tc2 * 512 + 512])
                            return e.tensor_mul(
                                sqC[:, (ft % 2) * TPC + tc2 * 512:
                                    (ft % 2) * TPC + (tc2 + 1) * 512],
                                dsttile, dsttile)
                        evaft = [c_mm[(ft, tc2)], c_st.get(ft - 2)]
                        if phase == 0:
                            evaft.append(c_x[ft])
                        c_ev[(ft, tc2)] = grp("vector", ev_o, after=evaft)
                    if ft + 2 < 32:
                        emit_ldw(ft + 2, False)
                emit_mmst(31)

                if DEBUG and phase == 0 and "bigB0" in dbg:
                    def dump_bigB(e):
                        return e.dma_start(out=dbg["bigB0"][:, :], in_=bigB[:, :])
                    grp("gpsimd", dump_bigB)
                if DEBUG and phase == 1 and "res2d" in dbg:
                    def dump_res2(e):
                        return e.dma_start(out=dbg["res2d"][:, :], in_=bigB[:, :])
                    grp("gpsimd", dump_res2)

                def ev_st(e):
                    for tc2 in range(2):
                        e.tensor_copy(stat[:, tc2 * 512:(tc2 + 1) * 512], cSM[tc2][:, :])
                        e.tensor_copy(stat[:, TPC + tc2 * 512: TPC + (tc2 + 1) * 512],
                                      cSS[tc2][:, :])
                    e.tensor_scalar_mul(stat[:, 2 * TPC:3 * TPC], stat[:, 0:TPC],
                                        1.0 / D)
                    e.tensor_scalar_mul(stat[:, TPC:2 * TPC], stat[:, TPC:2 * TPC],
                                        1.0 / D)
                    e.tensor_mul(stat[:, 3 * TPC:4 * TPC], stat[:, 2 * TPC:3 * TPC],
                                 stat[:, 2 * TPC:3 * TPC])
                    e.tensor_sub(stat[:, TPC:2 * TPC], stat[:, TPC:2 * TPC],
                                 stat[:, 3 * TPC:4 * TPC])
                    return e.tensor_scalar_add(stat[:, TPC:2 * TPC],
                                               stat[:, TPC:2 * TPC], EPS)
                tst1 = grp("vector", ev_st)

                def st_sqrt(e):
                    return e.activation(stat[:, 3 * TPC:4 * TPC], stat[:, TPC:2 * TPC],
                                        ACT.Sqrt, bias=0.0, scale=1.0)
                tst2 = grp("scalar", st_sqrt, after=[tst1])

                def st_fin(e):
                    e.reciprocal(stat[:, TPC:2 * TPC], stat[:, 3 * TPC:4 * TPC])
                    return e.tensor_scalar_mul(stat[:, 2 * TPC:3 * TPC],
                                               stat[:, 2 * TPC:3 * TPC], -1.0)
                tst3 = grp("vector", st_fin, after=[tst2])

                tprev_mm = None
                tprev_ev = None
                tbcs = []
                for which in range(2):
                    dstb = nmB if which == 0 else rsB
                    srcoff = 2 * TPC if which == 0 else TPC
                    for tc2 in range(2):
                        def bc_prep(e, srcoff=srcoff, tc2=tc2):
                            e.memset(zrowC[:], 0.0)
                            return e.tensor_copy(
                                zrowC[0:1, :],
                                stat[:, srcoff + tc2 * 512: srcoff + (tc2 + 1) * 512])
                        tp = grp("vector", bc_prep, after=[tst3, tprev_mm])

                        def bc_mm(e):
                            return e.matmul(cBC[:, :], e1tf[:, :], zrowC[:, :],
                                            start=True, stop=True, skip_group_check=True)
                        tprev_mm = grp("tensor", bc_mm, after=[tp, tprev_ev])

                        def bc_ev(e, dstb=dstb, tc2=tc2):
                            return e.tensor_copy(dstb[:, tc2 * 512:(tc2 + 1) * 512],
                                                 cBC[:, :])
                        tprev_ev = grp("vector", bc_ev, after=[tprev_mm])
                        tbcs.append(tprev_ev)

                if phase == 0:
                    for ft in range(32):
                        def nrm(e, ft=ft):
                            sl = slice(ft * TPC, (ft + 1) * TPC)
                            e.tensor_add(bigA[:, sl], bigB[:, sl], nmB[:, :])
                            return e.tensor_mul(bigA[:, sl], bigA[:, sl], rsB[:, :])
                        grp("vector" if ft % 2 == 0 else "gpsimd", nrm,
                            after=list(tbcs))

                    tmh = None
                    for tc2 in range(2):
                        def mm_h2(e, tc2=tc2):
                            for kt in range(32):
                                last = e.matmul(
                                    cPA[tc2][:, :], laTb[:, kt * 128:(kt + 1) * 128],
                                    bigA[:, kt * TPC + tc2 * 512: kt * TPC + tc2 * 512 + 512],
                                    start=(kt == 0), stop=(kt == 31),
                                    skip_group_check=True)
                            return last
                        tmh = grp("tensor", mm_h2, after=(None if tc2 == 0 else [tmh]))

                        def ev_h2(e, tc2=tc2):
                            e.tensor_copy(hTb[:, tc2 * 512:(tc2 + 1) * 512], cPA[tc2][:, :])
                            return e.tensor_scalar_mul(
                                hTb[:, tc2 * 512:(tc2 + 1) * 512],
                                hTb[:, tc2 * 512:(tc2 + 1) * 512], coefb[:, :])
                        grp("vector", ev_h2, after=[tmh])
                else:
                    tnrm = {}; tsty = {}
                    for ft in range(32):
                        def nrm2(e, ft=ft):
                            sl = slice(ft * TPC, (ft + 1) * TPC)
                            yb = youtb[ft % 2]
                            e.tensor_add(yb[:, :], bigB[:, sl], nmB[:, :])
                            return e.tensor_mul(yb[:, :], yb[:, :], rsB[:, :])
                        tnrm[ft] = grp("vector" if ft % 2 == 0 else "gpsimd", nrm2,
                                       after=list(tbcs) + [tsty.get(ft - 2)])

                        def st_y(e, ft=ft):
                            return e.dma_start(out=y[ft * 128:(ft + 1) * 128, :],
                                               in_=youtb[ft % 2][:, :])
                        tsty[ft] = grp("sync", st_y, after=[tnrm[ft]])

            def fin(e):
                return e.nop()
            grp("sync", fin)

    return nc


def kernel(**inputs):
    shared = _host_prep_shared(inputs)
    consts_keys = ["kw1T", "vw1T", "kb1", "vb1", "kw2T", "vw2T", "kb2", "vb2z",
                   "woT", "wfT", "bfb", "laT", "lbT", "lw128", "identc", "ones128",
                   "e1t", "bst32", "bst8", "e32", "e8", "e8ones", "ebigt", "ownt",
                   "smaskt0", "smaskt1", "cosT", "sinT", "xT"]
    in_maps = []
    for c in range(NCORE):
        d = _host_prep_core(inputs, c)
        for k in consts_keys:
            d[k] = shared[k]
        d["xTown"] = np.ascontiguousarray(shared["xT"][:, c * TPC:(c + 1) * TPC])
        in_maps.append(d)
    nc = build_nc()
    res = run_bass_kernel_spmd(nc, in_maps, core_ids=list(range(NCORE)))
    out = np.concatenate(
        [np.ascontiguousarray(res.results[c]["yT"].T) for c in range(NCORE)], axis=0)
    return out.reshape(Bq, S, D)


# version 11
# speedup vs baseline: 2.5355x; 1.2453x over previous
import contextlib
import numpy as np
import ml_dtypes
import concourse.bass as bass
import concourse.mybir as mybir
from concourse.bass_utils import run_bass_kernel_spmd
from concourse.aot_env import allow_ici_all_to_all

F32 = mybir.dt.float32
BF16 = mybir.dt.bfloat16
AL = mybir.AluOpType
AX = mybir.AxisListType
ACT = mybir.ActivationFunctionType

NCORE = 8
H, KVH, DH, G = 32, 8, 128, 4
Bq, S, D = 2, 4096, 4096
CB = 16
WIN = 64
SCALE = float(DH) ** -0.5
TCH = 256
NT = Bq * S
TPC = NT // NCORE
EPS = 1e-5
BIG = 1e30
RSTR = 648

DEBUG = False
FORCE_BARRIER = False
A2A_OUT = False
_DBG_SHAPES = {}


def _host_prep_core(inp, core):
    P = np.concatenate([np.arange(0, DH, 2), np.arange(1, DH, 2)])
    d = {}
    wqkv = inp["L0_attn_w_qkv"]
    qrows = np.concatenate(
        [wqkv[(4 * core + g) * DH + P] for g in range(G)], axis=0)
    krows = wqkv[H * DH + core * DH + P]
    vrows = wqkv[H * DH + KVH * DH + core * DH: H * DH + KVH * DH + (core + 1) * DH]
    d["wqkvT"] = np.ascontiguousarray(
        np.concatenate([qrows, krows, vrows], axis=0).T.astype(np.float32))
    wg = inp["L0_attn_w_gate"]
    grows = np.stack([wg[3 * (4 * core + g) + j] for g in range(G) for j in range(3)])
    d["wgT"] = np.ascontiguousarray(grows.T.astype(np.float32))
    bgv = inp["L0_attn_b_gate"]
    d["bg"] = np.ascontiguousarray(
        np.array([bgv[3 * (4 * core + g) + j] for g in range(G) for j in range(3)],
                 np.float32).reshape(12, 1))
    d["kposT"] = np.ascontiguousarray(inp["L0_attn_k_pos"][core][:, P].T.astype(np.float32))
    d["vposT"] = np.ascontiguousarray(inp["L0_attn_v_pos"][core].T.astype(np.float32))
    d["memk"] = np.ascontiguousarray(
        inp["L0_attn_mem_kv"][0, core, 0][P].reshape(128, 1).astype(np.float32))
    d["memv"] = np.ascontiguousarray(
        inp["L0_attn_mem_kv"][1, core, 0].reshape(1, 128).astype(np.float32))
    return d


def _host_prep_shared(inp):
    P = np.concatenate([np.arange(0, DH, 2), np.arange(1, DH, 2)])
    d = {}
    X = np.ascontiguousarray(inp["L0_attn_hidden_states"].reshape(NT, D))
    d["xT"] = np.ascontiguousarray(X.T)
    perm2048 = (np.arange(CB)[:, None] * DH + P[None, :]).reshape(-1)
    d["kw1T"] = np.ascontiguousarray(inp["L0_attn_k_w1"].T[perm2048].astype(np.float32))
    d["vw1T"] = np.ascontiguousarray(inp["L0_attn_v_w1"].T.astype(np.float32))
    d["kb1"] = np.ascontiguousarray(inp["L0_attn_k_b1"].reshape(2048, 1).astype(np.float32))
    d["vb1"] = np.ascontiguousarray(inp["L0_attn_v_b1"].reshape(2048, 1).astype(np.float32))
    d["kw2T"] = np.ascontiguousarray(inp["L0_attn_k_w2"].T[:, P].astype(np.float32))
    d["vw2T"] = np.ascontiguousarray(inp["L0_attn_v_w2"].T.astype(np.float32))
    d["kb2"] = np.ascontiguousarray(inp["L0_attn_k_b2"][P].reshape(128, 1).astype(np.float32))
    vb2z = np.zeros((128, 128), np.float32)
    vb2z[0, :] = inp["L0_attn_v_b2"]
    d["vb2z"] = vb2z
    d["woT"] = np.ascontiguousarray(inp["L0_attn_w_o"].T.astype(np.float32))
    d["wfT"] = np.ascontiguousarray(inp["L0_ffn_W"].T.astype(np.float32))
    d["bfb"] = np.ascontiguousarray(
        inp["L0_ffn_bias"].reshape(32, 128).T.astype(np.float32))
    d["laT"] = np.ascontiguousarray(
        inp["L0_ffn_lora_A"].reshape(128, D).T.astype(np.float32))
    d["lbT"] = np.ascontiguousarray(
        inp["L0_ffn_lora_B"].transpose(0, 2, 1).reshape(128, D).astype(np.float32))
    d["lw128"] = np.ascontiguousarray(
        np.repeat(np.asarray(inp["L0_ffn_lora_weights"]), 16).reshape(128, 1).astype(np.float32))
    d["identc"] = np.eye(128, dtype=np.float32)
    d["ones128"] = np.ones((128, 1), np.float32)
    e1t = np.zeros((128, 128), np.float32)
    e1t[0, :] = 1.0
    d["e1t"] = e1t
    xs = np.arange(896) - 256
    bst32 = np.zeros((32, 896), np.float32)
    for r in range(32):
        bst32[r, xs >= r + 1] = -BIG
    d["bst32"] = bst32
    bst8 = np.zeros((8, 896), np.float32)
    for r in range(8):
        bst8[r, xs >= r + 1] = -BIG
    d["bst8"] = bst8
    e32 = np.zeros((32, 512), np.float32)
    for q in range(512):
        e32[q // 16, q] = 1.0
    d["e32"] = e32
    e8 = np.zeros((8, 128), np.float32)
    for q in range(128):
        e8[q // 16, q] = 1.0
    d["e8"] = e8
    e8o = np.zeros((8, 1024), np.float32)
    for r in range(8):
        e8o[r, r * 128:(r + 1) * 128] = 1.0
    d["e8ones"] = e8o
    ebig = np.zeros((128, 16 * 128), np.float32)
    for j in range(16):
        for m in range(128):
            ebig[8 * j + m // 16, j * 128 + m] = BIG
    d["ebigt"] = ebig
    ownt = np.zeros((128, 128), np.float32)
    for k in range(128):
        for q in range(128):
            if k // 16 == q // 16 and k % 16 <= q % 16:
                ownt[k, q] = BIG
    d["ownt"] = ownt
    sm0 = np.full((128, 64), -BIG, np.float32)
    sm1 = np.full((128, 64), -BIG, np.float32)
    for k in range(128):
        for q in range(64):
            if k <= q:
                sm0[k, q] = 0.0
            if q <= k <= q + 64:
                sm1[k, q] = 0.0
    d["smaskt0"] = sm0
    d["smaskt1"] = sm1
    inv = (1.0 / 10000.0 ** (np.arange(0, DH, 2, dtype=np.float64) / DH))
    ang = inv[:, None] * np.arange(S, dtype=np.float64)[None, :]
    d["cosT"] = np.cos(ang).astype(np.float32)
    d["sinT"] = np.sin(ang).astype(np.float32)
    return d


def build_nc():
    nc = bass.Bass("TRN2", target_bir_lowering=False)

    def dram(name, shape, dt=F32, kind="ExternalInput"):
        return nc.dram_tensor(name, list(shape), dt, kind=kind).ap()

    xT = dram("xT", (D, NT))
    wqkvT = dram("wqkvT", (D, 768))
    wgT = dram("wgT", (D, 12))
    bg = dram("bg", (12, 1))
    kw1T = dram("kw1T", (2048, 2048))
    vw1T = dram("vw1T", (2048, 2048))
    kb1 = dram("kb1", (2048, 1))
    vb1 = dram("vb1", (2048, 1))
    kw2T = dram("kw2T", (2048, 128))
    vw2T = dram("vw2T", (2048, 128))
    kb2 = dram("kb2", (128, 1))
    vb2z = dram("vb2z", (128, 128))
    kposT = dram("kposT", (128, 16))
    vposT = dram("vposT", (128, 16))
    memk = dram("memk", (128, 1))
    memv = dram("memv", (1, 128))
    woT = dram("woT", (D, D))
    wfT = dram("wfT", (D, D))
    bfb_h = dram("bfb", (128, 32))
    laT = dram("laT", (D, 128))
    lbT = dram("lbT", (128, D))
    lw128 = dram("lw128", (128, 1))
    identc = dram("identc", (128, 128))
    ones128 = dram("ones128", (128, 1))
    e1t_h = dram("e1t", (128, 128))
    bst32_h = dram("bst32", (32, 896))
    bst8_h = dram("bst8", (8, 896))
    e32_h = dram("e32", (32, 512))
    e8_h = dram("e8", (8, 128))
    e8ones_h = dram("e8ones", (8, 1024))
    ebigt_h = dram("ebigt", (128, 2048))
    ownt_h = dram("ownt", (128, 128))
    sm0_h = dram("smaskt0", (128, 64))
    sm1_h = dram("smaskt1", (128, 64))
    cosT = dram("cosT", (64, S))
    sinT = dram("sinT", (64, S))

    y = dram("yT", (D, TPC), F32, "ExternalOutput")

    qsc = dram("qsc", (G, 128, NT), BF16, None) if False else \
        nc.dram_tensor("qsc", [G, 128, NT], BF16).ap()
    rqsc = nc.dram_tensor("rqsc", [G, 2, 64, NT], BF16).ap()
    gsc = nc.dram_tensor("gsc", [12, NT], BF16).ap()
    bfsc = nc.dram_tensor("bfsc", [G, NT], F32).ap()
    qnsc = nc.dram_tensor("qnsc", [G, NT], F32).ap()
    bcsc = nc.dram_tensor("bcsc", [G, NT], F32).ap()
    bisc = nc.dram_tensor("bisc", [1, NT], F32).ap()
    a2a_in_h = nc.dram_tensor("a2a_in", [D, TPC], BF16)
    a2a_out_h = nc.dram_tensor("a2a_out", [D, TPC], BF16)
    a2a_in = a2a_in_h.ap()
    a2a_out = a2a_out_h.ap()

    dbg = {}
    if DEBUG:
        for nm, shp in _DBG_SHAPES.items():
            dbg[nm] = dram("dbg_" + nm, shp, F32, "ExternalOutput")

    cnt = dict(ds=0, dg=0, ts=0, vs=0, ss=0, cs=0)

    st0 = contextlib.ExitStack()
    with st0:
        EC = st0.enter_context
        identb = EC(nc.sbuf_tensor("identb", [128, 128], BF16))
        ones128b = EC(nc.sbuf_tensor("ones128b", [128, 1], BF16))
        e1tb = EC(nc.sbuf_tensor("e1tb", [128, 128], BF16))
        e1tf = EC(nc.sbuf_tensor("e1tf", [128, 128], F32))
        mck2 = EC(nc.sbuf_tensor("mck2", [1, 1], F32))
        mckr = EC(nc.sbuf_tensor("mckr", [1, 1], F32))
        scal1 = EC(nc.sbuf_tensor("scal1", [1, 8], F32))
        sem_ds = EC(nc.semaphore("sem_ds"))
        sem_dg = EC(nc.semaphore("sem_dg"))
        sem_ts = EC(nc.semaphore("sem_ts"))
        sem_vs = EC(nc.semaphore("sem_vs"))
        sem_ss = EC(nc.semaphore("sem_ss"))
        sem_cs = EC(nc.semaphore("sem_cs"))
        block = EC(nc.Block())
        SEM = dict(ds=sem_ds, dg=sem_dg, ts=sem_ts, vs=sem_vs, ss=sem_ss, cs=sem_cs)
        OWNSEM = dict(sync="ds", gpsimd="dg", tensor="ts", vector="vs", scalar="ss")

        def vdr(e, inst):
            inst.then_inc(SEM["vs"], 1)
            cnt["vs"] += 1
            e.wait_ge(SEM["vs"], cnt["vs"])

        def grp(engname, fn, dman=0, after=None):
            own = OWNSEM[engname]
            n = 16 if engname in ("sync", "gpsimd") else 1
            if FORCE_BARRIER:
                after = None
            if after is None:
                waits = [(s, cnt[s]) for s in ("ds", "dg", "ts", "vs", "ss", "cs")
                         if cnt[s] > 0]
            else:
                bys = {}
                for t in after:
                    if t is None:
                        continue
                    s, vv = t
                    if vv > bys.get(s, -1):
                        bys[s] = vv
                waits = [(s, vv) for s, vv in sorted(bys.items()) if vv > 0]

            def body(e):
                for s, vv in waits:
                    e.wait_ge(SEM[s], vv)
                last = fn(e)
                last.then_inc(SEM[own], n)
            getattr(block, engname)(body)
            cnt[own] += n + dman * 16
            return (own, cnt[own])

        def reserve(engname, k):
            own = OWNSEM[engname]
            toks = [(own, cnt[own] + j + 1) for j in range(k)]
            cnt[own] += k
            return toks

        def _mergew(after):
            bys = {}
            for t in after:
                if t is None:
                    continue
                s, vv = t
                if vv > bys.get(s, -1):
                    bys[s] = vv
            return sorted(bys.items())

        def emit_reserved(engname, phases):
            own = OWNSEM[engname]

            def body(e):
                for after, fn in phases:
                    for s, vv in _mergew(after):
                        e.wait_ge(SEM[s], vv)
                    fn(e).then_inc(SEM[own], 1)
            getattr(block, engname)(body)

        def grpM(engname, phases):
            own = OWNSEM[engname]
            toks = []

            def body(e):
                for after, fn in phases:
                    for s, vv in _mergew(after):
                        e.wait_ge(SEM[s], vv)
                    fn(e).then_inc(SEM[own], 1)
            getattr(block, engname)(body)
            for _ in phases:
                cnt[own] += 1
                toks.append((own, cnt[own]))
            return toks

        def coll_grp(fn):
            waits = [(s, cnt[s]) for s in ("ds", "dg", "ts", "vs", "ss")
                     if cnt[s] > 0]

            def body(e):
                for s, vv in waits:
                    e.wait_ge(SEM[s], vv)
                fn(e).then_inc(SEM["cs"], 1)
            block.gpsimd(body)
            cnt["cs"] += 1

        st1 = contextlib.ExitStack()
        with st1:
            E1 = st1.enter_context
            kT0 = E1(nc.sbuf_tensor("kT0", [128, S], BF16))
            kT1 = E1(nc.sbuf_tensor("kT1", [128, S], BF16))
            rkT0 = E1(nc.sbuf_tensor("rkT0", [128, S], BF16))
            rkT1 = E1(nc.sbuf_tensor("rkT1", [128, S], BF16))
            vT0 = E1(nc.sbuf_tensor("vT0", [128, S], BF16))
            vT1 = E1(nc.sbuf_tensor("vT1", [128, S], BF16))
            vnat0 = E1(nc.sbuf_tensor("vnat0", [128, 32 * 129], BF16))
            vnat1 = E1(nc.sbuf_tensor("vnat1", [128, 32 * 129], BF16))
            vnatS0 = E1(nc.sbuf_tensor("vnatS0", [128, 32 * 129], BF16))
            vnatS1 = E1(nc.sbuf_tensor("vnatS1", [128, 32 * 129], BF16))
            ckT0 = E1(nc.sbuf_tensor("ckT0", [128, 385], BF16))
            ckT1 = E1(nc.sbuf_tensor("ckT1", [128, 385], BF16))
            cv0 = E1(nc.sbuf_tensor("cv0", [128, 3 * 129], BF16))
            cv1 = E1(nc.sbuf_tensor("cv1", [128, 3 * 129], BF16))
            kT = [kT0, kT1]
            rkT = [rkT0, rkT1]
            vT = [vT0, vT1]
            vnat = [vnat0, vnat1]
            vnatS = [vnatS0, vnatS1]
            ckT = [ckT0, ckT1]
            cvn = [cv0, cv1]

            def ld_consts(e):
                e.dma_start(out=identb[:], in_=identc[:, :]).then_inc(sem_dg, 16)
                e.dma_start(out=ones128b[:], in_=ones128[:, :]).then_inc(sem_dg, 16)
                e.dma_start(out=e1tb[:], in_=e1t_h[:, :]).then_inc(sem_dg, 16)
                return e.dma_start(out=e1tf[:], in_=e1t_h[:, :])
            grp("gpsimd", ld_consts, dman=3)

            def init_misc(e):
                e.memset(mck2[:], 0.0)
                for b in range(Bq):
                    e.memset(vnat[b][:], 1.0)
                    e.memset(vnatS[b][:], 1.0)
                    e.memset(ckT[b][:], 0.0)
                    e.memset(cvn[b][:], 0.0)
                return e.memset(scal1[:], 0.0)
            grp("vector", init_misc)

            stA = contextlib.ExitStack()
            with stA:
                EA = stA.enter_context
                wqvb = EA(nc.sbuf_tensor("wqvb", [128, 32 * 768], BF16))
                wgb = EA(nc.sbuf_tensor("wgb", [128, 32 * 12], BF16))
                bgb = EA(nc.sbuf_tensor("bgb", [12, 1], F32))
                cosb = [EA(nc.sbuf_tensor("cosb%d" % p, [64, TCH], BF16)) for p in range(2)]
                sinb = [EA(nc.sbuf_tensor("sinb%d" % p, [64, TCH], BF16)) for p in range(2)]
                xbuf = [EA(nc.sbuf_tensor("xbuf%d" % p, [128, 32 * TCH], BF16)) for p in range(2)]
                qbuf = [EA(nc.sbuf_tensor("qbuf%d" % p, [128, G * TCH], BF16)) for p in range(2)]
                hiA = [EA(nc.sbuf_tensor("hiA%d" % p, [64, 5 * TCH], BF16)) for p in range(2)]
                rlo = [EA(nc.sbuf_tensor("rlo%d" % p, [64, 5 * TCH], BF16)) for p in range(2)]
                rhi = [EA(nc.sbuf_tensor("rhi%d" % p, [64, 5 * TCH], BF16)) for p in range(2)]
                rt1 = EA(nc.sbuf_tensor("rt1", [64, TCH], F32))
                rt2 = EA(nc.sbuf_tensor("rt2", [64, TCH], F32))
                gbuf = [EA(nc.sbuf_tensor("gbuf%d" % p, [12, TCH], BF16)) for p in range(2)]
                pq01 = [EA(nc.psum_tensor("pq01_%d" % p, [128, 512], F32)) for p in range(2)]
                pq23 = [EA(nc.psum_tensor("pq23_%d" % p, [128, 512], F32)) for p in range(2)]
                pkv = [EA(nc.psum_tensor("pkv%d" % p, [128, 512], F32)) for p in range(2)]
                pmx = EA(nc.psum_tensor("pmx", [128, TCH], F32))
                ptb = EA(nc.psum_tensor("ptb", [128, TCH], BF16))

                def ld_w(e):
                    src = wqkvT.rearrange("(kd p) f -> p kd f", p=128)
                    dst = wqvb.ap().rearrange("p (kd f) -> p kd f", kd=32)
                    e.dma_start(out=dst[:, :, :], in_=src[:, :, :]).then_inc(sem_dg, 16)
                    src = wgT.rearrange("(kd p) f -> p kd f", p=128)
                    dst = wgb.ap().rearrange("p (kd f) -> p kd f", kd=32)
                    e.dma_start(out=dst[:, :, :], in_=src[:, :, :]).then_inc(sem_dg, 16)
                    return e.dma_start(out=bgb[:], in_=bg[:, :])
                grp("gpsimd", ld_w, dman=2)

                NC1 = NT // TCH
                t_ldg = {}; t_lds = {}; t_mm = {}; t_ev1 = {}; t_evg = {}
                t_dsh = {}; t_rp = {}; t_srq = {}; t_trv = {}; t_evv = {}
                t_stg = {}

                def emit_trv(ci):
                    b, tc = ci // (S // TCH), ci % (S // TCH)
                    t0 = tc * TCH
                    p = ci % 2

                    def tr_v(e, b=b, t0=t0, p=p):
                        for j in range(TCH // 128):
                            last = e.transpose(
                                ptb[:, j * 128:(j + 1) * 128],
                                vT[b][:, t0 + j * 128: t0 + (j + 1) * 128],
                                identb[:, :])
                        return last
                    t_trv[ci] = grp("tensor", tr_v,
                                    after=[t_ev1[ci], t_evv.get(ci - 1)])

                    def ev_v(e, b=b, t0=t0, p=p):
                        for j in range(TCH // 128):
                            kt2 = t0 // 128 + j
                            last = e.tensor_copy(
                                vnat[b][:, kt2 * 129: kt2 * 129 + 128],
                                ptb[:, j * 128:(j + 1) * 128])
                        return last
                    t_evv[ci] = grp("vector", ev_v, after=[t_trv[ci]])

                for ci in range(NC1):
                    b, tc = ci // (S // TCH), ci % (S // TCH)
                    t0 = tc * TCH
                    gt0 = b * S + t0
                    p = ci % 2

                    def ld_x_g(e, gt0=gt0, t0=t0, p=p):
                        src = xT.rearrange("(kd p) t -> p kd t", p=128)
                        dst = xbuf[p].ap().rearrange("p (kd t) -> p kd t", kd=32)
                        e.dma_start(out=dst[:, :, :],
                                    in_=src[:, :, gt0:gt0 + TCH]).then_inc(sem_dg, 16)
                        e.dma_start(out=cosb[p][:], in_=cosT[:, t0:t0 + TCH]).then_inc(sem_dg, 16)
                        return e.dma_start(out=sinb[p][:], in_=sinT[:, t0:t0 + TCH])
                    t_ldg[ci] = grp("gpsimd", ld_x_g, dman=2,
                                    after=[t_mm.get(ci - 2), t_rp.get(ci - 2)])

                    def mm_qkv(e, p=p):
                        tgt = [(pq01[p], 0), (pq01[p], 256), (pq23[p], 0),
                               (pq23[p], 256), (pkv[p], 0), (pkv[p], 256)]
                        for m in range(6):
                            dstp, c0 = tgt[m]
                            for kd in range(32):
                                e.matmul(dstp[:, c0:c0 + TCH],
                                         wqvb[:, kd * 768 + m * 128: kd * 768 + (m + 1) * 128],
                                         xbuf[p][:, kd * TCH:(kd + 1) * TCH],
                                         start=(kd == 0), stop=(kd == 31),
                                         skip_group_check=True)
                        for kd in range(32):
                            last = e.matmul(pmx[0:12, 0:TCH],
                                            wgb[:, kd * 12:(kd + 1) * 12],
                                            xbuf[p][:, kd * TCH:(kd + 1) * TCH],
                                            start=(kd == 0), stop=(kd == 31),
                                            skip_group_check=True)
                        return last
                    t_mm[ci] = grp("tensor", mm_qkv,
                                   after=[t_ldg[ci], t_ev1.get(ci - 2),
                                          t_evg.get(ci - 1)])

                    def ev1(e, b=b, t0=t0, p=p):
                        for g in range(G):
                            src = [pq01[p], pq01[p], pq23[p], pq23[p]][g]
                            c0 = [0, 256, 0, 256][g]
                            e.tensor_copy(qbuf[p][:, g * TCH:(g + 1) * TCH],
                                          src[:, c0:c0 + TCH])
                        e.tensor_copy(kT[b][:, t0:t0 + TCH], pkv[p][:, 0:TCH])
                        return e.tensor_copy(vT[b][:, t0:t0 + TCH], pkv[p][:, 256:256 + TCH])
                    t_ev1[ci] = grp("vector", ev1,
                                    after=[t_mm[ci], t_dsh.get(ci - 2)])

                    def ev_g(e, gt0=gt0, p=p):
                        return e.activation(gbuf[p][:, :], pmx[0:12, 0:TCH], ACT.Sigmoid,
                                            bias=bgb[:, :], scale=1.0)
                    t_evg[ci] = grp("scalar", ev_g,
                                    after=[t_mm[ci], t_stg.get(ci - 2)])

                    def st_g(e, gt0=gt0, p=p):
                        return e.dma_start(out=gsc[:, gt0:gt0 + TCH],
                                           in_=gbuf[p][:, :])
                    t_stg[ci] = grp("sync", st_g, after=[t_evg[ci]])

                    def dshift(e, b=b, t0=t0, gt0=gt0, p=p):
                        for g in range(G):
                            e.dma_start(out=hiA[p][:, g * TCH:(g + 1) * TCH],
                                        in_=qbuf[p][64:128, g * TCH:(g + 1) * TCH]
                                        ).then_inc(sem_ds, 16)
                            e.dma_start(out=qsc[g, :, gt0:gt0 + TCH],
                                        in_=qbuf[p][:, g * TCH:(g + 1) * TCH]
                                        ).then_inc(sem_ds, 16)
                        return e.dma_start(out=hiA[p][:, 4 * TCH:5 * TCH],
                                           in_=kT[b][64:128, t0:t0 + TCH])
                    t_dsh[ci] = grp("sync", dshift, dman=8,
                                    after=[t_ev1[ci], t_rp.get(ci - 2)])

                    def rope(e, b=b, t0=t0, p=p):
                        for u in range(5):
                            lo = (qbuf[p][0:64, u * TCH:(u + 1) * TCH] if u < 4
                                  else kT[b][0:64, t0:t0 + TCH])
                            hi = hiA[p][:, u * TCH:(u + 1) * TCH]
                            e.tensor_mul(rt1[:, :], lo, cosb[p][:, :])
                            e.tensor_mul(rt2[:, :], hi, sinb[p][:, :])
                            e.tensor_sub(rlo[p][:, u * TCH:(u + 1) * TCH], rt1[:, :], rt2[:, :])
                            e.tensor_mul(rt1[:, :], hi, cosb[p][:, :])
                            e.tensor_mul(rt2[:, :], lo, sinb[p][:, :])
                            e.tensor_add(rhi[p][:, u * TCH:(u + 1) * TCH], rt1[:, :], rt2[:, :])
                        return e.tensor_copy(rkT[b][0:64, t0:t0 + TCH],
                                             rlo[p][:, 4 * TCH:5 * TCH])
                    t_rp[ci] = grp("vector", rope,
                                   after=[t_dsh[ci], t_ldg[ci], t_srq.get(ci - 2)])

                    def st_rq(e, b=b, t0=t0, gt0=gt0, p=p):
                        for g in range(G):
                            e.dma_start(out=rqsc[g, 0, :, gt0:gt0 + TCH],
                                        in_=rlo[p][:, g * TCH:(g + 1) * TCH]).then_inc(sem_ds, 16)
                            e.dma_start(out=rqsc[g, 1, :, gt0:gt0 + TCH],
                                        in_=rhi[p][:, g * TCH:(g + 1) * TCH]).then_inc(sem_ds, 16)
                        return e.dma_start(out=rkT[b][64:128, t0:t0 + TCH],
                                           in_=rhi[p][:, 4 * TCH:5 * TCH])
                    t_srq[ci] = grp("sync", st_rq, dman=8, after=[t_rp[ci]])

                    if ci >= 1:
                        emit_trv(ci - 1)
                emit_trv(NC1 - 1)

            stB = contextlib.ExitStack()
            with stB:
                EB = stB.enter_context
                ckin = EB(nc.sbuf_tensor("ckin", [128, 16 * 512], BF16))
                w1b = EB(nc.sbuf_tensor("w1b", [128, 16 * 128], BF16))
                w2b = EB(nc.sbuf_tensor("w2b", [128, 16 * 128], BF16))
                b1b = EB(nc.sbuf_tensor("b1b", [128, 16], F32))
                b2b = EB(nc.sbuf_tensor("b2b", [128, 1], F32))
                vb2b = EB(nc.sbuf_tensor("vb2b", [128, 128], BF16))
                posb = EB(nc.sbuf_tensor("posb", [128, 16], F32))
                reluT = EB(nc.sbuf_tensor("reluT", [128, 16 * RSTR], BF16))
                sqc = EB(nc.sbuf_tensor("sqc", [128, 512], BF16))
                nrow2 = EB(nc.sbuf_tensor("nrow2", [1, 512], F32))
                qrow = EB(nc.sbuf_tensor("qrow", [1, S], F32))
                qrow2 = EB(nc.sbuf_tensor("qrow2", [1, S], F32))
                qacc = EB(nc.sbuf_tensor("qacc", [1, S], F32))
                n2 = EB(nc.sbuf_tensor("n2", [1, 1], F32))
                ph = EB(nc.psum_tensor("ph", [128, 512], F32))
                pck = EB(nc.psum_tensor("pck", [128, 256], F32))
                pcv = EB(nc.psum_tensor("pcv", [128, 128], F32))
                pnn = EB(nc.psum_tensor("pnn", [1, 512], F32))
                pbb = EB(nc.psum_tensor("pbb", [128, 1], F32))
                for kind in range(2):
                    w1ap = kw1T if kind == 0 else vw1T
                    w2ap = kw2T if kind == 0 else vw2T
                    b1ap = kb1 if kind == 0 else vb1
                    posap = kposT if kind == 0 else vposT
                    cache = kT if kind == 0 else vT

                    def ld_small(e, w2ap=w2ap, b1ap=b1ap, posap=posap, kind=kind):
                        src = w2ap.rearrange("(u p) f -> p u f", p=128)
                        dst = w2b.ap().rearrange("p (u f) -> p u f", u=16)
                        e.dma_start(out=dst[:, :, :], in_=src[:, :, :]).then_inc(sem_dg, 16)
                        src = b1ap.rearrange("(u p) one -> p u one", p=128)
                        e.dma_start(out=b1b.ap()[:, :], in_=src[:, :, 0],
                                    allow_slow_non_contiguous=True).then_inc(sem_dg, 16)
                        e.dma_start(out=posb[:, :], in_=posap[:, :]).then_inc(sem_dg, 16)
                        if kind == 0:
                            return e.dma_start(out=b2b[:, :], in_=kb2[:, :])
                        return e.dma_start(out=vb2b[:, :], in_=vb2z[:, :])
                    grp("gpsimd", ld_small, dman=3)

                    def mk_in(e, cache=cache):
                        for p in range(16):
                            for b in range(Bq):
                                src = cache[b].ap().rearrange("d (c p) -> d c p", p=16)
                                e.tensor_scalar_add(
                                    ckin[:, p * 512 + b * 256: p * 512 + (b + 1) * 256],
                                    src[:, :, p], posb[:, p:p + 1])
                        return e.memset(reluT[:], 0.0)
                    grp("vector", mk_in)

                    for u in range(16):
                        def ld_w1(e, u=u, w1ap=w1ap):
                            src = w1ap.rearrange("(p pp) f -> pp p f", pp=128)
                            dst = w1b.ap().rearrange("pp (p f) -> pp p f", p=16)
                            return e.dma_start(out=dst[:, :, :],
                                               in_=src[:, :, u * 128:(u + 1) * 128])
                        grp("gpsimd", ld_w1)

                        def mm_h(e):
                            for p in range(16):
                                last = e.matmul(ph[:, :], w1b[:, p * 128:(p + 1) * 128],
                                                ckin[:, p * 512:(p + 1) * 512],
                                                start=(p == 0), stop=(p == 15),
                                                skip_group_check=True)
                            return last
                        grp("tensor", mm_h)

                        def ev_h(e, u=u):
                            return e.activation(reluT[:, u * RSTR + 1: u * RSTR + 513],
                                                ph[:, :], ACT.Relu,
                                                bias=b1b[:, u:u + 1], scale=1.0)
                        grp("scalar", ev_h)

                    if kind == 0:
                        for b in range(Bq):
                            def mm_ck(e, b=b):
                                for u in range(16):
                                    last = e.matmul(
                                        pck[:, :], w2b[:, u * 128:(u + 1) * 128],
                                        reluT[:, u * RSTR + 1 + b * 256:
                                              u * RSTR + 1 + (b + 1) * 256],
                                        start=(u == 0), stop=(u == 15),
                                        skip_group_check=True)
                                return last
                            grp("tensor", mm_ck)

                            def ev_ck(e, b=b):
                                e.tensor_copy(ckT[b][:, 1:257], pck[:, :])
                                return e.tensor_scalar_add(ckT[b][:, 1:257],
                                                           ckT[b][:, 1:257], b2b[:, :])
                            grp("vector", ev_ck)

                        def ld_mem(e):
                            e.dma_start(out=ckT[0][:, 0:1], in_=memk[:, :]).then_inc(sem_dg, 16)
                            return e.dma_start(out=ckT[1][:, 0:1], in_=memk[:, :])
                        grp("gpsimd", ld_mem, dman=1)
                    else:
                        for b in range(Bq):
                            for ct in range(3):
                                def mm_cv(e, b=b, ct=ct):
                                    for u in range(16):
                                        e.matmul(
                                            pcv[:, :],
                                            reluT[:, u * RSTR + b * 256 + ct * 128:
                                                  u * RSTR + b * 256 + ct * 128 + 128],
                                            w2b[:, u * 128:(u + 1) * 128],
                                            start=(u == 0), stop=False,
                                            skip_group_check=True)
                                    return e.matmul(pcv[:, :], e1tb[:, :], vb2b[:, :],
                                                    start=False, stop=True,
                                                    skip_group_check=True)
                                grp("tensor", mm_cv)

                                def ev_cv(e, b=b, ct=ct):
                                    return e.tensor_copy(
                                        cvn[b][:, ct * 129: ct * 129 + 128], pcv[:, :])
                                grp("vector", ev_cv)

                        def ld_memv(e):
                            e.dma_start(out=cvn[0][0:1, 0:128], in_=memv[:, :]).then_inc(sem_dg, 16)
                            return e.dma_start(out=cvn[1][0:1, 0:128], in_=memv[:, :])
                        grp("gpsimd", ld_memv, dman=1)

                        def fix_cv(e):
                            v0 = cvn[0].ap().rearrange("p (c t) -> p c t", c=3)
                            e.memset(v0[:, :, 128:129], 1.0)
                            v1 = cvn[1].ap().rearrange("p (c t) -> p c t", c=3)
                            return e.memset(v1[:, :, 128:129], 1.0)
                        grp("vector", fix_cv)

                def mk_vs(e):
                    for b in range(Bq):
                        sv = vnat[b].ap().rearrange("p (c t) -> p c t", c=32)
                        dv = vnatS[b].ap().rearrange("p (c t) -> p c t", c=32)
                        e.dma_start(out=dv[0:64, :, :], in_=sv[64:128, :, :]).then_inc(sem_ds, 16)
                        if b == 1:
                            return e.dma_start(out=dv[64:128, 0:31, :], in_=sv[0:64, 1:32, :])
                        e.dma_start(out=dv[64:128, 0:31, :], in_=sv[0:64, 1:32, :]).then_inc(sem_ds, 16)
                grp("sync", mk_vs, dman=3)

            stC = contextlib.ExitStack()
            with stC:
                EE = stC.enter_context
                bst32b = EE(nc.sbuf_tensor("bst32b", [32, 896], BF16))
                bst8b = EE(nc.sbuf_tensor("bst8b", [8, 896], BF16))
                e32b = EE(nc.sbuf_tensor("e32b", [32, 512], BF16))
                e8b = EE(nc.sbuf_tensor("e8b", [8, 128], BF16))
                mx4 = EE(nc.sbuf_tensor("mx4", [128, 1], F32))
                ebigtb = EE(nc.sbuf_tensor("ebigtb", [128, 2048], BF16))
                owntb = EE(nc.sbuf_tensor("owntb", [128, 128], BF16))
                sm0b = EE(nc.sbuf_tensor("sm0b", [128, 64], BF16))
                sm1b = EE(nc.sbuf_tensor("sm1b", [128, 64], BF16))
                qTb = [EE(nc.sbuf_tensor("qTbP%d" % p, [128, G * 512], BF16)) for p in range(2)]
                rqTb = [EE(nc.sbuf_tensor("rqTbP%d" % p, [128, G * 512], BF16)) for p in range(2)]
                qsT = [EE(nc.sbuf_tensor("qsTP%d" % p, [128, 512], BF16)) for p in range(2)]
                gnatb = [EE(nc.sbuf_tensor("gnatbP%d" % p, [128, 48], F32)) for p in range(2)]
                pimp = EE(nc.sbuf_tensor("pimp", [128, 4 * 384], F32))
                mx8 = EE(nc.sbuf_tensor("mx8", [128, 8], F32))
                zap = EE(nc.sbuf_tensor("zap", [128, 384], F32))
                selw = EE(nc.sbuf_tensor("selw", [128, 384], F32))
                selb = [EE(nc.sbuf_tensor("selbP%d" % p, [128, 384], BF16)) for p in range(4)]
                zrow = EE(nc.sbuf_tensor("zrow", [128, 1], F32))
                thr = EE(nc.sbuf_tensor("thr", [128, 1], F32))
                selBIP = [EE(nc.sbuf_tensor("selBIPP%d" % p, [128, 3 * 512], BF16)) for p in range(2)]
                pcT = EE(nc.sbuf_tensor("pcT", [128, 3 * 512], BF16))
                pfT = EE(nc.sbuf_tensor("pfT", [128, 4 * 512], BF16))
                psT = [EE(nc.sbuf_tensor("psTP%d" % p, [128, 512], BF16)) for p in range(2)]
                recb = EE(nc.sbuf_tensor("recb", [128, 1], F32))
                recb4 = EE(nc.sbuf_tensor("recb4", [128, 4], F32))
                couts = EE(nc.sbuf_tensor("couts", [128, G * 4 * 128], BF16))
                fouts = EE(nc.sbuf_tensor("fouts", [128, G * 4 * 128], BF16))
                souts = EE(nc.sbuf_tensor("souts", [128, G * 4 * 128], BF16))
                mbuf = EE(nc.sbuf_tensor("mbuf", [128, 512], BF16))
                mTb = [EE(nc.sbuf_tensor("mTbP%d" % p, [128, 512], BF16)) for p in range(2)]
                pool = [EE(nc.psum_tensor("pool%d" % s, [128, 512], F32))
                        for s in range(3)]
                pvt = [EE(nc.psum_tensor("pvt%d" % s, [128, 129], F32))
                       for s in range(4)]
                bPT = EE(nc.psum_tensor("bPT", [128, 512], BF16))

                def pv(it):
                    return pvt[it][:, :]

                def pvp(it, p0, p1):
                    return pvt[it][p0:p1, :]

                slot_rd = [None] * 3
                slot_ctr = [0]

                def take_slots(n):
                    assert n <= 3
                    out = []
                    for _ in range(n):
                        out.append(slot_ctr[0] % 3)
                        slot_ctr[0] += 1
                    return out

                def ld_bconsts(e):
                    e.dma_start(out=bst32b[:], in_=bst32_h[:, :]).then_inc(sem_dg, 16)
                    e.dma_start(out=bst8b[:], in_=bst8_h[:, :]).then_inc(sem_dg, 16)
                    e.dma_start(out=e32b[:], in_=e32_h[:, :]).then_inc(sem_dg, 16)
                    e.dma_start(out=e8b[:], in_=e8_h[:, :]).then_inc(sem_dg, 16)
                    e.dma_start(out=ebigtb[:], in_=ebigt_h[:, :]).then_inc(sem_dg, 16)
                    e.dma_start(out=owntb[:], in_=ownt_h[:, :]).then_inc(sem_dg, 16)
                    e.dma_start(out=sm0b[:], in_=sm0_h[:, :]).then_inc(sem_dg, 16)
                    return e.dma_start(out=sm1b[:], in_=sm1_h[:, :])
                grp("gpsimd", ld_bconsts, dman=7)

                T = {}
                for b in range(Bq):
                    for qc in range(8):
                        i = b * 8 + qc
                        P = i % 2
                        off = b * S + qc * 512
                        base = 32 * qc
                        nct = (32 * qc + 33 + 127) // 128
                        nkt = 4 * (qc + 1)
                        first = (i == 0)

                        def ld_q(e, off=off, P=P):
                            for g in range(G):
                                e.dma_start(out=qTb[P][:, g * 512:(g + 1) * 512],
                                            in_=qsc[g, :, off:off + 512]).then_inc(sem_ds, 16)
                                e.dma_start(out=rqTb[P][0:64, g * 512:(g + 1) * 512],
                                            in_=rqsc[g, 0, :, off:off + 512]).then_inc(sem_ds, 16)
                                if g == G - 1:
                                    return e.dma_start(
                                        out=rqTb[P][64:128, g * 512:(g + 1) * 512],
                                        in_=rqsc[g, 1, :, off:off + 512])
                                e.dma_start(out=rqTb[P][64:128, g * 512:(g + 1) * 512],
                                            in_=rqsc[g, 1, :, off:off + 512]).then_inc(sem_ds, 16)
                        T[('ldq', i)] = grp("sync", ld_q, dman=11,
                                            after=(None if first else
                                                   [T.get(('end', i - 2)),
                                                    T.get(('endv', i - 2))]))

                        def ld_gn(e, off=off, P=P):
                            for it in range(4):
                                s2 = gsc[:, off + it * 128: off + it * 128 + 128]
                                s2 = s2.rearrange("g p -> p g")
                                d2 = gnatb[P].ap()[:, it * 12:(it + 1) * 12]
                                if it == 3:
                                    return e.dma_start(out=d2, in_=s2,
                                                       allow_slow_non_contiguous=True)
                                e.dma_start(out=d2, in_=s2,
                                            allow_slow_non_contiguous=True
                                            ).then_inc(sem_dg, 16)
                        T[('gn', i)] = grp("gpsimd", ld_gn, dman=3,
                                           after=(None if first else
                                                  [T.get(('endv', i - 2))]))

                        def mk_qs(e, P=P):
                            e.tensor_add(qsT[P][:, :], qTb[P][:, 0:512], qTb[P][:, 512:1024])
                            e.tensor_add(qsT[P][:, :], qsT[P][:, :], qTb[P][:, 1024:1536])
                            return e.tensor_add(qsT[P][:, :], qsT[P][:, :], qTb[P][:, 1536:2048])
                        T[('mkqs', i)] = grp("vector", mk_qs,
                                             after=[T[('ldq', i)], T.get(('end', i - 2))])

                        for h in range(2):
                            s_imp = take_slots(2)

                            def mm_imp(e, base=base, b=b, P=P, h=h,
                                       s_imp=tuple(s_imp)):
                                for x in range(2):
                                    it = 2 * h + x
                                    ibase = base + 8 * it
                                    e.matmul(pool[s_imp[x]][:, 0:384], e8b[:, :],
                                             bst8b[:, 257 - ibase: 257 - ibase + 384],
                                             start=True, stop=False, skip_group_check=True)
                                    last = e.matmul(pool[s_imp[x]][:, 0:384],
                                                    qsT[P][:, it * 128:(it + 1) * 128],
                                                    ckT[b][:, 1:385],
                                                    start=False, stop=True,
                                                    skip_group_check=True)
                                return last
                            T[('mmimp', i, h)] = grp("tensor", mm_imp,
                                                     after=[T[('mkqs', i)]]
                                                     + [slot_rd[s] for s in s_imp])

                            def exp_imp(e, h=h, s_imp=tuple(s_imp)):
                                for x in range(2):
                                    it = 2 * h + x
                                    last = e.activation(pimp[:, it * 384:(it + 1) * 384],
                                                        pool[s_imp[x]][:, 0:384],
                                                        ACT.Exp, bias=0.0,
                                                        scale=SCALE / 4.0)
                                return last
                            T[('expimp', i, h)] = grp("scalar", exp_imp,
                                                      after=[T[('mmimp', i, h)]]
                                                      + [T.get(('sel', i - 1, it))
                                                         for it in range(4)])
                            for s in s_imp:
                                slot_rd[s] = T[('expimp', i, h)]

                        for it in range(4):
                            def sel(e, it=it):
                                pp = pimp[:, it * 384:(it + 1) * 384]
                                e.reduce_sum(zrow[:, :], pp, axis=AX.X)
                                vdr(e, e.max(out=mx8[:, :], in_=pp))
                                e.tensor_scalar_mul(thr[:, :], zrow[:, :], 1e-10)
                                vdr(e, e.tensor_copy(mx4[:, :], mx8[:, 3:4]))
                                e.tensor_scalar(selw[:, :], pp,
                                                mx4[:, :], None, AL.is_ge)
                                e.tensor_scalar(zap[:, :], pp,
                                                thr[:, :], None, AL.is_gt)
                                e.tensor_mul(selw[:, :], selw[:, :], zap[:, :])
                                return e.tensor_copy(selb[it][:, :], selw[:, :])
                            T[('sel', i, it)] = grp(
                                "vector", sel,
                                after=[T[('expimp', i, it // 2)],
                                       T.get(('trsel', i - 1, it))])

                        bpt_rd = [T.get(('evm', i - 1, 3))]

                        for g in range(G):
                            qg = qTb[P][:, g * 512:(g + 1) * 512]
                            s_c = take_slots(nct)
                            ctoks = reserve("scalar", nct)

                            def mm_c(e, g=g, qg=qg, b=b, base=base, nct=nct,
                                     s_c=tuple(s_c)):
                                for ct in range(nct):
                                    col = 256 + 128 * ct - base
                                    e.matmul(pool[s_c[ct]][:, :], bst32b[:, col:col + 128],
                                             e32b[:, :], start=True, stop=False,
                                             skip_group_check=True)
                                    last = e.matmul(pool[s_c[ct]][:, :],
                                                    ckT[b][:, ct * 128:(ct + 1) * 128],
                                                    qg, start=False, stop=True,
                                                    skip_group_check=True)
                                return last

                            def tr_sel(e, it=g):
                                for ct in range(3):
                                    last = e.transpose(bPT[:, ct * 128:(ct + 1) * 128],
                                                       selb[it][:, ct * 128:(ct + 1) * 128],
                                                       identb[:, :])
                                return last

                            def mm_cpv(e, b=b, nct=nct):
                                for it in range(4):
                                    for ct in range(nct):
                                        last = e.matmul(
                                            pv(it),
                                            pcT[:, ct * 512 + it * 128:
                                                ct * 512 + (it + 1) * 128],
                                            cvn[b][:, ct * 129:(ct + 1) * 129],
                                            start=(ct == 0), stop=(ct == nct - 1),
                                            skip_group_check=True)
                                return last
                            pv_war = (T.get(('divf', i - 1, 3)) if g == 0
                                      else T.get(('divc', i, g - 1)))
                            pcw = (T.get(('cpv', i, g - 1)) if g > 0
                                   else T.get(('cpv', i - 1, 3)))
                            tph = grpM("tensor", [
                                ([T[('ldq', i)]] + [slot_rd[s] for s in s_c], mm_c),
                                ([T[('sel', i, g)]] + bpt_rd, tr_sel),
                                ([ctoks[-1], pv_war], mm_cpv),
                            ])
                            T[('mmc', i, g)] = tph[0]
                            T[('trsel', i, g)] = tph[1]
                            T[('cpv', i, g)] = tph[2]

                            def exp_c(e, nct=nct, s_c=tuple(s_c)):
                                for ct in range(nct):
                                    last = e.activation(pcT[:, ct * 512:(ct + 1) * 512],
                                                        pool[s_c[ct]][:, :], ACT.Exp,
                                                        bias=0.0, scale=SCALE)
                                return last
                            emit_reserved("scalar",
                                          [([T[('mmc', i, g)], pcw], exp_c)]
                                          if nct == 1 else
                                          [([T[('mmc', i, g)], pcw],
                                            (lambda e, ct=ct, s=s_c[ct]:
                                             e.activation(pcT[:, ct * 512:(ct + 1) * 512],
                                                          pool[s][:, :], ACT.Exp,
                                                          bias=0.0, scale=SCALE)))
                                           for ct in range(nct)])
                            for x, s in enumerate(s_c):
                                slot_rd[s] = ctoks[x]

                            def ev_sel(e, it=g, P=P):
                                for ct in range(3):
                                    e.tensor_copy(
                                        selBIP[P][:, ct * 512 + it * 128:
                                                  ct * 512 + (it + 1) * 128],
                                        bPT[:, ct * 128:(ct + 1) * 128])
                                    last = e.tensor_scalar_add(
                                        selBIP[P][:, ct * 512 + it * 128:
                                                  ct * 512 + (it + 1) * 128],
                                        selBIP[P][:, ct * 512 + it * 128:
                                                  ct * 512 + (it + 1) * 128], -1.0)
                                return last
                            T[('evsel', i, g)] = grp("vector", ev_sel,
                                                     after=[T[('trsel', i, g)],
                                                            T.get(('end', i - 2))])
                            bpt_rd = [T[('evsel', i, g)]]

                            def div_c(e, g=g):
                                for it in range(4):
                                    o = (g * 4 + it) * 128
                                    vdr(e, e.reciprocal(recb[:, :], pv(it)[:, 128:129]))
                                    e.tensor_copy(couts[:, o:o + 128], pv(it)[:, 0:128])
                                    last = e.tensor_scalar_mul(
                                        couts[:, o:o + 128], couts[:, o:o + 128],
                                        recb[:, :])
                                return last
                            T[('divc', i, g)] = grp("vector", div_c,
                                                    after=[T[('cpv', i, g)]])

                        for g in range(G):
                            rqg = rqTb[P][:, g * 512:(g + 1) * 512]
                            s_s = take_slots(1)[0]
                            sftoks = reserve("scalar", 1 + nkt)
                            vt2 = reserve("vector", 2)
                            vtok_divs = vt2[1]
                            s_f = [take_slots(1)[0] for _ in range(nkt)]
                            mm_s_war = slot_rd[s_s]

                            def mm_s(e, rqg=rqg, qc=qc, b=b, s_s=s_s):
                                sl = pool[s_s]
                                for wl in range(8):
                                    w = 8 * qc + wl
                                    ocol = wl * 64
                                    if w == 0:
                                        e.matmul(sl[:, ocol:ocol + 64], identb[:, :],
                                                 sm0b[:, :], start=True, stop=False,
                                                 skip_group_check=True)
                                        last = e.matmul(sl[:, ocol:ocol + 64],
                                                        rkT[b][:, 0:128],
                                                        rqg[:, 0:64],
                                                        start=False, stop=True,
                                                        skip_group_check=True)
                                    else:
                                        e.matmul(sl[:, ocol:ocol + 64], identb[:, :],
                                                 sm1b[:, :], start=True, stop=False,
                                                 skip_group_check=True)
                                        last = e.matmul(sl[:, ocol:ocol + 64],
                                                        rkT[b][:, 64 * (w - 1): 64 * (w + 1)],
                                                        rqg[:, wl * 64: wl * 64 + 64],
                                                        start=False, stop=True,
                                                        skip_group_check=True)
                                return last

                            def mk_mmf(kt, sl):
                                def mm_f(e, kt=kt, sl=sl, rqg=rqg, qc=qc, b=b, P=P):
                                    slp = pool[sl]
                                    ct = kt // 16
                                    jj = kt % 16
                                    e.matmul(slp[:, :], ebigtb[:, jj * 128:(jj + 1) * 128],
                                             selBIP[P][:, ct * 512:(ct + 1) * 512],
                                             start=True, stop=False,
                                             skip_group_check=True)
                                    it2 = kt - 4 * qc
                                    if 0 <= it2 < 4:
                                        e.matmul(slp[:, it2 * 128:(it2 + 1) * 128],
                                                 identb[:, :], owntb[:, :],
                                                 start=False, stop=False,
                                                 skip_group_check=True)
                                    return e.matmul(slp[:, :],
                                                    rkT[b][:, kt * 128:(kt + 1) * 128],
                                                    rqg, start=False, stop=True,
                                                    skip_group_check=True)
                                return mm_f

                            def mk_fpv(kt, x):
                                def mm_fpv(e, kt=kt, x=x, qc=qc, b=b):
                                    for it2 in range(max(0, kt - 4 * qc), 4):
                                        last = e.matmul(
                                            pv(it2),
                                            pfT[:, x * 512 + it2 * 128:
                                                x * 512 + (it2 + 1) * 128],
                                            vnat[b][:, kt * 129: kt * 129 + 129],
                                            start=(kt == 0),
                                            stop=(kt == 4 * qc + it2),
                                            skip_group_check=True)
                                    return last
                                return mm_fpv

                            evsel_all = [T[('evsel', i, gg)] for gg in range(4)]
                            sv_war = (T.get(('divc', i, 3)) if g == 0
                                      else T.get(('divf', i, g - 1)))
                            fpv_phase = {}
                            t_phases = [
                                ([T[('ldq', i)], mm_s_war], mm_s),
                                (evsel_all + [slot_rd[s_f[0]]], mk_mmf(0, s_f[0])),
                                ([sftoks[0], sv_war], None),
                            ]

                            def mm_spv(e, qc=qc, b=b, g=g):
                                for wl in range(8):
                                    w = 8 * qc + wl
                                    it2 = wl // 2
                                    half = wl % 2
                                    if w == 0:
                                        rv = vnat[b][:, 0:129]
                                    elif w % 2 == 1:
                                        kt2 = (w - 1) // 2
                                        rv = vnat[b][:, kt2 * 129: kt2 * 129 + 129]
                                    else:
                                        kt2 = (w - 2) // 2
                                        rv = vnatS[b][:, kt2 * 129: kt2 * 129 + 129]
                                    last = e.matmul(
                                        pvp(it2, 64 * half, 64 * half + 64),
                                        psT[g % 2][:, wl * 64: wl * 64 + 64], rv,
                                        start=True, stop=True,
                                        tile_position=(0, 64 * half),
                                        skip_group_check=True)
                                return last
                            t_phases[2] = ([sftoks[0], sv_war], mm_spv)
                            fpv_idx = {}
                            for kt in range(1, nkt):
                                t_phases.append(
                                    ([slot_rd[s_f[kt]]], mk_mmf(kt, s_f[kt])))
                                fw = [sftoks[kt], vtok_divs] if kt == 1 else [sftoks[kt]]
                                pftw = []
                                t_phases.append((fw + pftw, mk_fpv(kt - 1, (kt - 1) % 4)))
                                fpv_idx[kt - 1] = len(t_phases) - 1
                            t_phases.append(([sftoks[nkt]], mk_fpv(nkt - 1, (nkt - 1) % 4)))
                            fpv_idx[nkt - 1] = len(t_phases) - 1
                            tph = grpM("tensor", t_phases)
                            T[('mms', i, g)] = tph[0]
                            T[('spv', i, g)] = tph[2]
                            for kt, pi_ in fpv_idx.items():
                                T[('fpv', i, g, kt)] = tph[pi_]
                            mmf_tok = {0: tph[1]}
                            x = 3
                            for kt in range(1, nkt):
                                mmf_tok[kt] = tph[x]
                                x += 2

                            sc_phases = [([T[('mms', i, g)],
                                           (T.get(('spv', i, g - 2)) if g >= 2
                                            else T.get(('spv', i - 1, g + 2)))],
                                          (lambda e, s=s_s, g=g:
                                           e.activation(psT[g % 2][:, :], pool[s][:, :],
                                                        ACT.Exp, bias=0.0, scale=SCALE)))]
                            for kt in range(nkt):
                                pft_war = (T[('fpv', i, g, kt - 4)]
                                           if kt >= 4 else
                                           (T.get(('fpv', i, g - 1, nkt - 4 + kt))
                                            if g > 0 else None))
                                sc_phases.append(
                                    ([mmf_tok[kt], pft_war],
                                     (lambda e, s=s_f[kt], x=kt % 4:
                                      e.activation(pfT[:, x * 512:(x + 1) * 512],
                                                   pool[s][:, :], ACT.Exp,
                                                   bias=0.0, scale=SCALE))))
                            emit_reserved("scalar", sc_phases)
                            slot_rd[s_s] = sftoks[0]
                            for kt in range(nkt):
                                slot_rd[s_f[kt]] = sftoks[1 + kt]

                            def div_s_a(e):
                                for it in range(4):
                                    last = e.reciprocal(recb4[:, it:it + 1],
                                                        pv(it)[:, 128:129])
                                return last

                            def div_s_b(e, g=g):
                                for it in range(4):
                                    o = (g * 4 + it) * 128
                                    e.tensor_copy(souts[:, o:o + 128], pv(it)[:, 0:128])
                                    last = e.tensor_scalar_mul(
                                        souts[:, o:o + 128], souts[:, o:o + 128],
                                        recb4[:, it:it + 1])
                                return last
                            emit_reserved("vector", [([T[('spv', i, g)]], div_s_a),
                                                     ([vt2[0]], div_s_b)])
                            T[('divs', i, g)] = vtok_divs

                            def div_f(e, g=g):
                                for it in range(4):
                                    o = (g * 4 + it) * 128
                                    vdr(e, e.reciprocal(recb[:, :], pv(it)[:, 128:129]))
                                    e.tensor_copy(fouts[:, o:o + 128], pv(it)[:, 0:128])
                                    last = e.tensor_scalar_mul(
                                        fouts[:, o:o + 128], fouts[:, o:o + 128],
                                        recb[:, :])
                                return last
                            T[('divf', i, g)] = grp("vector", div_f,
                                                    after=[T[('fpv', i, g, nkt - 1)]])


                        for it in range(4):
                            def mrg(e, it=it, P=P):
                                for g in range(G):
                                    o = (g * 4 + it) * 128
                                    gn = gnatb[P]
                                    e.tensor_scalar_mul(mbuf[:, g * 128:(g + 1) * 128],
                                                        couts[:, o:o + 128],
                                                        gn[:, it * 12 + 3 * g:
                                                           it * 12 + 3 * g + 1])
                                    e.scalar_tensor_tensor(
                                        mbuf[:, g * 128:(g + 1) * 128],
                                        fouts[:, o:o + 128],
                                        gn[:, it * 12 + 3 * g + 1: it * 12 + 3 * g + 2],
                                        mbuf[:, g * 128:(g + 1) * 128], AL.mult, AL.add)
                                    last = e.scalar_tensor_tensor(
                                        mbuf[:, g * 128:(g + 1) * 128],
                                        souts[:, o:o + 128],
                                        gn[:, it * 12 + 3 * g + 2: it * 12 + 3 * g + 3],
                                        mbuf[:, g * 128:(g + 1) * 128], AL.mult, AL.add)
                                return last
                            T[('mrg', i, it)] = grp("vector", mrg,
                                                    after=[T[('gn', i)],
                                                           T[('divf', i, 3)],
                                                           T.get(('trm', i, it - 1))])

                            def tr_m(e, it=it):
                                for j in range(4):
                                    last = e.transpose(bPT[:, j * 128:(j + 1) * 128],
                                                       mbuf[:, j * 128:(j + 1) * 128],
                                                       identb[:, :])
                                return last
                            T[('trm', i, it)] = grp("tensor", tr_m,
                                                    after=[T[('mrg', i, it)]] + bpt_rd)

                            def ev_m(e, it=it):
                                return e.tensor_copy(mTb[it % 2][:, :], bPT[:, :])
                            T[('evm', i, it)] = grp("vector", ev_m,
                                                    after=[T[('trm', i, it)],
                                                           T.get(('stm', i, it - 2))])
                            bpt_rd = [T[('evm', i, it)]]

                            def st_m(e, b=b, qc=qc, it=it):
                                tglob = b * S + qc * 512 + it * 128
                                owner = tglob // TPC
                                tcol = tglob % TPC
                                dst = a2a_in[owner * 512:(owner + 1) * 512,
                                             tcol:tcol + 128].rearrange(
                                                 "(j p) t -> p j t", p=128)
                                src = mTb[it % 2].ap().rearrange("p (j t) -> p j t", j=4)
                                return e.dma_start(out=dst[:, :, :], in_=src[:, :, :])
                            T[('stm', i, it)] = grp("sync", st_m,
                                                    after=[T[('evm', i, it)]])

                        T[('end', i)] = T[('trm', i, 3)]
                        T[('endv', i)] = T[('evm', i, 3)]

                        if DEBUG and (b, qc) in getattr(build_nc, "_dbg_bq", ()):
                            def dbg_dump(e, b=b, qc=qc, P=P):
                                key = "bq_%d_%d" % (b, qc)
                                e.dma_start(out=dbg[key + "_c"][:, :], in_=couts[:, :]).then_inc(sem_dg, 16)
                                e.dma_start(out=dbg[key + "_f"][:, :], in_=fouts[:, :]).then_inc(sem_dg, 16)
                                e.dma_start(out=dbg[key + "_s"][:, :], in_=souts[:, :]).then_inc(sem_dg, 16)
                                return e.dma_start(out=dbg[key + "_sel"][:, :],
                                                   in_=selBIP[P][:, 0:1024])
                            grp("gpsimd", dbg_dump, dman=3,
                                after=[T[('divf', i, 3)], T[('evsel', i, 3)]])

        def coll(e):
            with allow_ici_all_to_all():
                return e.collective_compute(
                    "AllToAll", AL.bypass,
                    replica_groups=[list(range(NCORE))],
                    ins=[a2a_in_h.ap().opt()],
                    outs=[a2a_out_h.ap().opt()])
        coll_grp(coll)

        xTown = dram("xTown", (D, TPC))
        stD = contextlib.ExitStack()
        with stD:
            ED = stD.enter_context
            bigA = ED(nc.sbuf_tensor("bigA", [128, 32 * TPC], BF16))
            bigB = ED(nc.sbuf_tensor("bigB", [128, 32 * TPC], BF16))
            wstr = [ED(nc.sbuf_tensor("wstr%d" % p, [128, 32 * 128], BF16)) for p in range(2)]
            xcb = [ED(nc.sbuf_tensor("xcb%d" % p, [128, TPC], BF16)) for p in range(2)]
            sqC = ED(nc.sbuf_tensor("sqC", [128, 2 * TPC], BF16))
            stat = ED(nc.sbuf_tensor("stat", [1, 4 * TPC], F32))
            nmB = ED(nc.sbuf_tensor("nmB", [128, TPC], F32))
            rsB = ED(nc.sbuf_tensor("rsB", [128, TPC], F32))
            laTb = ED(nc.sbuf_tensor("laTb", [128, 32 * 128], BF16))
            lbTb = ED(nc.sbuf_tensor("lbTb", [128, D], BF16))
            bfbb = ED(nc.sbuf_tensor("bfbb", [128, 32], F32))
            coefb = ED(nc.sbuf_tensor("coefb", [128, 1], F32))
            hTb = ED(nc.sbuf_tensor("hTb", [128, TPC], BF16))
            youtb = [ED(nc.sbuf_tensor("youtb%d" % p, [128, TPC], F32)) for p in range(2)]
            zrowC = youtb[0].ap()[:, 0:512]
            cPA = [ED(nc.psum_tensor("cPA%d" % p, [128, 512], F32)) for p in range(2)]
            cSM0 = ED(nc.psum_tensor("cSM0", [1, 512], F32))
            cSM1 = ED(nc.psum_tensor("cSM1", [1, 512], F32))
            cSS0 = ED(nc.psum_tensor("cSS0", [1, 512], F32))
            cSS1 = ED(nc.psum_tensor("cSS1", [1, 512], F32))
            cBC = ED(nc.psum_tensor("cBC", [128, 512], F32))
            cSM = [cSM0, cSM1]
            cSS = [cSS0, cSS1]

            def ld_c(e):
                src = a2a_out.rearrange("(kt p) t -> p kt t", p=128)
                dst = bigA.ap().rearrange("p (kt t) -> p kt t", kt=32)
                return e.dma_start(out=dst[:, 0:16, :], in_=src[:, 0:16, :])
            grp("sync", ld_c)

            def ld_cb(e):
                src = a2a_out.rearrange("(kt p) t -> p kt t", p=128)
                dst = bigA.ap().rearrange("p (kt t) -> p kt t", kt=32)
                return e.dma_start(out=dst[:, 16:32, :], in_=src[:, 16:32, :])
            grp("gpsimd", ld_cb)

            def ld_c2(e):
                src = laT.rearrange("(kt p) r -> p kt r", p=128)
                dst = laTb.ap().rearrange("p (kt r) -> p kt r", kt=32)
                e.dma_start(out=dst[:, :, :], in_=src[:, :, :]).then_inc(sem_dg, 16)
                e.dma_start(out=lbTb[:], in_=lbT[:, :]).then_inc(sem_dg, 16)
                e.dma_start(out=bfbb[:], in_=bfb_h[:, :]).then_inc(sem_dg, 16)
                return e.dma_start(out=coefb[:], in_=lw128[:, :])
            grp("gpsimd", ld_c2, dman=3)

            def mk_coef(e):
                return e.tensor_scalar_mul(coefb[:, :], coefb[:, :], 2.0)
            grp("vector", mk_coef)

            if DEBUG and "bigA0" in dbg:
                def dump_bigA(e):
                    return e.dma_start(out=dbg["bigA0"][:, :], in_=bigA[:, :])
                grp("gpsimd", dump_bigA)

            for phase in range(2):
                if DEBUG and phase == 1 and "x1d" in dbg:
                    def dump_x1(e):
                        e.dma_start(out=dbg["x1d"][:, :], in_=bigA[:, :]).then_inc(sem_dg, 16)
                        e.dma_start(out=dbg["htbd"][:, :], in_=hTb[:, :]).then_inc(sem_dg, 16)
                        return e.dma_start(out=dbg["statd"][:, :], in_=stat[:, :])
                    grp("gpsimd", dump_x1, dman=2)
                wsrc = woT if phase == 0 else wfT
                c_w = {}; c_x = {}; c_mm = {}; c_ev = {}; c_st = {}

                def emit_ldw(ft, first, wsrc=wsrc):
                    p = ft % 2

                    def ld_wt(e, ft=ft, wsrc=wsrc, p=p):
                        src = wsrc.rearrange("(kt p) f -> p kt f", p=128)
                        dst = wstr[p].ap().rearrange("p (kt f) -> p kt f", kt=32)
                        return e.dma_start(out=dst[:, :, :],
                                           in_=src[:, :, ft * 128:(ft + 1) * 128])
                    c_w[ft] = grp("gpsimd", ld_wt,
                                  after=(None if first else [c_mm.get((ft - 2, 1))]))
                    if phase == 0:
                        def ld_xc(e, ft=ft, p=p):
                            return e.dma_start(out=xcb[p][:, :],
                                               in_=xTown[ft * 128:(ft + 1) * 128, :])
                        c_x[ft] = grp("gpsimd", ld_xc,
                                      after=(None if first else
                                             [c_ev.get((ft - 2, 0)), c_ev.get((ft - 2, 1))]))

                def emit_mmst(ft):
                    def mm_st(e, ft=ft):
                        sq0 = (ft % 2) * TPC
                        for tc2 in range(2):
                            e.matmul(cSM[tc2][:, :], ones128b[:, :],
                                     bigB[:, ft * TPC + tc2 * 512: ft * TPC + tc2 * 512 + 512],
                                     start=(ft == 0), stop=(ft == 31),
                                     skip_group_check=True)
                            last = e.matmul(cSS[tc2][:, :], ones128b[:, :],
                                            sqC[:, sq0 + tc2 * 512: sq0 + (tc2 + 1) * 512],
                                            start=(ft == 0), stop=(ft == 31),
                                            skip_group_check=True)
                        return last
                    c_st[ft] = grp("tensor", mm_st,
                                   after=[c_ev[(ft, 0)], c_ev[(ft, 1)]])

                emit_ldw(0, True)
                emit_ldw(1, False)
                for ft in range(32):
                    p = ft % 2
                    for tc2 in range(TPC // 512):
                        def mm_o(e, ft=ft, tc2=tc2, phase=phase, p=p):
                            for kt in range(32):
                                last = e.matmul(
                                    cPA[tc2][:, :], wstr[p][:, kt * 128:(kt + 1) * 128],
                                    bigA[:, kt * TPC + tc2 * 512: kt * TPC + tc2 * 512 + 512],
                                    start=(kt == 0),
                                    stop=(kt == 31 and phase == 0),
                                    skip_group_check=True)
                            if phase == 1:
                                last = e.matmul(
                                    cPA[tc2][:, :], lbTb[:, ft * 128:(ft + 1) * 128],
                                    hTb[:, tc2 * 512:(tc2 + 1) * 512],
                                    start=False, stop=True, skip_group_check=True)
                            return last
                        aft = [c_w[ft], c_ev.get((ft - 1, tc2))]
                        if ft == 0 and tc2 == 0:
                            aft = None
                        c_mm[(ft, tc2)] = grp("tensor", mm_o, after=aft)
                        if tc2 == 1 and ft >= 1:
                            emit_mmst(ft - 1)

                        def ev_o(e, ft=ft, tc2=tc2, phase=phase, p=p):
                            dsttile = bigB[:, ft * TPC + tc2 * 512: ft * TPC + tc2 * 512 + 512]
                            if phase == 0:
                                e.tensor_add(dsttile, cPA[tc2][:, :],
                                             xcb[p][:, tc2 * 512:(tc2 + 1) * 512])
                            else:
                                e.tensor_scalar_add(zrowC[:, :], cPA[tc2][:, :],
                                                    bfbb[:, ft:ft + 1])
                                e.tensor_add(dsttile, zrowC[:, :],
                                             bigA[:, ft * TPC + tc2 * 512:
                                                  ft * TPC + tc2 * 512 + 512])
                            return e.tensor_mul(
                                sqC[:, (ft % 2) * TPC + tc2 * 512:
                                    (ft % 2) * TPC + (tc2 + 1) * 512],
                                dsttile, dsttile)
                        evaft = [c_mm[(ft, tc2)], c_st.get(ft - 2)]
                        if phase == 0:
                            evaft.append(c_x[ft])
                        c_ev[(ft, tc2)] = grp("vector", ev_o, after=evaft)
                    if ft + 2 < 32:
                        emit_ldw(ft + 2, False)
                emit_mmst(31)

                if DEBUG and phase == 0 and "bigB0" in dbg:
                    def dump_bigB(e):
                        return e.dma_start(out=dbg["bigB0"][:, :], in_=bigB[:, :])
                    grp("gpsimd", dump_bigB)
                if DEBUG and phase == 1 and "res2d" in dbg:
                    def dump_res2(e):
                        return e.dma_start(out=dbg["res2d"][:, :], in_=bigB[:, :])
                    grp("gpsimd", dump_res2)

                def ev_st(e):
                    for tc2 in range(2):
                        e.tensor_copy(stat[:, tc2 * 512:(tc2 + 1) * 512], cSM[tc2][:, :])
                        e.tensor_copy(stat[:, TPC + tc2 * 512: TPC + (tc2 + 1) * 512],
                                      cSS[tc2][:, :])
                    e.tensor_scalar_mul(stat[:, 2 * TPC:3 * TPC], stat[:, 0:TPC],
                                        1.0 / D)
                    e.tensor_scalar_mul(stat[:, TPC:2 * TPC], stat[:, TPC:2 * TPC],
                                        1.0 / D)
                    e.tensor_mul(stat[:, 3 * TPC:4 * TPC], stat[:, 2 * TPC:3 * TPC],
                                 stat[:, 2 * TPC:3 * TPC])
                    e.tensor_sub(stat[:, TPC:2 * TPC], stat[:, TPC:2 * TPC],
                                 stat[:, 3 * TPC:4 * TPC])
                    return e.tensor_scalar_add(stat[:, TPC:2 * TPC],
                                               stat[:, TPC:2 * TPC], EPS)
                tst1 = grp("vector", ev_st)

                def st_sqrt(e):
                    return e.activation(stat[:, 3 * TPC:4 * TPC], stat[:, TPC:2 * TPC],
                                        ACT.Sqrt, bias=0.0, scale=1.0)
                tst2 = grp("scalar", st_sqrt, after=[tst1])

                def st_fin(e):
                    e.reciprocal(stat[:, TPC:2 * TPC], stat[:, 3 * TPC:4 * TPC])
                    return e.tensor_scalar_mul(stat[:, 2 * TPC:3 * TPC],
                                               stat[:, 2 * TPC:3 * TPC], -1.0)
                tst3 = grp("vector", st_fin, after=[tst2])

                tprev_mm = None
                tprev_ev = None
                tbcs = []
                for which in range(2):
                    dstb = nmB if which == 0 else rsB
                    srcoff = 2 * TPC if which == 0 else TPC
                    for tc2 in range(2):
                        def bc_prep(e, srcoff=srcoff, tc2=tc2):
                            e.memset(zrowC[:], 0.0)
                            return e.tensor_copy(
                                zrowC[0:1, :],
                                stat[:, srcoff + tc2 * 512: srcoff + (tc2 + 1) * 512])
                        tp = grp("vector", bc_prep, after=[tst3, tprev_mm])

                        def bc_mm(e):
                            return e.matmul(cBC[:, :], e1tf[:, :], zrowC[:, :],
                                            start=True, stop=True, skip_group_check=True)
                        tprev_mm = grp("tensor", bc_mm, after=[tp, tprev_ev])

                        def bc_ev(e, dstb=dstb, tc2=tc2):
                            return e.tensor_copy(dstb[:, tc2 * 512:(tc2 + 1) * 512],
                                                 cBC[:, :])
                        tprev_ev = grp("vector", bc_ev, after=[tprev_mm])
                        tbcs.append(tprev_ev)

                if phase == 0:
                    for ft in range(32):
                        def nrm(e, ft=ft):
                            sl = slice(ft * TPC, (ft + 1) * TPC)
                            e.tensor_add(bigA[:, sl], bigB[:, sl], nmB[:, :])
                            return e.tensor_mul(bigA[:, sl], bigA[:, sl], rsB[:, :])
                        grp("vector" if ft % 2 == 0 else "gpsimd", nrm,
                            after=list(tbcs))

                    tmh = None
                    for tc2 in range(2):
                        def mm_h2(e, tc2=tc2):
                            for kt in range(32):
                                last = e.matmul(
                                    cPA[tc2][:, :], laTb[:, kt * 128:(kt + 1) * 128],
                                    bigA[:, kt * TPC + tc2 * 512: kt * TPC + tc2 * 512 + 512],
                                    start=(kt == 0), stop=(kt == 31),
                                    skip_group_check=True)
                            return last
                        tmh = grp("tensor", mm_h2, after=(None if tc2 == 0 else [tmh]))

                        def ev_h2(e, tc2=tc2):
                            e.tensor_copy(hTb[:, tc2 * 512:(tc2 + 1) * 512], cPA[tc2][:, :])
                            return e.tensor_scalar_mul(
                                hTb[:, tc2 * 512:(tc2 + 1) * 512],
                                hTb[:, tc2 * 512:(tc2 + 1) * 512], coefb[:, :])
                        grp("vector", ev_h2, after=[tmh])
                else:
                    tnrm = {}; tsty = {}
                    for ft in range(32):
                        def nrm2(e, ft=ft):
                            sl = slice(ft * TPC, (ft + 1) * TPC)
                            yb = youtb[ft % 2]
                            e.tensor_add(yb[:, :], bigB[:, sl], nmB[:, :])
                            return e.tensor_mul(yb[:, :], yb[:, :], rsB[:, :])
                        tnrm[ft] = grp("vector" if ft % 2 == 0 else "gpsimd", nrm2,
                                       after=list(tbcs) + [tsty.get(ft - 2)])

                        def st_y(e, ft=ft):
                            return e.dma_start(out=y[ft * 128:(ft + 1) * 128, :],
                                               in_=youtb[ft % 2][:, :])
                        tsty[ft] = grp("sync", st_y, after=[tnrm[ft]])

            def fin(e):
                return e.nop()
            grp("sync", fin)

    return nc


def kernel(**inputs):
    shared = _host_prep_shared(inputs)
    consts_keys = ["kw1T", "vw1T", "kb1", "vb1", "kw2T", "vw2T", "kb2", "vb2z",
                   "woT", "wfT", "bfb", "laT", "lbT", "lw128", "identc", "ones128",
                   "e1t", "bst32", "bst8", "e32", "e8", "e8ones", "ebigt", "ownt",
                   "smaskt0", "smaskt1", "cosT", "sinT", "xT"]
    in_maps = []
    for c in range(NCORE):
        d = _host_prep_core(inputs, c)
        for k in consts_keys:
            d[k] = shared[k]
        d["xTown"] = np.ascontiguousarray(shared["xT"][:, c * TPC:(c + 1) * TPC])
        in_maps.append(d)
    nc = build_nc()
    res = run_bass_kernel_spmd(nc, in_maps, core_ids=list(range(NCORE)))
    out = np.concatenate(
        [np.ascontiguousarray(res.results[c]["yT"].T) for c in range(NCORE)], axis=0)
    return out.reshape(Bq, S, D)


# version 12 (final)
# speedup vs baseline: 2.7209x; 1.0731x over previous
import contextlib
import numpy as np
import ml_dtypes
import concourse.bass as bass
import concourse.mybir as mybir
from concourse.bass_utils import run_bass_kernel_spmd
from concourse.aot_env import allow_ici_all_to_all

F32 = mybir.dt.float32
BF16 = mybir.dt.bfloat16
AL = mybir.AluOpType
AX = mybir.AxisListType
ACT = mybir.ActivationFunctionType

NCORE = 8
H, KVH, DH, G = 32, 8, 128, 4
Bq, S, D = 2, 4096, 4096
CB = 16
WIN = 64
SCALE = float(DH) ** -0.5
TCH = 256
NT = Bq * S
TPC = NT // NCORE
EPS = 1e-5
BIG = 1e30
RSTR = 648

DEBUG = False
FORCE_BARRIER = False
A2A_OUT = False
_DBG_SHAPES = {}


def _host_prep_core(inp, core):
    P = np.concatenate([np.arange(0, DH, 2), np.arange(1, DH, 2)])
    d = {}
    wqkv = inp["L0_attn_w_qkv"]
    qrows = np.concatenate(
        [wqkv[(4 * core + g) * DH + P] for g in range(G)], axis=0)
    krows = wqkv[H * DH + core * DH + P]
    vrows = wqkv[H * DH + KVH * DH + core * DH: H * DH + KVH * DH + (core + 1) * DH]
    d["wqkvT"] = np.ascontiguousarray(
        np.concatenate([qrows, krows, vrows], axis=0).T.astype(np.float32))
    wg = inp["L0_attn_w_gate"]
    grows = np.stack([wg[3 * (4 * core + g) + j] for g in range(G) for j in range(3)])
    d["wgT"] = np.ascontiguousarray(grows.T.astype(np.float32))
    bgv = inp["L0_attn_b_gate"]
    d["bg"] = np.ascontiguousarray(
        np.array([bgv[3 * (4 * core + g) + j] for g in range(G) for j in range(3)],
                 np.float32).reshape(12, 1))
    d["kposT"] = np.ascontiguousarray(inp["L0_attn_k_pos"][core][:, P].T.astype(np.float32))
    d["vposT"] = np.ascontiguousarray(inp["L0_attn_v_pos"][core].T.astype(np.float32))
    d["memk"] = np.ascontiguousarray(
        inp["L0_attn_mem_kv"][0, core, 0][P].reshape(128, 1).astype(np.float32))
    d["memv"] = np.ascontiguousarray(
        inp["L0_attn_mem_kv"][1, core, 0].reshape(1, 128).astype(np.float32))
    return d


def _host_prep_shared(inp):
    P = np.concatenate([np.arange(0, DH, 2), np.arange(1, DH, 2)])
    d = {}
    X = np.ascontiguousarray(inp["L0_attn_hidden_states"].reshape(NT, D))
    d["xT"] = np.ascontiguousarray(X.T)
    perm2048 = (np.arange(CB)[:, None] * DH + P[None, :]).reshape(-1)
    d["kw1T"] = np.ascontiguousarray(inp["L0_attn_k_w1"].T[perm2048].astype(np.float32))
    d["vw1T"] = np.ascontiguousarray(inp["L0_attn_v_w1"].T.astype(np.float32))
    d["kb1"] = np.ascontiguousarray(inp["L0_attn_k_b1"].reshape(2048, 1).astype(np.float32))
    d["vb1"] = np.ascontiguousarray(inp["L0_attn_v_b1"].reshape(2048, 1).astype(np.float32))
    d["kw2T"] = np.ascontiguousarray(inp["L0_attn_k_w2"].T[:, P].astype(np.float32))
    d["vw2T"] = np.ascontiguousarray(inp["L0_attn_v_w2"].T.astype(np.float32))
    d["kb2"] = np.ascontiguousarray(inp["L0_attn_k_b2"][P].reshape(128, 1).astype(np.float32))
    vb2z = np.zeros((128, 128), np.float32)
    vb2z[0, :] = inp["L0_attn_v_b2"]
    d["vb2z"] = vb2z
    d["woT"] = np.ascontiguousarray(inp["L0_attn_w_o"].T.astype(np.float32))
    d["wfT"] = np.ascontiguousarray(inp["L0_ffn_W"].T.astype(np.float32))
    d["bfb"] = np.ascontiguousarray(
        inp["L0_ffn_bias"].reshape(32, 128).T.astype(np.float32))
    d["laT"] = np.ascontiguousarray(
        inp["L0_ffn_lora_A"].reshape(128, D).T.astype(np.float32))
    d["lbT"] = np.ascontiguousarray(
        inp["L0_ffn_lora_B"].transpose(0, 2, 1).reshape(128, D).astype(np.float32))
    d["lw128"] = np.ascontiguousarray(
        np.repeat(np.asarray(inp["L0_ffn_lora_weights"]), 16).reshape(128, 1).astype(np.float32))
    d["identc"] = np.eye(128, dtype=np.float32)
    d["ones128"] = np.ones((128, 1), np.float32)
    e1t = np.zeros((128, 128), np.float32)
    e1t[0, :] = 1.0
    d["e1t"] = e1t
    xs = np.arange(896) - 256
    bst32 = np.zeros((32, 896), np.float32)
    for r in range(32):
        bst32[r, xs >= r + 1] = -BIG
    d["bst32"] = bst32
    bst8 = np.zeros((8, 896), np.float32)
    for r in range(8):
        bst8[r, xs >= r + 1] = -BIG
    d["bst8"] = bst8
    e32 = np.zeros((32, 512), np.float32)
    for q in range(512):
        e32[q // 16, q] = 1.0
    d["e32"] = e32
    e8 = np.zeros((8, 128), np.float32)
    for q in range(128):
        e8[q // 16, q] = 1.0
    d["e8"] = e8
    e8o = np.zeros((8, 1024), np.float32)
    for r in range(8):
        e8o[r, r * 128:(r + 1) * 128] = 1.0
    d["e8ones"] = e8o
    ebig = np.zeros((128, 16 * 128), np.float32)
    for j in range(16):
        for m in range(128):
            ebig[8 * j + m // 16, j * 128 + m] = BIG
    d["ebigt"] = ebig
    ownt = np.zeros((128, 128), np.float32)
    for k in range(128):
        for q in range(128):
            if k // 16 == q // 16 and k % 16 <= q % 16:
                ownt[k, q] = BIG
    d["ownt"] = ownt
    sm0 = np.full((128, 64), -BIG, np.float32)
    sm1 = np.full((128, 64), -BIG, np.float32)
    for k in range(128):
        for q in range(64):
            if k <= q:
                sm0[k, q] = 0.0
            if q <= k <= q + 64:
                sm1[k, q] = 0.0
    d["smaskt0"] = sm0
    d["smaskt1"] = sm1
    inv = (1.0 / 10000.0 ** (np.arange(0, DH, 2, dtype=np.float64) / DH))
    ang = inv[:, None] * np.arange(S, dtype=np.float64)[None, :]
    d["cosT"] = np.cos(ang).astype(np.float32)
    d["sinT"] = np.sin(ang).astype(np.float32)
    return d


def build_nc():
    nc = bass.Bass("TRN2", target_bir_lowering=False)

    def dram(name, shape, dt=F32, kind="ExternalInput"):
        return nc.dram_tensor(name, list(shape), dt, kind=kind).ap()

    xT = dram("xT", (D, NT))
    wqkvT = dram("wqkvT", (D, 768))
    wgT = dram("wgT", (D, 12))
    bg = dram("bg", (12, 1))
    kw1T = dram("kw1T", (2048, 2048))
    vw1T = dram("vw1T", (2048, 2048))
    kb1 = dram("kb1", (2048, 1))
    vb1 = dram("vb1", (2048, 1))
    kw2T = dram("kw2T", (2048, 128))
    vw2T = dram("vw2T", (2048, 128))
    kb2 = dram("kb2", (128, 1))
    vb2z = dram("vb2z", (128, 128))
    kposT = dram("kposT", (128, 16))
    vposT = dram("vposT", (128, 16))
    memk = dram("memk", (128, 1))
    memv = dram("memv", (1, 128))
    woT = dram("woT", (D, D))
    wfT = dram("wfT", (D, D))
    bfb_h = dram("bfb", (128, 32))
    laT = dram("laT", (D, 128))
    lbT = dram("lbT", (128, D))
    lw128 = dram("lw128", (128, 1))
    identc = dram("identc", (128, 128))
    ones128 = dram("ones128", (128, 1))
    e1t_h = dram("e1t", (128, 128))
    bst32_h = dram("bst32", (32, 896))
    bst8_h = dram("bst8", (8, 896))
    e32_h = dram("e32", (32, 512))
    e8_h = dram("e8", (8, 128))
    e8ones_h = dram("e8ones", (8, 1024))
    ebigt_h = dram("ebigt", (128, 2048))
    ownt_h = dram("ownt", (128, 128))
    sm0_h = dram("smaskt0", (128, 64))
    sm1_h = dram("smaskt1", (128, 64))
    cosT = dram("cosT", (64, S))
    sinT = dram("sinT", (64, S))

    y = dram("yT", (D, TPC), F32, "ExternalOutput")

    qsc = dram("qsc", (G, 128, NT), BF16, None) if False else \
        nc.dram_tensor("qsc", [G, 128, NT], BF16).ap()
    rqsc = nc.dram_tensor("rqsc", [G, 2, 64, NT], BF16).ap()
    gsc = nc.dram_tensor("gsc", [12, NT], BF16).ap()
    bfsc = nc.dram_tensor("bfsc", [G, NT], F32).ap()
    qnsc = nc.dram_tensor("qnsc", [G, NT], F32).ap()
    bcsc = nc.dram_tensor("bcsc", [G, NT], F32).ap()
    bisc = nc.dram_tensor("bisc", [1, NT], F32).ap()
    a2a_in_h = nc.dram_tensor("a2a_in", [D, TPC], BF16)
    a2a_out_h = nc.dram_tensor("a2a_out", [D, TPC], BF16)
    a2a_in = a2a_in_h.ap()
    a2a_out = a2a_out_h.ap()

    dbg = {}
    if DEBUG:
        for nm, shp in _DBG_SHAPES.items():
            dbg[nm] = dram("dbg_" + nm, shp, F32, "ExternalOutput")

    cnt = dict(ds=0, dg=0, ts=0, vs=0, ss=0, cs=0)

    st0 = contextlib.ExitStack()
    with st0:
        EC = st0.enter_context
        identb = EC(nc.sbuf_tensor("identb", [128, 128], BF16))
        ones128b = EC(nc.sbuf_tensor("ones128b", [128, 1], BF16))
        e1tb = EC(nc.sbuf_tensor("e1tb", [128, 128], BF16))
        e1tf = EC(nc.sbuf_tensor("e1tf", [128, 128], F32))
        mck2 = EC(nc.sbuf_tensor("mck2", [1, 1], F32))
        mckr = EC(nc.sbuf_tensor("mckr", [1, 1], F32))
        scal1 = EC(nc.sbuf_tensor("scal1", [1, 8], F32))
        sem_ds = EC(nc.semaphore("sem_ds"))
        sem_dg = EC(nc.semaphore("sem_dg"))
        sem_ts = EC(nc.semaphore("sem_ts"))
        sem_vs = EC(nc.semaphore("sem_vs"))
        sem_ss = EC(nc.semaphore("sem_ss"))
        sem_cs = EC(nc.semaphore("sem_cs"))
        block = EC(nc.Block())
        SEM = dict(ds=sem_ds, dg=sem_dg, ts=sem_ts, vs=sem_vs, ss=sem_ss, cs=sem_cs)
        OWNSEM = dict(sync="ds", gpsimd="dg", tensor="ts", vector="vs", scalar="ss")

        def vdr(e, inst):
            inst.then_inc(SEM["vs"], 1)
            cnt["vs"] += 1
            e.wait_ge(SEM["vs"], cnt["vs"])

        def grp(engname, fn, dman=0, after=None):
            own = OWNSEM[engname]
            n = 16 if engname in ("sync", "gpsimd") else 1
            if FORCE_BARRIER:
                after = None
            if after is None:
                waits = [(s, cnt[s]) for s in ("ds", "dg", "ts", "vs", "ss", "cs")
                         if cnt[s] > 0]
            else:
                bys = {}
                for t in after:
                    if t is None:
                        continue
                    s, vv = t
                    if vv > bys.get(s, -1):
                        bys[s] = vv
                waits = [(s, vv) for s, vv in sorted(bys.items()) if vv > 0]

            def body(e):
                for s, vv in waits:
                    e.wait_ge(SEM[s], vv)
                last = fn(e)
                last.then_inc(SEM[own], n)
            getattr(block, engname)(body)
            cnt[own] += n + dman * 16
            return (own, cnt[own])

        def reserve(engname, k):
            own = OWNSEM[engname]
            toks = [(own, cnt[own] + j + 1) for j in range(k)]
            cnt[own] += k
            return toks

        def _mergew(after):
            bys = {}
            for t in after:
                if t is None:
                    continue
                s, vv = t
                if vv > bys.get(s, -1):
                    bys[s] = vv
            return sorted(bys.items())

        def emit_reserved(engname, phases):
            own = OWNSEM[engname]

            def body(e):
                for after, fn in phases:
                    for s, vv in _mergew(after):
                        e.wait_ge(SEM[s], vv)
                    fn(e).then_inc(SEM[own], 1)
            getattr(block, engname)(body)

        def grpM(engname, phases):
            own = OWNSEM[engname]
            toks = []

            def body(e):
                for after, fn in phases:
                    for s, vv in _mergew(after):
                        e.wait_ge(SEM[s], vv)
                    fn(e).then_inc(SEM[own], 1)
            getattr(block, engname)(body)
            for _ in phases:
                cnt[own] += 1
                toks.append((own, cnt[own]))
            return toks

        def coll_grp(fn):
            waits = [(s, cnt[s]) for s in ("ds", "dg", "ts", "vs", "ss")
                     if cnt[s] > 0]

            def body(e):
                for s, vv in waits:
                    e.wait_ge(SEM[s], vv)
                fn(e).then_inc(SEM["cs"], 1)
            block.gpsimd(body)
            cnt["cs"] += 1

        st1 = contextlib.ExitStack()
        with st1:
            E1 = st1.enter_context
            kT0 = E1(nc.sbuf_tensor("kT0", [128, S], BF16))
            kT1 = E1(nc.sbuf_tensor("kT1", [128, S], BF16))
            rkT0 = E1(nc.sbuf_tensor("rkT0", [128, S], BF16))
            rkT1 = E1(nc.sbuf_tensor("rkT1", [128, S], BF16))
            vT0 = E1(nc.sbuf_tensor("vT0", [128, S], BF16))
            vT1 = E1(nc.sbuf_tensor("vT1", [128, S], BF16))
            vnat0 = E1(nc.sbuf_tensor("vnat0", [128, 32 * 129], BF16))
            vnat1 = E1(nc.sbuf_tensor("vnat1", [128, 32 * 129], BF16))
            vnatS0 = E1(nc.sbuf_tensor("vnatS0", [128, 32 * 129], BF16))
            vnatS1 = E1(nc.sbuf_tensor("vnatS1", [128, 32 * 129], BF16))
            ckT0 = E1(nc.sbuf_tensor("ckT0", [128, 385], BF16))
            ckT1 = E1(nc.sbuf_tensor("ckT1", [128, 385], BF16))
            cv0 = E1(nc.sbuf_tensor("cv0", [128, 3 * 129], BF16))
            cv1 = E1(nc.sbuf_tensor("cv1", [128, 3 * 129], BF16))
            kT = [kT0, kT1]
            rkT = [rkT0, rkT1]
            vT = [vT0, vT1]
            vnat = [vnat0, vnat1]
            vnatS = [vnatS0, vnatS1]
            ckT = [ckT0, ckT1]
            cvn = [cv0, cv1]

            def ld_consts(e):
                e.dma_start(out=identb[:], in_=identc[:, :]).then_inc(sem_dg, 16)
                e.dma_start(out=ones128b[:], in_=ones128[:, :]).then_inc(sem_dg, 16)
                e.dma_start(out=e1tb[:], in_=e1t_h[:, :]).then_inc(sem_dg, 16)
                return e.dma_start(out=e1tf[:], in_=e1t_h[:, :])
            grp("gpsimd", ld_consts, dman=3)

            def init_misc(e):
                e.memset(mck2[:], 0.0)
                for b in range(Bq):
                    e.memset(vnat[b][:], 1.0)
                    e.memset(vnatS[b][:], 1.0)
                    e.memset(ckT[b][:], 0.0)
                    e.memset(cvn[b][:], 0.0)
                return e.memset(scal1[:], 0.0)
            grp("vector", init_misc)

            stA = contextlib.ExitStack()
            with stA:
                EA = stA.enter_context
                wqvb = EA(nc.sbuf_tensor("wqvb", [128, 32 * 768], BF16))
                wgb = EA(nc.sbuf_tensor("wgb", [128, 32 * 12], BF16))
                bgb = EA(nc.sbuf_tensor("bgb", [12, 1], F32))
                cosb = [EA(nc.sbuf_tensor("cosb%d" % p, [64, TCH], BF16)) for p in range(2)]
                sinb = [EA(nc.sbuf_tensor("sinb%d" % p, [64, TCH], BF16)) for p in range(2)]
                xbuf = [EA(nc.sbuf_tensor("xbuf%d" % p, [128, 32 * TCH], BF16)) for p in range(2)]
                qbuf = [EA(nc.sbuf_tensor("qbuf%d" % p, [128, G * TCH], BF16)) for p in range(2)]
                hiA = [EA(nc.sbuf_tensor("hiA%d" % p, [64, 5 * TCH], BF16)) for p in range(2)]
                rlo = [EA(nc.sbuf_tensor("rlo%d" % p, [64, 5 * TCH], BF16)) for p in range(2)]
                rhi = [EA(nc.sbuf_tensor("rhi%d" % p, [64, 5 * TCH], BF16)) for p in range(2)]
                rt1 = EA(nc.sbuf_tensor("rt1", [64, TCH], F32))
                rt2 = EA(nc.sbuf_tensor("rt2", [64, TCH], F32))
                gbuf = [EA(nc.sbuf_tensor("gbuf%d" % p, [12, TCH], BF16)) for p in range(2)]
                pq01 = [EA(nc.psum_tensor("pq01_%d" % p, [128, 512], F32)) for p in range(2)]
                pq23 = [EA(nc.psum_tensor("pq23_%d" % p, [128, 512], F32)) for p in range(2)]
                pkv = [EA(nc.psum_tensor("pkv%d" % p, [128, 512], F32)) for p in range(2)]
                pmx = EA(nc.psum_tensor("pmx", [128, TCH], F32))
                ptb = EA(nc.psum_tensor("ptb", [128, TCH], BF16))

                def ld_w(e):
                    src = wqkvT.rearrange("(kd p) f -> p kd f", p=128)
                    dst = wqvb.ap().rearrange("p (kd f) -> p kd f", kd=32)
                    e.dma_start(out=dst[:, :, :], in_=src[:, :, :]).then_inc(sem_dg, 16)
                    src = wgT.rearrange("(kd p) f -> p kd f", p=128)
                    dst = wgb.ap().rearrange("p (kd f) -> p kd f", kd=32)
                    e.dma_start(out=dst[:, :, :], in_=src[:, :, :]).then_inc(sem_dg, 16)
                    return e.dma_start(out=bgb[:], in_=bg[:, :])
                grp("gpsimd", ld_w, dman=2)

                NC1 = NT // TCH
                t_ldg = {}; t_lds = {}; t_mm = {}; t_ev1 = {}; t_evg = {}
                t_dsh = {}; t_rp = {}; t_srq = {}; t_trv = {}; t_evv = {}
                t_stg = {}

                def emit_trv(ci):
                    b, tc = ci // (S // TCH), ci % (S // TCH)
                    t0 = tc * TCH
                    p = ci % 2

                    def tr_v(e, b=b, t0=t0, p=p):
                        for j in range(TCH // 128):
                            last = e.transpose(
                                ptb[:, j * 128:(j + 1) * 128],
                                vT[b][:, t0 + j * 128: t0 + (j + 1) * 128],
                                identb[:, :])
                        return last
                    t_trv[ci] = grp("tensor", tr_v,
                                    after=[t_ev1[ci], t_evv.get(ci - 1)])

                    def ev_v(e, b=b, t0=t0, p=p):
                        for j in range(TCH // 128):
                            kt2 = t0 // 128 + j
                            last = e.tensor_copy(
                                vnat[b][:, kt2 * 129: kt2 * 129 + 128],
                                ptb[:, j * 128:(j + 1) * 128])
                        return last
                    t_evv[ci] = grp("vector", ev_v, after=[t_trv[ci]])

                for ci in range(NC1):
                    b, tc = ci // (S // TCH), ci % (S // TCH)
                    t0 = tc * TCH
                    gt0 = b * S + t0
                    p = ci % 2

                    def ld_x_g(e, gt0=gt0, t0=t0, p=p):
                        src = xT.rearrange("(kd p) t -> p kd t", p=128)
                        dst = xbuf[p].ap().rearrange("p (kd t) -> p kd t", kd=32)
                        e.dma_start(out=dst[:, :, :],
                                    in_=src[:, :, gt0:gt0 + TCH]).then_inc(sem_dg, 16)
                        e.dma_start(out=cosb[p][:], in_=cosT[:, t0:t0 + TCH]).then_inc(sem_dg, 16)
                        return e.dma_start(out=sinb[p][:], in_=sinT[:, t0:t0 + TCH])
                    t_ldg[ci] = grp("gpsimd", ld_x_g, dman=2,
                                    after=[t_mm.get(ci - 2), t_rp.get(ci - 2)])

                    def mm_qkv(e, p=p):
                        tgt = [(pq01[p], 0), (pq01[p], 256), (pq23[p], 0),
                               (pq23[p], 256), (pkv[p], 0), (pkv[p], 256)]
                        for m in range(6):
                            dstp, c0 = tgt[m]
                            for kd in range(32):
                                e.matmul(dstp[:, c0:c0 + TCH],
                                         wqvb[:, kd * 768 + m * 128: kd * 768 + (m + 1) * 128],
                                         xbuf[p][:, kd * TCH:(kd + 1) * TCH],
                                         start=(kd == 0), stop=(kd == 31),
                                         skip_group_check=True)
                        for kd in range(32):
                            last = e.matmul(pmx[0:12, 0:TCH],
                                            wgb[:, kd * 12:(kd + 1) * 12],
                                            xbuf[p][:, kd * TCH:(kd + 1) * TCH],
                                            start=(kd == 0), stop=(kd == 31),
                                            skip_group_check=True)
                        return last
                    t_mm[ci] = grp("tensor", mm_qkv,
                                   after=[t_ldg[ci], t_ev1.get(ci - 2),
                                          t_evg.get(ci - 1)])

                    def ev1(e, b=b, t0=t0, p=p):
                        for g in range(G):
                            src = [pq01[p], pq01[p], pq23[p], pq23[p]][g]
                            c0 = [0, 256, 0, 256][g]
                            e.tensor_copy(qbuf[p][:, g * TCH:(g + 1) * TCH],
                                          src[:, c0:c0 + TCH])
                        e.tensor_copy(kT[b][:, t0:t0 + TCH], pkv[p][:, 0:TCH])
                        return e.tensor_copy(vT[b][:, t0:t0 + TCH], pkv[p][:, 256:256 + TCH])
                    t_ev1[ci] = grp("vector", ev1,
                                    after=[t_mm[ci], t_dsh.get(ci - 2)])

                    def ev_g(e, gt0=gt0, p=p):
                        return e.activation(gbuf[p][:, :], pmx[0:12, 0:TCH], ACT.Sigmoid,
                                            bias=bgb[:, :], scale=1.0)
                    t_evg[ci] = grp("scalar", ev_g,
                                    after=[t_mm[ci], t_stg.get(ci - 2)])

                    def st_g(e, gt0=gt0, p=p):
                        return e.dma_start(out=gsc[:, gt0:gt0 + TCH],
                                           in_=gbuf[p][:, :])
                    t_stg[ci] = grp("sync", st_g, after=[t_evg[ci]])

                    def dshift(e, b=b, t0=t0, gt0=gt0, p=p):
                        for g in range(G):
                            e.dma_start(out=hiA[p][:, g * TCH:(g + 1) * TCH],
                                        in_=qbuf[p][64:128, g * TCH:(g + 1) * TCH]
                                        ).then_inc(sem_ds, 16)
                            e.dma_start(out=qsc[g, :, gt0:gt0 + TCH],
                                        in_=qbuf[p][:, g * TCH:(g + 1) * TCH]
                                        ).then_inc(sem_ds, 16)
                        return e.dma_start(out=hiA[p][:, 4 * TCH:5 * TCH],
                                           in_=kT[b][64:128, t0:t0 + TCH])
                    t_dsh[ci] = grp("sync", dshift, dman=8,
                                    after=[t_ev1[ci], t_rp.get(ci - 2)])

                    def rope(e, b=b, t0=t0, p=p):
                        for u in range(5):
                            lo = (qbuf[p][0:64, u * TCH:(u + 1) * TCH] if u < 4
                                  else kT[b][0:64, t0:t0 + TCH])
                            hi = hiA[p][:, u * TCH:(u + 1) * TCH]
                            e.tensor_mul(rt1[:, :], lo, cosb[p][:, :])
                            e.tensor_mul(rt2[:, :], hi, sinb[p][:, :])
                            e.tensor_sub(rlo[p][:, u * TCH:(u + 1) * TCH], rt1[:, :], rt2[:, :])
                            e.tensor_mul(rt1[:, :], hi, cosb[p][:, :])
                            e.tensor_mul(rt2[:, :], lo, sinb[p][:, :])
                            e.tensor_add(rhi[p][:, u * TCH:(u + 1) * TCH], rt1[:, :], rt2[:, :])
                        return e.tensor_copy(rkT[b][0:64, t0:t0 + TCH],
                                             rlo[p][:, 4 * TCH:5 * TCH])
                    t_rp[ci] = grp("vector", rope,
                                   after=[t_dsh[ci], t_ldg[ci], t_srq.get(ci - 2)])

                    def st_rq(e, b=b, t0=t0, gt0=gt0, p=p):
                        for g in range(G):
                            e.dma_start(out=rqsc[g, 0, :, gt0:gt0 + TCH],
                                        in_=rlo[p][:, g * TCH:(g + 1) * TCH]).then_inc(sem_ds, 16)
                            e.dma_start(out=rqsc[g, 1, :, gt0:gt0 + TCH],
                                        in_=rhi[p][:, g * TCH:(g + 1) * TCH]).then_inc(sem_ds, 16)
                        return e.dma_start(out=rkT[b][64:128, t0:t0 + TCH],
                                           in_=rhi[p][:, 4 * TCH:5 * TCH])
                    t_srq[ci] = grp("sync", st_rq, dman=8, after=[t_rp[ci]])

                    if ci >= 1:
                        emit_trv(ci - 1)
                emit_trv(NC1 - 1)

            stB = contextlib.ExitStack()
            with stB:
                EB = stB.enter_context
                ckin = EB(nc.sbuf_tensor("ckin", [128, 16 * 512], BF16))
                w1b = EB(nc.sbuf_tensor("w1b", [128, 16 * 128], BF16))
                w2b = EB(nc.sbuf_tensor("w2b", [128, 16 * 128], BF16))
                b1b = EB(nc.sbuf_tensor("b1b", [128, 16], F32))
                b2b = EB(nc.sbuf_tensor("b2b", [128, 1], F32))
                vb2b = EB(nc.sbuf_tensor("vb2b", [128, 128], BF16))
                posb = EB(nc.sbuf_tensor("posb", [128, 16], F32))
                reluT = EB(nc.sbuf_tensor("reluT", [128, 16 * RSTR], BF16))
                sqc = EB(nc.sbuf_tensor("sqc", [128, 512], BF16))
                nrow2 = EB(nc.sbuf_tensor("nrow2", [1, 512], F32))
                qrow = EB(nc.sbuf_tensor("qrow", [1, S], F32))
                qrow2 = EB(nc.sbuf_tensor("qrow2", [1, S], F32))
                qacc = EB(nc.sbuf_tensor("qacc", [1, S], F32))
                n2 = EB(nc.sbuf_tensor("n2", [1, 1], F32))
                ph = EB(nc.psum_tensor("ph", [128, 512], F32))
                pck = EB(nc.psum_tensor("pck", [128, 256], F32))
                pcv = EB(nc.psum_tensor("pcv", [128, 128], F32))
                pnn = EB(nc.psum_tensor("pnn", [1, 512], F32))
                pbb = EB(nc.psum_tensor("pbb", [128, 1], F32))
                for kind in range(2):
                    w1ap = kw1T if kind == 0 else vw1T
                    w2ap = kw2T if kind == 0 else vw2T
                    b1ap = kb1 if kind == 0 else vb1
                    posap = kposT if kind == 0 else vposT
                    cache = kT if kind == 0 else vT

                    def ld_small(e, w2ap=w2ap, b1ap=b1ap, posap=posap, kind=kind):
                        src = w2ap.rearrange("(u p) f -> p u f", p=128)
                        dst = w2b.ap().rearrange("p (u f) -> p u f", u=16)
                        e.dma_start(out=dst[:, :, :], in_=src[:, :, :]).then_inc(sem_dg, 16)
                        src = b1ap.rearrange("(u p) one -> p u one", p=128)
                        e.dma_start(out=b1b.ap()[:, :], in_=src[:, :, 0],
                                    allow_slow_non_contiguous=True).then_inc(sem_dg, 16)
                        e.dma_start(out=posb[:, :], in_=posap[:, :]).then_inc(sem_dg, 16)
                        if kind == 0:
                            return e.dma_start(out=b2b[:, :], in_=kb2[:, :])
                        return e.dma_start(out=vb2b[:, :], in_=vb2z[:, :])
                    grp("gpsimd", ld_small, dman=3)

                    def mk_in(e, cache=cache):
                        for p in range(16):
                            for b in range(Bq):
                                src = cache[b].ap().rearrange("d (c p) -> d c p", p=16)
                                e.tensor_scalar_add(
                                    ckin[:, p * 512 + b * 256: p * 512 + (b + 1) * 256],
                                    src[:, :, p], posb[:, p:p + 1])
                        return e.memset(reluT[:], 0.0)
                    grp("vector", mk_in)

                    for u in range(16):
                        def ld_w1(e, u=u, w1ap=w1ap):
                            src = w1ap.rearrange("(p pp) f -> pp p f", pp=128)
                            dst = w1b.ap().rearrange("pp (p f) -> pp p f", p=16)
                            return e.dma_start(out=dst[:, :, :],
                                               in_=src[:, :, u * 128:(u + 1) * 128])
                        grp("gpsimd", ld_w1)

                        def mm_h(e):
                            for p in range(16):
                                last = e.matmul(ph[:, :], w1b[:, p * 128:(p + 1) * 128],
                                                ckin[:, p * 512:(p + 1) * 512],
                                                start=(p == 0), stop=(p == 15),
                                                skip_group_check=True)
                            return last
                        grp("tensor", mm_h)

                        def ev_h(e, u=u):
                            return e.activation(reluT[:, u * RSTR + 1: u * RSTR + 513],
                                                ph[:, :], ACT.Relu,
                                                bias=b1b[:, u:u + 1], scale=1.0)
                        grp("scalar", ev_h)

                    if kind == 0:
                        for b in range(Bq):
                            def mm_ck(e, b=b):
                                for u in range(16):
                                    last = e.matmul(
                                        pck[:, :], w2b[:, u * 128:(u + 1) * 128],
                                        reluT[:, u * RSTR + 1 + b * 256:
                                              u * RSTR + 1 + (b + 1) * 256],
                                        start=(u == 0), stop=(u == 15),
                                        skip_group_check=True)
                                return last
                            grp("tensor", mm_ck)

                            def ev_ck(e, b=b):
                                e.tensor_copy(ckT[b][:, 1:257], pck[:, :])
                                return e.tensor_scalar_add(ckT[b][:, 1:257],
                                                           ckT[b][:, 1:257], b2b[:, :])
                            grp("vector", ev_ck)

                        def ld_mem(e):
                            e.dma_start(out=ckT[0][:, 0:1], in_=memk[:, :]).then_inc(sem_dg, 16)
                            return e.dma_start(out=ckT[1][:, 0:1], in_=memk[:, :])
                        grp("gpsimd", ld_mem, dman=1)
                    else:
                        for b in range(Bq):
                            for ct in range(3):
                                def mm_cv(e, b=b, ct=ct):
                                    for u in range(16):
                                        e.matmul(
                                            pcv[:, :],
                                            reluT[:, u * RSTR + b * 256 + ct * 128:
                                                  u * RSTR + b * 256 + ct * 128 + 128],
                                            w2b[:, u * 128:(u + 1) * 128],
                                            start=(u == 0), stop=False,
                                            skip_group_check=True)
                                    return e.matmul(pcv[:, :], e1tb[:, :], vb2b[:, :],
                                                    start=False, stop=True,
                                                    skip_group_check=True)
                                grp("tensor", mm_cv)

                                def ev_cv(e, b=b, ct=ct):
                                    return e.tensor_copy(
                                        cvn[b][:, ct * 129: ct * 129 + 128], pcv[:, :])
                                grp("vector", ev_cv)

                        def ld_memv(e):
                            e.dma_start(out=cvn[0][0:1, 0:128], in_=memv[:, :]).then_inc(sem_dg, 16)
                            return e.dma_start(out=cvn[1][0:1, 0:128], in_=memv[:, :])
                        grp("gpsimd", ld_memv, dman=1)

                        def fix_cv(e):
                            v0 = cvn[0].ap().rearrange("p (c t) -> p c t", c=3)
                            e.memset(v0[:, :, 128:129], 1.0)
                            v1 = cvn[1].ap().rearrange("p (c t) -> p c t", c=3)
                            return e.memset(v1[:, :, 128:129], 1.0)
                        grp("vector", fix_cv)

                def mk_vs(e):
                    for b in range(Bq):
                        sv = vnat[b].ap().rearrange("p (c t) -> p c t", c=32)
                        dv = vnatS[b].ap().rearrange("p (c t) -> p c t", c=32)
                        e.dma_start(out=dv[0:64, :, :], in_=sv[64:128, :, :]).then_inc(sem_ds, 16)
                        if b == 1:
                            return e.dma_start(out=dv[64:128, 0:31, :], in_=sv[0:64, 1:32, :])
                        e.dma_start(out=dv[64:128, 0:31, :], in_=sv[0:64, 1:32, :]).then_inc(sem_ds, 16)
                grp("sync", mk_vs, dman=3)

            stC = contextlib.ExitStack()
            with stC:
                EE = stC.enter_context
                bst32b = EE(nc.sbuf_tensor("bst32b", [32, 896], BF16))
                bst8b = EE(nc.sbuf_tensor("bst8b", [8, 896], BF16))
                e32b = EE(nc.sbuf_tensor("e32b", [32, 512], BF16))
                e8b = EE(nc.sbuf_tensor("e8b", [8, 128], BF16))
                mx4 = EE(nc.sbuf_tensor("mx4", [128, 1], F32))
                ebigtb = EE(nc.sbuf_tensor("ebigtb", [128, 2048], BF16))
                owntb = EE(nc.sbuf_tensor("owntb", [128, 128], BF16))
                sm0b = EE(nc.sbuf_tensor("sm0b", [128, 64], BF16))
                sm1b = EE(nc.sbuf_tensor("sm1b", [128, 64], BF16))
                qTb = [EE(nc.sbuf_tensor("qTbP%d" % p, [128, G * 512], BF16)) for p in range(3)]
                rqTb = [EE(nc.sbuf_tensor("rqTbP%d" % p, [128, G * 512], BF16)) for p in range(3)]
                qsT = [EE(nc.sbuf_tensor("qsTP%d" % p, [128, 512], BF16)) for p in range(2)]
                gnatb = [EE(nc.sbuf_tensor("gnatbP%d" % p, [128, 48], F32)) for p in range(2)]
                pimp = EE(nc.sbuf_tensor("pimp", [128, 4 * 384], F32))
                mx8 = EE(nc.sbuf_tensor("mx8", [128, 8], F32))
                zap = EE(nc.sbuf_tensor("zap", [128, 384], F32))
                selw = EE(nc.sbuf_tensor("selw", [128, 384], F32))
                selb = [EE(nc.sbuf_tensor("selbP%d" % p, [128, 384], BF16)) for p in range(4)]
                zrow = EE(nc.sbuf_tensor("zrow", [128, 1], F32))
                thr = EE(nc.sbuf_tensor("thr", [128, 1], F32))
                selBIP = [EE(nc.sbuf_tensor("selBIPP%d" % p, [128, 3 * 512], BF16)) for p in range(2)]
                pcT = EE(nc.sbuf_tensor("pcT", [128, 3 * 512], BF16))
                pfT = EE(nc.sbuf_tensor("pfT", [128, 4 * 512], BF16))
                psT = [EE(nc.sbuf_tensor("psTP%d" % p, [128, 512], BF16)) for p in range(2)]
                recb = EE(nc.sbuf_tensor("recb", [128, 1], F32))
                recb4 = EE(nc.sbuf_tensor("recb4", [128, 4], F32))
                couts = EE(nc.sbuf_tensor("couts", [128, G * 4 * 128], BF16))
                fouts = EE(nc.sbuf_tensor("fouts", [128, G * 4 * 128], BF16))
                souts = EE(nc.sbuf_tensor("souts", [128, G * 4 * 128], BF16))
                mbuf = EE(nc.sbuf_tensor("mbuf", [128, 512], BF16))
                mTb = [EE(nc.sbuf_tensor("mTbP%d" % p, [128, 512], BF16)) for p in range(2)]
                pool = [EE(nc.psum_tensor("pool%d" % s, [128, 512], F32))
                        for s in range(3)]
                pvt = [EE(nc.psum_tensor("pvt%d" % s, [128, 129], F32))
                       for s in range(4)]
                bPT = EE(nc.psum_tensor("bPT", [128, 512], BF16))

                def pv(it):
                    return pvt[it][:, :]

                def pvp(it, p0, p1):
                    return pvt[it][p0:p1, :]

                slot_rd = [None] * 3
                slot_ctr = [0]

                def take_slots(n):
                    assert n <= 3
                    out = []
                    for _ in range(n):
                        out.append(slot_ctr[0] % 3)
                        slot_ctr[0] += 1
                    return out

                def ld_bconsts(e):
                    e.dma_start(out=bst32b[:], in_=bst32_h[:, :]).then_inc(sem_dg, 16)
                    e.dma_start(out=bst8b[:], in_=bst8_h[:, :]).then_inc(sem_dg, 16)
                    e.dma_start(out=e32b[:], in_=e32_h[:, :]).then_inc(sem_dg, 16)
                    e.dma_start(out=e8b[:], in_=e8_h[:, :]).then_inc(sem_dg, 16)
                    e.dma_start(out=ebigtb[:], in_=ebigt_h[:, :]).then_inc(sem_dg, 16)
                    e.dma_start(out=owntb[:], in_=ownt_h[:, :]).then_inc(sem_dg, 16)
                    e.dma_start(out=sm0b[:], in_=sm0_h[:, :]).then_inc(sem_dg, 16)
                    return e.dma_start(out=sm1b[:], in_=sm1_h[:, :])
                grp("gpsimd", ld_bconsts, dman=7)

                T = {}
                for b in range(Bq):
                    for qc in range(8):
                        i = b * 8 + qc
                        P = i % 2
                        P3 = i % 3
                        off = b * S + qc * 512
                        base = 32 * qc
                        nct = (32 * qc + 33 + 127) // 128
                        nkt = 4 * (qc + 1)
                        first = (i == 0)

                        def mk_ldq(ii):
                            offq = (ii // 8) * S + (ii % 8) * 512
                            Pq = ii % 3

                            def ld_q(e, offq=offq, Pq=Pq):
                                for g in range(G):
                                    e.dma_start(out=qTb[Pq][:, g * 512:(g + 1) * 512],
                                                in_=qsc[g, :, offq:offq + 512]).then_inc(sem_ds, 16)
                                    e.dma_start(out=rqTb[Pq][0:64, g * 512:(g + 1) * 512],
                                                in_=rqsc[g, 0, :, offq:offq + 512]).then_inc(sem_ds, 16)
                                    if g == G - 1:
                                        return e.dma_start(
                                            out=rqTb[Pq][64:128, g * 512:(g + 1) * 512],
                                            in_=rqsc[g, 1, :, offq:offq + 512])
                                    e.dma_start(out=rqTb[Pq][64:128, g * 512:(g + 1) * 512],
                                                in_=rqsc[g, 1, :, offq:offq + 512]).then_inc(sem_ds, 16)
                            T[('ldq', ii)] = grp(
                                "sync", ld_q, dman=11,
                                after=(None if ii <= 2 else
                                       [T.get(('end', ii - 3)),
                                        T.get(('endv', ii - 3))]))

                        if i == 0:
                            mk_ldq(0)
                        if i + 1 < Bq * 8:
                            mk_ldq(i + 1)

                        def ld_gn(e, off=off, P=P):
                            for it in range(4):
                                s2 = gsc[:, off + it * 128: off + it * 128 + 128]
                                s2 = s2.rearrange("g p -> p g")
                                d2 = gnatb[P].ap()[:, it * 12:(it + 1) * 12]
                                if it == 3:
                                    return e.dma_start(out=d2, in_=s2,
                                                       allow_slow_non_contiguous=True)
                                e.dma_start(out=d2, in_=s2,
                                            allow_slow_non_contiguous=True
                                            ).then_inc(sem_dg, 16)
                        T[('gn', i)] = grp("gpsimd", ld_gn, dman=3,
                                           after=(None if first else
                                                  [T.get(('endv', i - 2))]))

                        def mk_qs(e, P=P):
                            e.tensor_add(qsT[P][:, :], qTb[P3][:, 0:512], qTb[P3][:, 512:1024])
                            e.tensor_add(qsT[P][:, :], qsT[P][:, :], qTb[P3][:, 1024:1536])
                            return e.tensor_add(qsT[P][:, :], qsT[P][:, :], qTb[P3][:, 1536:2048])
                        T[('mkqs', i)] = grp("vector", mk_qs,
                                             after=[T[('ldq', i)], T.get(('end', i - 2))])

                        for h in range(2):
                            s_imp = take_slots(2)

                            def mm_imp(e, base=base, b=b, P=P, h=h,
                                       s_imp=tuple(s_imp)):
                                for x in range(2):
                                    it = 2 * h + x
                                    ibase = base + 8 * it
                                    e.matmul(pool[s_imp[x]][:, 0:384], e8b[:, :],
                                             bst8b[:, 257 - ibase: 257 - ibase + 384],
                                             start=True, stop=False, skip_group_check=True)
                                    last = e.matmul(pool[s_imp[x]][:, 0:384],
                                                    qsT[P][:, it * 128:(it + 1) * 128],
                                                    ckT[b][:, 1:385],
                                                    start=False, stop=True,
                                                    skip_group_check=True)
                                return last
                            T[('mmimp', i, h)] = grp("tensor", mm_imp,
                                                     after=[T[('mkqs', i)]]
                                                     + [slot_rd[s] for s in s_imp])

                            def exp_imp(e, h=h, s_imp=tuple(s_imp)):
                                for x in range(2):
                                    it = 2 * h + x
                                    last = e.activation(pimp[:, it * 384:(it + 1) * 384],
                                                        pool[s_imp[x]][:, 0:384],
                                                        ACT.Exp, bias=0.0,
                                                        scale=SCALE / 4.0)
                                return last
                            T[('expimp', i, h)] = grp("scalar", exp_imp,
                                                      after=[T[('mmimp', i, h)]]
                                                      + [T.get(('sel', i - 1, it))
                                                         for it in range(4)])
                            for s in s_imp:
                                slot_rd[s] = T[('expimp', i, h)]

                        for it in range(4):
                            def sel(e, it=it):
                                pp = pimp[:, it * 384:(it + 1) * 384]
                                e.reduce_sum(zrow[:, :], pp, axis=AX.X)
                                vdr(e, e.max(out=mx8[:, :], in_=pp))
                                e.tensor_scalar_mul(thr[:, :], zrow[:, :], 1e-10)
                                vdr(e, e.tensor_copy(mx4[:, :], mx8[:, 3:4]))
                                e.tensor_scalar(selw[:, :], pp,
                                                mx4[:, :], None, AL.is_ge)
                                e.tensor_scalar(zap[:, :], pp,
                                                thr[:, :], None, AL.is_gt)
                                e.tensor_mul(selw[:, :], selw[:, :], zap[:, :])
                                return e.tensor_copy(selb[it][:, :], selw[:, :])
                            T[('sel', i, it)] = grp(
                                "vector", sel,
                                after=[T[('expimp', i, it // 2)],
                                       T.get(('trsel', i - 1, it))])

                        bpt_rd = [T.get(('evm', i - 1, 3))]

                        for g in range(G):
                            qg = qTb[P3][:, g * 512:(g + 1) * 512]
                            s_c = take_slots(nct)
                            ctoks = reserve("scalar", nct)

                            def mm_c(e, g=g, qg=qg, b=b, base=base, nct=nct,
                                     s_c=tuple(s_c)):
                                for ct in range(nct):
                                    col = 256 + 128 * ct - base
                                    e.matmul(pool[s_c[ct]][:, :], bst32b[:, col:col + 128],
                                             e32b[:, :], start=True, stop=False,
                                             skip_group_check=True)
                                    last = e.matmul(pool[s_c[ct]][:, :],
                                                    ckT[b][:, ct * 128:(ct + 1) * 128],
                                                    qg, start=False, stop=True,
                                                    skip_group_check=True)
                                return last

                            def tr_sel(e, it=g):
                                for ct in range(3):
                                    last = e.transpose(bPT[:, ct * 128:(ct + 1) * 128],
                                                       selb[it][:, ct * 128:(ct + 1) * 128],
                                                       identb[:, :])
                                return last

                            def mm_cpv(e, b=b, nct=nct):
                                for it in range(4):
                                    for ct in range(nct):
                                        last = e.matmul(
                                            pv(it),
                                            pcT[:, ct * 512 + it * 128:
                                                ct * 512 + (it + 1) * 128],
                                            cvn[b][:, ct * 129:(ct + 1) * 129],
                                            start=(ct == 0), stop=(ct == nct - 1),
                                            skip_group_check=True)
                                return last
                            pv_war = (T.get(('divf', i - 1, 3)) if g == 0
                                      else T.get(('divc', i, g - 1)))
                            pcw = (T.get(('cpv', i, g - 1)) if g > 0
                                   else T.get(('cpv', i - 1, 3)))
                            tph = grpM("tensor", [
                                ([T[('ldq', i)]] + [slot_rd[s] for s in s_c], mm_c),
                                ([T[('sel', i, g)]] + bpt_rd, tr_sel),
                                ([ctoks[-1], pv_war], mm_cpv),
                            ])
                            T[('mmc', i, g)] = tph[0]
                            T[('trsel', i, g)] = tph[1]
                            T[('cpv', i, g)] = tph[2]

                            def exp_c(e, nct=nct, s_c=tuple(s_c)):
                                for ct in range(nct):
                                    last = e.activation(pcT[:, ct * 512:(ct + 1) * 512],
                                                        pool[s_c[ct]][:, :], ACT.Exp,
                                                        bias=0.0, scale=SCALE)
                                return last
                            emit_reserved("scalar",
                                          [([T[('mmc', i, g)], pcw], exp_c)]
                                          if nct == 1 else
                                          [([T[('mmc', i, g)], pcw],
                                            (lambda e, ct=ct, s=s_c[ct]:
                                             e.activation(pcT[:, ct * 512:(ct + 1) * 512],
                                                          pool[s][:, :], ACT.Exp,
                                                          bias=0.0, scale=SCALE)))
                                           for ct in range(nct)])
                            for x, s in enumerate(s_c):
                                slot_rd[s] = ctoks[x]

                            def ev_sel(e, it=g, P=P):
                                for ct in range(3):
                                    e.tensor_copy(
                                        selBIP[P][:, ct * 512 + it * 128:
                                                  ct * 512 + (it + 1) * 128],
                                        bPT[:, ct * 128:(ct + 1) * 128])
                                    last = e.tensor_scalar_add(
                                        selBIP[P][:, ct * 512 + it * 128:
                                                  ct * 512 + (it + 1) * 128],
                                        selBIP[P][:, ct * 512 + it * 128:
                                                  ct * 512 + (it + 1) * 128], -1.0)
                                return last
                            T[('evsel', i, g)] = grp("vector", ev_sel,
                                                     after=[T[('trsel', i, g)],
                                                            T.get(('end', i - 2))])
                            bpt_rd = [T[('evsel', i, g)]]

                            def div_c(e, g=g):
                                for it in range(4):
                                    o = (g * 4 + it) * 128
                                    vdr(e, e.reciprocal(recb[:, :], pv(it)[:, 128:129]))
                                    e.tensor_copy(couts[:, o:o + 128], pv(it)[:, 0:128])
                                    last = e.tensor_scalar_mul(
                                        couts[:, o:o + 128], couts[:, o:o + 128],
                                        recb[:, :])
                                return last
                            T[('divc', i, g)] = grp("vector", div_c,
                                                    after=[T[('cpv', i, g)]])

                        for g in range(G):
                            rqg = rqTb[P3][:, g * 512:(g + 1) * 512]
                            s_s = take_slots(1)[0]
                            sftoks = reserve("scalar", 1 + nkt)
                            vt2 = reserve("vector", 2)
                            vtok_divs = vt2[1]
                            s_f = [take_slots(1)[0] for _ in range(nkt)]
                            mm_s_war = slot_rd[s_s]

                            def mm_s(e, rqg=rqg, qc=qc, b=b, s_s=s_s):
                                sl = pool[s_s]
                                for wl in range(8):
                                    w = 8 * qc + wl
                                    ocol = wl * 64
                                    if w == 0:
                                        e.matmul(sl[:, ocol:ocol + 64], identb[:, :],
                                                 sm0b[:, :], start=True, stop=False,
                                                 skip_group_check=True)
                                        last = e.matmul(sl[:, ocol:ocol + 64],
                                                        rkT[b][:, 0:128],
                                                        rqg[:, 0:64],
                                                        start=False, stop=True,
                                                        skip_group_check=True)
                                    else:
                                        e.matmul(sl[:, ocol:ocol + 64], identb[:, :],
                                                 sm1b[:, :], start=True, stop=False,
                                                 skip_group_check=True)
                                        last = e.matmul(sl[:, ocol:ocol + 64],
                                                        rkT[b][:, 64 * (w - 1): 64 * (w + 1)],
                                                        rqg[:, wl * 64: wl * 64 + 64],
                                                        start=False, stop=True,
                                                        skip_group_check=True)
                                return last

                            def mk_mmf(kt, sl):
                                def mm_f(e, kt=kt, sl=sl, rqg=rqg, qc=qc, b=b, P=P):
                                    slp = pool[sl]
                                    ct = kt // 16
                                    jj = kt % 16
                                    e.matmul(slp[:, :], ebigtb[:, jj * 128:(jj + 1) * 128],
                                             selBIP[P][:, ct * 512:(ct + 1) * 512],
                                             start=True, stop=False,
                                             skip_group_check=True)
                                    it2 = kt - 4 * qc
                                    if 0 <= it2 < 4:
                                        e.matmul(slp[:, it2 * 128:(it2 + 1) * 128],
                                                 identb[:, :], owntb[:, :],
                                                 start=False, stop=False,
                                                 skip_group_check=True)
                                    return e.matmul(slp[:, :],
                                                    rkT[b][:, kt * 128:(kt + 1) * 128],
                                                    rqg, start=False, stop=True,
                                                    skip_group_check=True)
                                return mm_f

                            def mk_fpv(kt, x):
                                def mm_fpv(e, kt=kt, x=x, qc=qc, b=b):
                                    for it2 in range(max(0, kt - 4 * qc), 4):
                                        last = e.matmul(
                                            pv(it2),
                                            pfT[:, x * 512 + it2 * 128:
                                                x * 512 + (it2 + 1) * 128],
                                            vnat[b][:, kt * 129: kt * 129 + 129],
                                            start=(kt == 0),
                                            stop=(kt == 4 * qc + it2),
                                            skip_group_check=True)
                                    return last
                                return mm_fpv

                            evsel_all = [T[('evsel', i, gg)] for gg in range(4)]
                            sv_war = (T.get(('divc', i, 3)) if g == 0
                                      else T.get(('divf', i, g - 1)))
                            fpv_phase = {}
                            t_phases = [
                                ([T[('ldq', i)], mm_s_war], mm_s),
                                (evsel_all + [slot_rd[s_f[0]]], mk_mmf(0, s_f[0])),
                                ([sftoks[0], sv_war], None),
                            ]

                            def mm_spv(e, qc=qc, b=b, g=g):
                                for wl in range(8):
                                    w = 8 * qc + wl
                                    it2 = wl // 2
                                    half = wl % 2
                                    if w == 0:
                                        rv = vnat[b][:, 0:129]
                                    elif w % 2 == 1:
                                        kt2 = (w - 1) // 2
                                        rv = vnat[b][:, kt2 * 129: kt2 * 129 + 129]
                                    else:
                                        kt2 = (w - 2) // 2
                                        rv = vnatS[b][:, kt2 * 129: kt2 * 129 + 129]
                                    last = e.matmul(
                                        pvp(it2, 64 * half, 64 * half + 64),
                                        psT[g % 2][:, wl * 64: wl * 64 + 64], rv,
                                        start=True, stop=True,
                                        tile_position=(0, 64 * half),
                                        skip_group_check=True)
                                return last
                            t_phases[2] = ([sftoks[0], sv_war], mm_spv)
                            fpv_idx = {}
                            for kt in range(1, nkt):
                                t_phases.append(
                                    ([slot_rd[s_f[kt]]], mk_mmf(kt, s_f[kt])))
                                fw = [sftoks[kt], vtok_divs] if kt == 1 else [sftoks[kt]]
                                pftw = []
                                t_phases.append((fw + pftw, mk_fpv(kt - 1, (kt - 1) % 4)))
                                fpv_idx[kt - 1] = len(t_phases) - 1
                            t_phases.append(([sftoks[nkt]], mk_fpv(nkt - 1, (nkt - 1) % 4)))
                            fpv_idx[nkt - 1] = len(t_phases) - 1
                            tph = grpM("tensor", t_phases)
                            T[('mms', i, g)] = tph[0]
                            T[('spv', i, g)] = tph[2]
                            for kt, pi_ in fpv_idx.items():
                                T[('fpv', i, g, kt)] = tph[pi_]
                            mmf_tok = {0: tph[1]}
                            x = 3
                            for kt in range(1, nkt):
                                mmf_tok[kt] = tph[x]
                                x += 2

                            sc_phases = [([T[('mms', i, g)],
                                           (T.get(('spv', i, g - 2)) if g >= 2
                                            else T.get(('spv', i - 1, g + 2)))],
                                          (lambda e, s=s_s, g=g:
                                           e.activation(psT[g % 2][:, :], pool[s][:, :],
                                                        ACT.Exp, bias=0.0, scale=SCALE)))]
                            for kt in range(nkt):
                                pft_war = (T[('fpv', i, g, kt - 4)]
                                           if kt >= 4 else
                                           (T.get(('fpv', i, g - 1, nkt - 4 + kt))
                                            if g > 0 else T.get(('end', i - 1))))
                                sc_phases.append(
                                    ([mmf_tok[kt], pft_war],
                                     (lambda e, s=s_f[kt], x=kt % 4:
                                      e.activation(pfT[:, x * 512:(x + 1) * 512],
                                                   pool[s][:, :], ACT.Exp,
                                                   bias=0.0, scale=SCALE))))
                            emit_reserved("scalar", sc_phases)
                            slot_rd[s_s] = sftoks[0]
                            for kt in range(nkt):
                                slot_rd[s_f[kt]] = sftoks[1 + kt]

                            def div_s_a(e):
                                for it in range(4):
                                    last = e.reciprocal(recb4[:, it:it + 1],
                                                        pv(it)[:, 128:129])
                                return last

                            def div_s_b(e, g=g):
                                for it in range(4):
                                    o = (g * 4 + it) * 128
                                    e.tensor_copy(souts[:, o:o + 128], pv(it)[:, 0:128])
                                    last = e.tensor_scalar_mul(
                                        souts[:, o:o + 128], souts[:, o:o + 128],
                                        recb4[:, it:it + 1])
                                return last
                            emit_reserved("vector", [([T[('spv', i, g)]], div_s_a),
                                                     ([vt2[0]], div_s_b)])
                            T[('divs', i, g)] = vtok_divs

                            def div_f(e, g=g):
                                for it in range(4):
                                    o = (g * 4 + it) * 128
                                    vdr(e, e.reciprocal(recb[:, :], pv(it)[:, 128:129]))
                                    e.tensor_copy(fouts[:, o:o + 128], pv(it)[:, 0:128])
                                    last = e.tensor_scalar_mul(
                                        fouts[:, o:o + 128], fouts[:, o:o + 128],
                                        recb[:, :])
                                return last
                            T[('divf', i, g)] = grp("vector", div_f,
                                                    after=[T[('fpv', i, g, nkt - 1)]])


                        for it in range(4):
                            def mrg(e, it=it, P=P):
                                for g in range(G):
                                    o = (g * 4 + it) * 128
                                    gn = gnatb[P]
                                    e.tensor_scalar_mul(mbuf[:, g * 128:(g + 1) * 128],
                                                        couts[:, o:o + 128],
                                                        gn[:, it * 12 + 3 * g:
                                                           it * 12 + 3 * g + 1])
                                    e.scalar_tensor_tensor(
                                        mbuf[:, g * 128:(g + 1) * 128],
                                        fouts[:, o:o + 128],
                                        gn[:, it * 12 + 3 * g + 1: it * 12 + 3 * g + 2],
                                        mbuf[:, g * 128:(g + 1) * 128], AL.mult, AL.add)
                                    last = e.scalar_tensor_tensor(
                                        mbuf[:, g * 128:(g + 1) * 128],
                                        souts[:, o:o + 128],
                                        gn[:, it * 12 + 3 * g + 2: it * 12 + 3 * g + 3],
                                        mbuf[:, g * 128:(g + 1) * 128], AL.mult, AL.add)
                                return last
                            T[('mrg', i, it)] = grp("vector", mrg,
                                                    after=[T[('gn', i)],
                                                           T[('divf', i, 3)],
                                                           (T.get(('trm', i, it - 1))
                                                            if it > 0 else
                                                            T.get(('end', i - 1)))])

                            def tr_m(e, it=it):
                                for j in range(4):
                                    last = e.transpose(bPT[:, j * 128:(j + 1) * 128],
                                                       mbuf[:, j * 128:(j + 1) * 128],
                                                       identb[:, :])
                                return last
                            T[('trm', i, it)] = grp("tensor", tr_m,
                                                    after=[T[('mrg', i, it)]] + bpt_rd)

                            def ev_m(e, it=it):
                                return e.tensor_copy(mTb[it % 2][:, :], bPT[:, :])
                            T[('evm', i, it)] = grp("vector", ev_m,
                                                    after=[T[('trm', i, it)],
                                                           (T.get(('stm', i, it - 2))
                                                            if it >= 2 else
                                                            T.get(('stm', i - 1, it + 2)))])
                            bpt_rd = [T[('evm', i, it)]]

                            def st_m(e, b=b, qc=qc, it=it):
                                tglob = b * S + qc * 512 + it * 128
                                owner = tglob // TPC
                                tcol = tglob % TPC
                                dst = a2a_in[owner * 512:(owner + 1) * 512,
                                             tcol:tcol + 128].rearrange(
                                                 "(j p) t -> p j t", p=128)
                                src = mTb[it % 2].ap().rearrange("p (j t) -> p j t", j=4)
                                return e.dma_start(out=dst[:, :, :], in_=src[:, :, :])
                            T[('stm', i, it)] = grp("sync", st_m,
                                                    after=[T[('evm', i, it)]])

                        T[('end', i)] = T[('trm', i, 3)]
                        T[('endv', i)] = T[('evm', i, 3)]

                        if DEBUG and (b, qc) in getattr(build_nc, "_dbg_bq", ()):
                            def dbg_dump(e, b=b, qc=qc, P=P):
                                key = "bq_%d_%d" % (b, qc)
                                e.dma_start(out=dbg[key + "_c"][:, :], in_=couts[:, :]).then_inc(sem_dg, 16)
                                e.dma_start(out=dbg[key + "_f"][:, :], in_=fouts[:, :]).then_inc(sem_dg, 16)
                                e.dma_start(out=dbg[key + "_s"][:, :], in_=souts[:, :]).then_inc(sem_dg, 16)
                                return e.dma_start(out=dbg[key + "_sel"][:, :],
                                                   in_=selBIP[P][:, 0:1024])
                            grp("gpsimd", dbg_dump, dman=3,
                                after=[T[('divf', i, 3)], T[('evsel', i, 3)]])

        def coll(e):
            with allow_ici_all_to_all():
                return e.collective_compute(
                    "AllToAll", AL.bypass,
                    replica_groups=[list(range(NCORE))],
                    ins=[a2a_in_h.ap().opt()],
                    outs=[a2a_out_h.ap().opt()])
        coll_grp(coll)

        xTown = dram("xTown", (D, TPC))
        stD = contextlib.ExitStack()
        with stD:
            ED = stD.enter_context
            bigA = ED(nc.sbuf_tensor("bigA", [128, 32 * TPC], BF16))
            bigB = ED(nc.sbuf_tensor("bigB", [128, 32 * TPC], BF16))
            wstr = [ED(nc.sbuf_tensor("wstr%d" % p, [128, 32 * 128], BF16)) for p in range(2)]
            xcb = [ED(nc.sbuf_tensor("xcb%d" % p, [128, TPC], BF16)) for p in range(2)]
            sqC = ED(nc.sbuf_tensor("sqC", [128, 2 * TPC], BF16))
            stat = ED(nc.sbuf_tensor("stat", [1, 4 * TPC], F32))
            nmB = ED(nc.sbuf_tensor("nmB", [128, TPC], F32))
            rsB = ED(nc.sbuf_tensor("rsB", [128, TPC], F32))
            laTb = ED(nc.sbuf_tensor("laTb", [128, 32 * 128], BF16))
            lbTb = ED(nc.sbuf_tensor("lbTb", [128, D], BF16))
            bfbb = ED(nc.sbuf_tensor("bfbb", [128, 32], F32))
            coefb = ED(nc.sbuf_tensor("coefb", [128, 1], F32))
            hTb = ED(nc.sbuf_tensor("hTb", [128, TPC], BF16))
            youtb = [ED(nc.sbuf_tensor("youtb%d" % p, [128, TPC], F32)) for p in range(2)]
            zrowC = youtb[0].ap()[:, 0:512]
            cPA = [ED(nc.psum_tensor("cPA%d" % p, [128, 512], F32)) for p in range(2)]
            cSM0 = ED(nc.psum_tensor("cSM0", [1, 512], F32))
            cSM1 = ED(nc.psum_tensor("cSM1", [1, 512], F32))
            cSS0 = ED(nc.psum_tensor("cSS0", [1, 512], F32))
            cSS1 = ED(nc.psum_tensor("cSS1", [1, 512], F32))
            cBC = ED(nc.psum_tensor("cBC", [128, 512], F32))
            cSM = [cSM0, cSM1]
            cSS = [cSS0, cSS1]

            def ld_c(e):
                src = a2a_out.rearrange("(kt p) t -> p kt t", p=128)
                dst = bigA.ap().rearrange("p (kt t) -> p kt t", kt=32)
                return e.dma_start(out=dst[:, 0:16, :], in_=src[:, 0:16, :])
            grp("sync", ld_c)

            def ld_cb(e):
                src = a2a_out.rearrange("(kt p) t -> p kt t", p=128)
                dst = bigA.ap().rearrange("p (kt t) -> p kt t", kt=32)
                return e.dma_start(out=dst[:, 16:32, :], in_=src[:, 16:32, :])
            grp("gpsimd", ld_cb)

            def ld_c2(e):
                src = laT.rearrange("(kt p) r -> p kt r", p=128)
                dst = laTb.ap().rearrange("p (kt r) -> p kt r", kt=32)
                e.dma_start(out=dst[:, :, :], in_=src[:, :, :]).then_inc(sem_dg, 16)
                e.dma_start(out=lbTb[:], in_=lbT[:, :]).then_inc(sem_dg, 16)
                e.dma_start(out=bfbb[:], in_=bfb_h[:, :]).then_inc(sem_dg, 16)
                return e.dma_start(out=coefb[:], in_=lw128[:, :])
            grp("gpsimd", ld_c2, dman=3)

            def mk_coef(e):
                return e.tensor_scalar_mul(coefb[:, :], coefb[:, :], 2.0)
            grp("vector", mk_coef)

            if DEBUG and "bigA0" in dbg:
                def dump_bigA(e):
                    return e.dma_start(out=dbg["bigA0"][:, :], in_=bigA[:, :])
                grp("gpsimd", dump_bigA)

            for phase in range(2):
                if DEBUG and phase == 1 and "x1d" in dbg:
                    def dump_x1(e):
                        e.dma_start(out=dbg["x1d"][:, :], in_=bigA[:, :]).then_inc(sem_dg, 16)
                        e.dma_start(out=dbg["htbd"][:, :], in_=hTb[:, :]).then_inc(sem_dg, 16)
                        return e.dma_start(out=dbg["statd"][:, :], in_=stat[:, :])
                    grp("gpsimd", dump_x1, dman=2)
                wsrc = woT if phase == 0 else wfT
                c_w = {}; c_x = {}; c_mm = {}; c_ev = {}; c_st = {}

                def emit_ldw(ft, first, wsrc=wsrc):
                    p = ft % 2

                    def ld_wt(e, ft=ft, wsrc=wsrc, p=p):
                        src = wsrc.rearrange("(kt p) f -> p kt f", p=128)
                        dst = wstr[p].ap().rearrange("p (kt f) -> p kt f", kt=32)
                        return e.dma_start(out=dst[:, :, :],
                                           in_=src[:, :, ft * 128:(ft + 1) * 128])
                    c_w[ft] = grp("gpsimd", ld_wt,
                                  after=(None if first else [c_mm.get((ft - 2, 1))]))
                    if phase == 0:
                        def ld_xc(e, ft=ft, p=p):
                            return e.dma_start(out=xcb[p][:, :],
                                               in_=xTown[ft * 128:(ft + 1) * 128, :])
                        c_x[ft] = grp("gpsimd", ld_xc,
                                      after=(None if first else
                                             [c_ev.get((ft - 2, 0)), c_ev.get((ft - 2, 1))]))

                def emit_mmst(ft):
                    def mm_st(e, ft=ft):
                        sq0 = (ft % 2) * TPC
                        for tc2 in range(2):
                            e.matmul(cSM[tc2][:, :], ones128b[:, :],
                                     bigB[:, ft * TPC + tc2 * 512: ft * TPC + tc2 * 512 + 512],
                                     start=(ft == 0), stop=(ft == 31),
                                     skip_group_check=True)
                            last = e.matmul(cSS[tc2][:, :], ones128b[:, :],
                                            sqC[:, sq0 + tc2 * 512: sq0 + (tc2 + 1) * 512],
                                            start=(ft == 0), stop=(ft == 31),
                                            skip_group_check=True)
                        return last
                    c_st[ft] = grp("tensor", mm_st,
                                   after=[c_ev[(ft, 0)], c_ev[(ft, 1)]])

                emit_ldw(0, True)
                emit_ldw(1, False)
                for ft in range(32):
                    p = ft % 2
                    for tc2 in range(TPC // 512):
                        def mm_o(e, ft=ft, tc2=tc2, phase=phase, p=p):
                            for kt in range(32):
                                last = e.matmul(
                                    cPA[tc2][:, :], wstr[p][:, kt * 128:(kt + 1) * 128],
                                    bigA[:, kt * TPC + tc2 * 512: kt * TPC + tc2 * 512 + 512],
                                    start=(kt == 0),
                                    stop=(kt == 31 and phase == 0),
                                    skip_group_check=True)
                            if phase == 1:
                                last = e.matmul(
                                    cPA[tc2][:, :], lbTb[:, ft * 128:(ft + 1) * 128],
                                    hTb[:, tc2 * 512:(tc2 + 1) * 512],
                                    start=False, stop=True, skip_group_check=True)
                            return last
                        aft = [c_w[ft], c_ev.get((ft - 1, tc2))]
                        if ft == 0 and tc2 == 0:
                            aft = None
                        c_mm[(ft, tc2)] = grp("tensor", mm_o, after=aft)
                        if tc2 == 1 and ft >= 1:
                            emit_mmst(ft - 1)

                        def ev_o(e, ft=ft, tc2=tc2, phase=phase, p=p):
                            dsttile = bigB[:, ft * TPC + tc2 * 512: ft * TPC + tc2 * 512 + 512]
                            if phase == 0:
                                e.tensor_add(dsttile, cPA[tc2][:, :],
                                             xcb[p][:, tc2 * 512:(tc2 + 1) * 512])
                            else:
                                e.tensor_scalar_add(zrowC[:, :], cPA[tc2][:, :],
                                                    bfbb[:, ft:ft + 1])
                                e.tensor_add(dsttile, zrowC[:, :],
                                             bigA[:, ft * TPC + tc2 * 512:
                                                  ft * TPC + tc2 * 512 + 512])
                            return e.tensor_mul(
                                sqC[:, (ft % 2) * TPC + tc2 * 512:
                                    (ft % 2) * TPC + (tc2 + 1) * 512],
                                dsttile, dsttile)
                        evaft = [c_mm[(ft, tc2)], c_st.get(ft - 2)]
                        if phase == 0:
                            evaft.append(c_x[ft])
                        c_ev[(ft, tc2)] = grp("vector", ev_o, after=evaft)
                    if ft + 2 < 32:
                        emit_ldw(ft + 2, False)
                emit_mmst(31)

                if DEBUG and phase == 0 and "bigB0" in dbg:
                    def dump_bigB(e):
                        return e.dma_start(out=dbg["bigB0"][:, :], in_=bigB[:, :])
                    grp("gpsimd", dump_bigB)
                if DEBUG and phase == 1 and "res2d" in dbg:
                    def dump_res2(e):
                        return e.dma_start(out=dbg["res2d"][:, :], in_=bigB[:, :])
                    grp("gpsimd", dump_res2)

                def ev_st(e):
                    for tc2 in range(2):
                        e.tensor_copy(stat[:, tc2 * 512:(tc2 + 1) * 512], cSM[tc2][:, :])
                        e.tensor_copy(stat[:, TPC + tc2 * 512: TPC + (tc2 + 1) * 512],
                                      cSS[tc2][:, :])
                    e.tensor_scalar_mul(stat[:, 2 * TPC:3 * TPC], stat[:, 0:TPC],
                                        1.0 / D)
                    e.tensor_scalar_mul(stat[:, TPC:2 * TPC], stat[:, TPC:2 * TPC],
                                        1.0 / D)
                    e.tensor_mul(stat[:, 3 * TPC:4 * TPC], stat[:, 2 * TPC:3 * TPC],
                                 stat[:, 2 * TPC:3 * TPC])
                    e.tensor_sub(stat[:, TPC:2 * TPC], stat[:, TPC:2 * TPC],
                                 stat[:, 3 * TPC:4 * TPC])
                    return e.tensor_scalar_add(stat[:, TPC:2 * TPC],
                                               stat[:, TPC:2 * TPC], EPS)
                tst1 = grp("vector", ev_st)

                def st_sqrt(e):
                    return e.activation(stat[:, 3 * TPC:4 * TPC], stat[:, TPC:2 * TPC],
                                        ACT.Sqrt, bias=0.0, scale=1.0)
                tst2 = grp("scalar", st_sqrt, after=[tst1])

                def st_fin(e):
                    e.reciprocal(stat[:, TPC:2 * TPC], stat[:, 3 * TPC:4 * TPC])
                    return e.tensor_scalar_mul(stat[:, 2 * TPC:3 * TPC],
                                               stat[:, 2 * TPC:3 * TPC], -1.0)
                tst3 = grp("vector", st_fin, after=[tst2])

                tprev_mm = None
                tprev_ev = None
                tbcs = []
                for which in range(2):
                    dstb = nmB if which == 0 else rsB
                    srcoff = 2 * TPC if which == 0 else TPC
                    for tc2 in range(2):
                        def bc_prep(e, srcoff=srcoff, tc2=tc2):
                            e.memset(zrowC[:], 0.0)
                            return e.tensor_copy(
                                zrowC[0:1, :],
                                stat[:, srcoff + tc2 * 512: srcoff + (tc2 + 1) * 512])
                        tp = grp("vector", bc_prep, after=[tst3, tprev_mm])

                        def bc_mm(e):
                            return e.matmul(cBC[:, :], e1tf[:, :], zrowC[:, :],
                                            start=True, stop=True, skip_group_check=True)
                        tprev_mm = grp("tensor", bc_mm, after=[tp, tprev_ev])

                        def bc_ev(e, dstb=dstb, tc2=tc2):
                            return e.tensor_copy(dstb[:, tc2 * 512:(tc2 + 1) * 512],
                                                 cBC[:, :])
                        tprev_ev = grp("vector", bc_ev, after=[tprev_mm])
                        tbcs.append(tprev_ev)

                if phase == 0:
                    for ft in range(32):
                        def nrm(e, ft=ft):
                            sl = slice(ft * TPC, (ft + 1) * TPC)
                            e.tensor_add(bigA[:, sl], bigB[:, sl], nmB[:, :])
                            return e.tensor_mul(bigA[:, sl], bigA[:, sl], rsB[:, :])
                        grp("vector" if ft % 2 == 0 else "gpsimd", nrm,
                            after=list(tbcs))

                    tmh = None
                    for tc2 in range(2):
                        def mm_h2(e, tc2=tc2):
                            for kt in range(32):
                                last = e.matmul(
                                    cPA[tc2][:, :], laTb[:, kt * 128:(kt + 1) * 128],
                                    bigA[:, kt * TPC + tc2 * 512: kt * TPC + tc2 * 512 + 512],
                                    start=(kt == 0), stop=(kt == 31),
                                    skip_group_check=True)
                            return last
                        tmh = grp("tensor", mm_h2, after=(None if tc2 == 0 else [tmh]))

                        def ev_h2(e, tc2=tc2):
                            e.tensor_copy(hTb[:, tc2 * 512:(tc2 + 1) * 512], cPA[tc2][:, :])
                            return e.tensor_scalar_mul(
                                hTb[:, tc2 * 512:(tc2 + 1) * 512],
                                hTb[:, tc2 * 512:(tc2 + 1) * 512], coefb[:, :])
                        grp("vector", ev_h2, after=[tmh])
                else:
                    tnrm = {}; tsty = {}
                    for ft in range(32):
                        def nrm2(e, ft=ft):
                            sl = slice(ft * TPC, (ft + 1) * TPC)
                            yb = youtb[ft % 2]
                            e.tensor_add(yb[:, :], bigB[:, sl], nmB[:, :])
                            return e.tensor_mul(yb[:, :], yb[:, :], rsB[:, :])
                        tnrm[ft] = grp("vector" if ft % 2 == 0 else "gpsimd", nrm2,
                                       after=list(tbcs) + [tsty.get(ft - 2)])

                        def st_y(e, ft=ft):
                            return e.dma_start(out=y[ft * 128:(ft + 1) * 128, :],
                                               in_=youtb[ft % 2][:, :])
                        tsty[ft] = grp("sync", st_y, after=[tnrm[ft]])

            def fin(e):
                return e.nop()
            grp("sync", fin)

    return nc


def kernel(**inputs):
    shared = _host_prep_shared(inputs)
    consts_keys = ["kw1T", "vw1T", "kb1", "vb1", "kw2T", "vw2T", "kb2", "vb2z",
                   "woT", "wfT", "bfb", "laT", "lbT", "lw128", "identc", "ones128",
                   "e1t", "bst32", "bst8", "e32", "e8", "e8ones", "ebigt", "ownt",
                   "smaskt0", "smaskt1", "cosT", "sinT", "xT"]
    in_maps = []
    for c in range(NCORE):
        d = _host_prep_core(inputs, c)
        for k in consts_keys:
            d[k] = shared[k]
        d["xTown"] = np.ascontiguousarray(shared["xT"][:, c * TPC:(c + 1) * TPC])
        in_maps.append(d)
    nc = build_nc()
    res = run_bass_kernel_spmd(nc, in_maps, core_ids=list(range(NCORE)))
    out = np.concatenate(
        [np.ascontiguousarray(res.results[c]["yT"].T) for c in range(NCORE)], axis=0)
    return out.reshape(Bq, S, D)
